# version 42
# baseline (speedup 1.0000x reference)
import contextlib
import ml_dtypes
import numpy as np
import concourse.bass as bass
import concourse.mybir as mybir
from concourse.bass_utils import run_bass_kernel_spmd

F32 = mybir.dt.float32
I32 = mybir.dt.int32
BF16 = mybir.dt.bfloat16
AF = mybir.ActivationFunctionType
ALU = mybir.AluOpType
AX = mybir.AxisListType

P = 128
D = 1024
ND = 8
T = 2048
H = 1024
NT = 8
F = 512
NJ = 4
NQ = 2
QW = 512
DF = 4096
NG = 4
EPS_FFT = 1e-8
EPS_LN = 1e-5

ENGS = ("sync", "gpsimd", "tensor", "vector", "scalar")


def make_tables():
    e = np.arange(D, dtype=np.float64)
    f = np.arange(F, dtype=np.float64)
    ang = 2.0 * np.pi * np.outer(e, f) / D
    cs64 = np.concatenate(
        [np.cos(ang), -np.sin(ang), np.cos(np.pi * e)[:, None]], axis=1)
    cs = cs64.astype(np.float32)
    cslo = (cs64 - cs.astype(np.float64)).astype(ml_dtypes.bfloat16)
    cslo2 = (cs64 - cs.astype(np.float64)
             - cslo.astype(np.float64)).astype(ml_dtypes.bfloat16)
    d = np.arange(D, dtype=np.float64)
    cf = np.where(f == 0, 1.0, 2.0)
    angG = 2.0 * np.pi * np.outer(f, d) / D
    g64 = np.concatenate(
        [cf[:, None] * np.cos(angG) / D,
         -cf[:, None] * np.sin(angG) / D,
         (np.cos(np.pi * d) / D)[None, :]], axis=0)
    g = g64.astype(np.float32)
    glo = (g64 - g.astype(np.float64)).astype(ml_dtypes.bfloat16)
    ident = np.eye(P, dtype=np.float32)
    return cs, cslo, cslo2, g, glo, ident


class Op:
    __slots__ = ("eng", "emit", "reads", "writes", "dma", "opid", "inc")

    def __init__(self, eng, emit, reads, writes, dma):
        self.eng, self.emit = eng, emit
        self.reads, self.writes = list(reads), list(writes)
        self.dma = dma
        self.inc = None


class Sched:
    def __init__(self):
        self.ops = []
        self.writer = {}
        self.readers = {}
        self.deps = []
        self.dma_mode = {}
        self.dma_last = {}
        self.batch_members = {}

    def dma_sem(self, name, mode):
        self.dma_mode[name] = mode
        self.batch_members[name] = []

    def op(self, eng, emit, reads=(), writes=(), dma=None):
        o = Op(eng, emit, reads, writes, dma)
        o.opid = len(self.ops)
        dep = set()
        for k in o.reads:
            w = self.writer.get(k)
            if w is not None:
                dep.add(w)
        for k in o.writes:
            w = self.writer.get(k)
            if w is not None:
                dep.add(w)
            for r in self.readers.get(k, ()):
                dep.add(r)
        if dma is not None:
            if self.dma_mode[dma] == "serial" and self.dma_last.get(dma) is not None:
                dep.add(self.dma_last[dma])
            self.dma_last[dma] = o.opid
            self.batch_members[dma].append(o.opid)
        dep.discard(o.opid)
        self.ops.append(o)
        self.deps.append(dep)
        for k in o.reads:
            self.readers.setdefault(k, []).append(o.opid)
        for k in o.writes:
            self.writer[k] = o.opid
            self.readers[k] = []
        return o.opid

    def build_prog(self):
        needed = set()
        for dep in self.deps:
            needed.update(dep)
        eng_count = {e: 0 for e in ENGS}
        for o in self.ops:
            if o.dma is not None:
                idx = self.batch_members[o.dma].index(o.opid) + 1
                o.inc = (o.dma, 16 * idx)
            elif o.opid in needed:
                eng_count[o.eng] += 1
                o.inc = ("E" + o.eng, eng_count[o.eng])
        prog = {e: [] for e in ENGS}
        watermark = {e: {} for e in ENGS}
        for o in self.ops:
            waits = {}
            for dd in sorted(self.deps[o.opid]):
                p = self.ops[dd]
                assert p.inc is not None, f"dep {dd} of {o.opid} has no inc"
                sem, val = p.inc
                if p.dma is not None and self.dma_mode[p.dma] == "batch":
                    val = 16 * len(self.batch_members[p.dma])
                waits[sem] = max(waits.get(sem, 0), val)
            wl = []
            for sem, val in waits.items():
                if watermark[o.eng].get(sem, 0) < val:
                    watermark[o.eng][sem] = val
                    wl.append((sem, val))
            prog[o.eng].append((wl, o.emit, o.inc))
        return prog


def build():
    nc = bass.Bass("TRN2", target_bir_lowering=False)
    AP = bass.AP

    x_d = nc.dram_tensor("x", [T, D], F32, kind="ExternalInput")
    wq_d = nc.dram_tensor("wq", [D, D], F32, kind="ExternalInput")
    wk_d = nc.dram_tensor("wk", [D, D], F32, kind="ExternalInput")
    wv_d = nc.dram_tensor("wv", [D, D], F32, kind="ExternalInput")
    w1_d = nc.dram_tensor("w1", [D, DF], F32, kind="ExternalInput")
    w2_d = nc.dram_tensor("w2", [DF, D], F32, kind="ExternalInput")
    b1_d = nc.dram_tensor("b1", [DF], F32, kind="ExternalInput")
    b2_d = nc.dram_tensor("b2", [D], F32, kind="ExternalInput")
    th_d = nc.dram_tensor("qth", [DF], F32, kind="ExternalInput")
    wr_d = nc.dram_tensor("qwr", [DF], F32, kind="ExternalInput")
    wi_d = nc.dram_tensor("qwi", [DF], F32, kind="ExternalInput")
    cs_d = nc.dram_tensor("cs", [D, 2 * F + 1], F32, kind="ExternalInput")
    cslo_d = nc.dram_tensor("cslo", [D, 2 * F + 1], BF16, kind="ExternalInput")
    cslo2_d = nc.dram_tensor("cslo2", [D, 2 * F + 1], BF16, kind="ExternalInput")
    g_d = nc.dram_tensor("gt", [2 * F + 1, D], F32, kind="ExternalInput")
    glo_d = nc.dram_tensor("gtlo", [2 * F + 1, D], BF16, kind="ExternalInput")
    id_d = nc.dram_tensor("identc", [P, P], F32, kind="ExternalInput")
    out_d = nc.dram_tensor("out", [T, D], F32, kind="ExternalOutput")
    fws_d = nc.dram_tensor("fwspill", [P, 24576], BF16)

    es = contextlib.ExitStack()
    sbuf = lambda n, s, dt=F32: es.enter_context(nc.sbuf_tensor(n, s, dt))
    XT = sbuf("XT", [P, NT * D])
    XNT = sbuf("XNT", [P, ND * H], BF16)
    FW = sbuf("FW", [P, 24576], BF16)
    VAL = sbuf("VAL", [P, 8 * H], BF16)
    TAB = sbuf("TAB", [P, ND * 1025], BF16)
    FWN = sbuf("FWN", [P, 24], BF16)
    SCR = sbuf("SCR", [P, 8192])
    BCH = sbuf("BCH", [P, 10240], BF16)
    XNB = sbuf("XNB", [P, 8 * D], BF16)
    CAR = sbuf("CAR", [P, 16])
    STAT = sbuf("STAT", [P, 96])
    QC = sbuf("QC", [P, 256])
    QIb = sbuf("QIb", [P, 32], I32)
    NVAL = sbuf("NVAL", [1, H], BF16)
    GNYQ = sbuf("GNYQ", [1, D], BF16)
    B2R = sbuf("B2R", [1, D], BF16)
    ONE1 = sbuf("ONE1", [1, P], BF16)
    EPSC = sbuf("EPSC", [P, 1])
    EPSLN = sbuf("EPSLN", [P, 1])
    EPS8 = sbuf("EPS8", [P, 1])
    IDN = sbuf("IDN", [P, P], BF16)
    IDNF = sbuf("IDNF", [P, P])
    PA = es.enter_context(nc.psum_tensor("PA", [P, 2048], F32))
    PB = es.enter_context(nc.psum_tensor("PB", [P, 2048], F32))

    semnames = ["Esync", "Egpsimd", "Etensor", "Evector", "Escalar",
                "dPRE", "dPREa", "dPS", "dG", "dW1", "dW2", "dX1", "dSPL", "dFWL",
                "dST0", "dST1", "dXB", "dLOa0", "dLOa1", "dLOa2", "dLOa3",
                "dLOb0", "dLOb1", "dLOb2", "dLOb3"]
    sems = {s: es.enter_context(nc.semaphore(s)) for s in semnames}

    S = Sched()
    for s, m in (("dPRE", "batch"), ("dPREa", "batch"), ("dPS", "batch"), ("dG", "batch"),
                 ("dW1", "serial"), ("dW2", "serial"), ("dX1", "serial"),
                 ("dSPL", "serial"), ("dFWL", "serial"),
                 ("dST0", "serial"), ("dST1", "serial"),
                 ("dXB", "batch"),
                 ("dLOa0", "batch"), ("dLOa1", "batch"), ("dLOa2", "batch"),
                 ("dLOa3", "batch"), ("dLOb0", "batch"),
                 ("dLOb1", "batch"), ("dLOb2", "batch"), ("dLOb3", "batch")):
        S.dma_sem(s, m)

    def scrk(lo, hi):
        return [f"SCR:{b}" for b in range(lo // 512, (hi + 511) // 512)]

    def mkslice(base, lo, hi):
        return SCR[:, base + lo: base + hi], scrk(base + lo, base + hi)

    def pairs_slices(p):
        b = p * 4096
        d = {}
        for nm, lo, hi in (("SQQ", 0, 1024), ("SQK", 1024, 2048),
                           ("M2QK", 2048, 3072), ("M2Q", 2048, 2560),
                           ("M2K", 2560, 3072), ("RQK", 3072, 4096),
                           ("RQ", 3072, 3584), ("RK", 3584, 4096)):
            d[nm] = mkslice(b, lo, hi)
        return d

    def bchk(lo, hi):
        return [f"BC:{b}" for b in range(lo // 512, (hi + 511) // 512)]

    def bch_slices(p):
        b = p * 5120
        d = {}
        for nm, lo, hi in (("FQN", 0, 1024), ("FKN", 1024, 2048),
                           ("TVB", 2048, 3072), ("PRD", 3072, 4096),
                           ("CSC", 4096, 5120)):
            d[nm] = (BCH[:, b + lo: b + hi], bchk(b + lo, b + hi))
        return d

    def post_slices(p):
        p = p % 4
        d = {"UU": mkslice(p * 2048, 0, 1024),
             "LNU": mkslice(p * 2048, 1024, 2048)}
        d["TRASH"] = (BCH[:, p * 1024:(p + 1) * 1024],
                      [f"BC:{b}" for b in range(p * 2, p * 2 + 2)])
        return d

    def ffn_h_slices(p):
        b = p * 2048
        return {"S_U": mkslice(b, 0, 1024), "C_U": mkslice(b, 1024, 2048)}

    def ffn_y_slices(p):
        return {"U2": mkslice(4096 + p * 1024, 0, 1024),
                "LNY": mkslice(6144 + p * 1024, 0, 1024),
                "TR4": (BCH[:, p * 1024:(p + 1) * 1024], bchk(p * 1024, (p + 1) * 1024))}

    K_CSC = scrk(5120, 6144)
    K_PRD = scrk(4096, 5120)
    TRASH, K_TRASH = SCR[:, 2048:3072], scrk(2048, 3072)

    THc, WRc, WIc, B1c = QC[:, 0:32], QC[:, 32:64], QC[:, 64:96], QC[:, 96:128]
    Ac, Bc, SB1, SC1 = QC[:, 128:160], QC[:, 160:192], QC[:, 192:224], QC[:, 224:256]

    ST_S1, ST_SS1, ST_R1, ST_B1 = 0, 8, 16, 24
    def stc(p, idx):
        return STAT[:, 32 + p * 8 + idx: 32 + p * 8 + idx + 1]

    FWKEYS = [f"FW:{w},{c}" for w in range(3) for c in range(ND)]
    VKEYS = [f"VAL:{g},{q}" for g in range(8) for q in range(NQ)]
    XNTKEYS = [f"XNT:{i}" for i in range(NT)]
    TABKEYS = [f"TAB:{r}" for r in range(ND)]

    def dma(eng, sem, out_ap, in_ap, reads, writes, accum=False):
        if accum:
            S.op(eng, lambda e, o=out_ap, i=in_ap: e.dma_start(
                o, i, accum_op=ALU.add), reads=reads, writes=writes, dma=sem)
        else:
            S.op(eng, lambda e, o=out_ap, i=in_ap: e.dma_start(o, i),
                 reads=reads, writes=writes, dma=sem)

    dma("gpsimd", "dPREa", TAB[:, :],
        AP(cs_d, 0, [[1025, P], [1025 * P, ND], [1, 1025]]), [], TABKEYS)
    dma("gpsimd", "dPREa", XNT[:, 0:8192],
        AP(wk_d, 0, [[D, P], [D * P, ND], [1, D]]), [], XNTKEYS)
    dma("gpsimd", "dPRE", FW[:, 0:8192],
        AP(wv_d, 0, [[D, P], [D * P, ND], [1, D]]),
        [], [f"FW:0,{c}" for c in range(ND)])
    dma("gpsimd", "dPRE", VAL[:, 0:8192],
        AP(wq_d, 0, [[D, P], [D * P, ND], [1, D]]), [], VKEYS)
    dma("gpsimd", "dPRE", IDN[:, :], AP(id_d, 0, [[P, P], [1, P]]), [], ["IDN"])
    dma("gpsimd", "dPRE", B2R[0:1, :], AP(b2_d, 0, [[D, 1], [1, D]]), [], ["B2R"])
    dma("gpsimd", "dPRE", GNYQ[0:1, :],
        AP(g_d, 1024 * D, [[D, 1], [1, D]]), [], ["GNYQ"])
    dma("sync", "dPS", IDNF[:, :], AP(id_d, 0, [[P, P], [1, P]]),
        [], ["IDNF"])
    for o, src in ((0, cslo_d), (1, cslo2_d)):
        for p in range(2):
            kstg = [f"XNB:{i}" for i in range(5)]
            dma("sync", f"dLOa{o * 2 + p}", XNB[:, 0:4100],
                AP(src, p * 4 * 1025 * P,
                   [[1025, P], [1025 * P, 4], [1, 1025]]),
                kstg, kstg)
            S.op("vector", lambda e, p=p: e.tensor_add(
                TAB[:, p * 4100:(p + 1) * 4100],
                TAB[:, p * 4100:(p + 1) * 4100], XNB[:, 0:4100]),
                reads=TABKEYS[p * 4:(p + 1) * 4] + kstg,
                writes=TABKEYS[p * 4:(p + 1) * 4])
    dma("sync", "dPS", XT[:, :],
        AP(x_d, 0, [[D, P], [D * P, NT], [1, D]]),
        [], [f"XT:{i}" for i in range(NT)])
    _QSTG = {"th": (th_d, THc, 4096, "QC:th"), "wr": (wr_d, WRc, 4224, "QC:wr"),
             "wi": (wi_d, WIc, 4352, "QC:wi"), "b1": (b1_d, B1c, 4480, "QC:b1")}
    for _nm, (_src, _dst, _off, _key) in _QSTG.items():
        dma("sync", "dPS", SCR[0:32, _off:_off + 128],
            AP(_src, 0, [[P, 32], [1, P]]), [], [f"qs:{_nm}"])
    for _qi, (_nm, (_src, _dst, _off, _key)) in enumerate(_QSTG.items()):
        S.op("tensor", lambda e, o=_off, qi=_qi: e.matmul(
            PB[:, 1024 + qi * 32: 1024 + (qi + 1) * 32],
            SCR[0:32, o:o + 128], IDNF[0:32, 0:32], start=True, stop=True),
            reads=[f"qs:{_nm}", "IDNF"], writes=[f"qp:{_nm}", "PB23"])
        S.op("scalar", lambda e, d=_dst, qi=_qi: e.copy(
            d, PB[:, 1024 + qi * 32: 1024 + (qi + 1) * 32]),
            reads=[f"qp:{_nm}", "PB23"], writes=[_key + ":0"])


    S.op("vector", lambda e: e.memset(ONE1[0:1, :], 1.0), writes=["ONE1"])
    S.op("vector", lambda e: e.memset(EPSC[:, :], 1e-30), writes=["EPSC"])
    S.op("vector", lambda e: e.memset(EPSLN[:, :], float(EPS_LN)), writes=["EPSLN"])
    S.op("vector", lambda e: e.memset(EPS8[:, :], float(EPS_FFT)), writes=["EPS8"])

    R0 = SCR[:, 5120:5152]
    T0 = SCR[:, 5152:5184]
    DD = SCR[:, 5184:5216]
    CC = SCR[:, 5216:5248]
    STH = SCR[:, 5248:5280]
    CTH = SCR[:, 5280:5312]
    PQ = SCR[:, 5312:5344]

    def rred(shift):
        S.op("vector", lambda e: e.tensor_scalar(
            R0, THc, float(1 / (2 * np.pi)), float(shift), ALU.mult, ALU.add),
            reads=["QC:th:0"], writes=K_CSC)
        S.op("vector", lambda e: e.tensor_copy(QIb[:, :], R0),
             reads=K_CSC, writes=["QIb"])
        S.op("vector", lambda e: e.tensor_copy(T0, QIb[:, :]),
             reads=["QIb"], writes=K_CSC)
        S.op("vector", lambda e: e.tensor_sub(DD, R0, T0),
             reads=K_CSC, writes=K_CSC)
        S.op("vector", lambda e: e.tensor_single_scalar(CC, DD, 0.5, ALU.is_ge),
             reads=K_CSC, writes=K_CSC)
        S.op("vector", lambda e: e.tensor_add(T0, T0, CC),
             reads=K_CSC, writes=K_CSC)
        S.op("vector", lambda e: e.tensor_single_scalar(CC, DD, -0.5, ALU.is_le),
             reads=K_CSC, writes=K_CSC)
        S.op("vector", lambda e: e.tensor_sub(T0, T0, CC),
             reads=K_CSC, writes=K_CSC)
        S.op("vector", lambda e: e.scalar_tensor_tensor(
            R0, T0, float(-2 * np.pi), THc, ALU.mult, ALU.add),
            reads=K_CSC + ["QC:th:0"], writes=K_CSC)
        if shift:
            S.op("vector", lambda e: e.tensor_scalar_add(
                R0, R0, float(shift * 2 * np.pi)), reads=K_CSC, writes=K_CSC)

    rred(0.0)
    S.op("scalar", lambda e: e.activation(STH, R0, AF.Sin),
         reads=K_CSC, writes=["q:sth"])
    rred(0.25)
    S.op("scalar", lambda e: e.activation(CTH, R0, AF.Sin),
         reads=K_CSC, writes=["q:cth"])
    S.op("vector", lambda e: e.tensor_mul(PQ, CTH, WRc),
         reads=["q:cth", "QC:wr:0"], writes=["q:pq"])
    S.op("vector", lambda e: e.tensor_mul(T0, STH, WIc),
         reads=["q:sth", "QC:wi:0", "q:cth"], writes=K_CSC)
    S.op("vector", lambda e: e.tensor_add(PQ, PQ, T0),
         reads=["q:pq"] + K_CSC, writes=["q:pq"])
    S.op("vector", lambda e: e.tensor_mul(T0, PQ, WIc),
         reads=["q:pq", "QC:wi:0"], writes=K_CSC)
    S.op("vector", lambda e: e.tensor_scalar_mul(T0, T0, 0.1),
         reads=K_CSC, writes=K_CSC)
    S.op("vector", lambda e: e.tensor_mul(Ac, T0, STH),
         reads=K_CSC + ["q:sth"], writes=["QC:A"])
    S.op("vector", lambda e: e.tensor_mul(Bc, T0, CTH),
         reads=K_CSC + ["q:cth"], writes=["QC:B"])
    S.op("vector", lambda e: e.tensor_scalar_mul(SB1, B1c, 0.1),
         reads=["QC:b1:0"], writes=["QC:SB1"])
    S.op("vector", lambda e: e.tensor_scalar(
        SC1, B1c, 0.1, float(np.pi / 2), ALU.mult, ALU.add),
        reads=["QC:b1:0"], writes=["QC:SC1"])

    def fin(par, s_ap, ss_ap, r_ap, b_ap, s_key, ss_key, r_key, b_key):
        m_ap, q_ap, v_ap = stc(par, 2), stc(par, 3), stc(par, 4)
        km, kq, kv = f"st:m{par}", f"st:q{par}", f"st:v{par}"
        S.op("vector", lambda e: e.tensor_scalar_mul(m_ap, s_ap, float(1.0 / D)),
             reads=[s_key], writes=[km])
        S.op("vector", lambda e: e.tensor_scalar(q_ap, m_ap, m_ap, None, ALU.mult),
             reads=[km], writes=[kq])
        S.op("vector", lambda e: e.scalar_tensor_tensor(
            v_ap, ss_ap, float(1.0 / D), q_ap, ALU.mult, ALU.subtract),
            reads=[ss_key, kq], writes=[kv])
        S.op("scalar", lambda e: e.activation(v_ap, v_ap, AF.Ln,
                                              bias=EPSLN[:, 0:1]),
             reads=[kv, "EPSLN"], writes=[kv])
        S.op("scalar", lambda e: e.activation(r_ap, v_ap, AF.Exp, scale=-0.5),
             reads=[kv], writes=[r_key])
        S.op("vector", lambda e: e.tensor_scalar(
            b_ap, m_ap, r_ap, -1.0, ALU.mult, ALU.mult),
            reads=[km, r_key], writes=[b_key])

    def ln_tile(i, out_ap, out_key, s_pre=None):
        par = i % 4
        TRp, K_TRp = post_slices(par)["TRASH"]
        sa = STAT[:, ST_S1 + i: ST_S1 + i + 1]
        ssa = STAT[:, ST_SS1 + i: ST_SS1 + i + 1]
        ra = STAT[:, ST_R1 + i: ST_R1 + i + 1]
        ba = STAT[:, ST_B1 + i: ST_B1 + i + 1]
        ks, kss = f"st:S1:{i}", f"st:SS1:{i}"
        kr, kb = f"st:R1:{i}", f"st:B1:{i}"
        if s_pre is None:
            S.op("vector", lambda e: e.reduce_sum(sa, XT[:, i * D:(i + 1) * D], AX.X),
                 reads=[f"XT:{i}"], writes=[ks])
        else:
            ks = s_pre
        S.op("vector", lambda e: e.scalar_tensor_tensor(
            TRp, XT[:, i * D:(i + 1) * D], 1.0, XT[:, i * D:(i + 1) * D],
            ALU.mult, ALU.mult, accum_out=ssa),
            reads=[f"XT:{i}"], writes=[kss] + K_TRp)
        fin(par, sa, ssa, ra, ba, ks, kss, kr, kb)
        S.op("scalar", lambda e: e.activation(
            out_ap, XT[:, i * D:(i + 1) * D], AF.Identity, bias=ba, scale=ra),
            reads=[f"XT:{i}", kr, kb], writes=[out_key])

    def transpose_tile(i, src_key, dst_pt, dst_off, pk, slot=None,
                       evac_eng="scalar"):
        sl = (i % 2) if slot is None else slot
        def mmtr(e, i=i, dst_pt=dst_pt, dst_off=dst_off, sl=sl):
            last = None
            for c in range(ND):
                last = e.matmul(dst_pt[:, dst_off + c * P: dst_off + (c + 1) * P],
                                XNB[:, sl * D + c * P: sl * D + (c + 1) * P],
                                IDN[:, :], start=True, stop=True)
            return last
        S.op("tensor", mmtr, reads=[src_key, "IDN"], writes=[pk])
        if evac_eng == "scalar":
            S.op("scalar", lambda e, i=i: e.copy(
                bass.AP(XNT, i * P, [[ND * H, P], [H, ND], [1, P]]),
                dst_pt[:, dst_off: dst_off + 1024]),
                reads=[pk], writes=[f"XNT:{i}"])
        else:
            S.op("vector", lambda e, i=i: e.tensor_copy(
                bass.AP(XNT, i * P, [[ND * H, P], [H, ND], [1, P]]),
                dst_pt[:, dst_off: dst_off + 1024]),
                reads=[pk], writes=[f"XNT:{i}"])

    def fold_one(w, stage_keys, stage_t):
        for c in range(ND):
            par = (w * ND + c) % 2
            PT = PA if par == 0 else PB
            pk = ["PA01", "PA23"] if par == 0 else ["PB01", "PB23"]

            def mmf(e, PT=PT, st_t=stage_t, c=c, region=0):
                last = None
                for r in range(ND):
                    lhsT = st_t[:, r * D + c * P: r * D + (c + 1) * P]
                    if region == 0:
                        last = e.matmul(PT[:, 0:512], lhsT,
                                        TAB[:, r * 1025: r * 1025 + 512],
                                        start=(r == 0), stop=(r == ND - 1))
                    elif region == 1:
                        last = e.matmul(PT[:, 512:1024], lhsT,
                                        TAB[:, r * 1025 + 512: r * 1025 + 1024],
                                        start=(r == 0), stop=(r == ND - 1))
                    else:
                        last = e.matmul(PT[:, 1024:1025], lhsT,
                                        TAB[:, r * 1025 + 1024: r * 1025 + 1025],
                                        start=(r == 0), stop=(r == ND - 1))
                return last
            S.op("tensor", lambda e, f=mmf: f(e, region=0),
                 reads=stage_keys + TABKEYS, writes=[pk[0]])
            S.op("tensor", lambda e, f=mmf: f(e, region=1),
                 reads=stage_keys + TABKEYS, writes=[pk[0]])
            S.op("tensor", lambda e, f=mmf: f(e, region=2),
                 reads=stage_keys + TABKEYS, writes=[pk[1]])
            S.op("scalar", lambda e, PT=PT, w=w, c=c: e.copy(
                FW[:, (w * ND + c) * D:(w * ND + c + 1) * D], PT[:, 0:1024]),
                reads=[pk[0]], writes=[f"FW:{w},{c}"])
            S.op("scalar", lambda e, PT=PT, w=w, c=c: e.copy(
                FWN[:, c * 3 + w: c * 3 + w + 1], PT[:, 1024:1025]),
                reads=[pk[1]], writes=[f"FWN:{w},{c}"])


    ST_M1 = 40

    def ln1_chains():
        for i in range(NT):
            sa = STAT[:, ST_S1 + i: ST_S1 + i + 1]
            ssa = STAT[:, ST_SS1 + i: ST_SS1 + i + 1]
            TRp, K_TRp = post_slices(i)["TRASH"]
            S.op("vector", lambda e, i=i, sa=sa: e.reduce_sum(
                sa, XT[:, i * D:(i + 1) * D], AX.X),
                reads=[f"XT:{i}"], writes=[f"st:S1:{i}"])
            S.op("vector", lambda e, i=i, ssa=ssa, TRp=TRp: e.scalar_tensor_tensor(
                TRp, XT[:, i * D:(i + 1) * D], 1.0, XT[:, i * D:(i + 1) * D],
                ALU.mult, ALU.mult, accum_out=ssa),
                reads=[f"XT:{i}"], writes=[f"st:SS1:{i}"] + K_TRp)
        S8 = STAT[:, ST_S1:ST_S1 + 8]
        SS8 = STAT[:, ST_SS1:ST_SS1 + 8]
        M8 = STAT[:, ST_M1:ST_M1 + 8]
        R8 = STAT[:, ST_R1:ST_R1 + 8]
        B8 = STAT[:, ST_B1:ST_B1 + 8]
        rdall = [f"st:S1:{i}" for i in range(NT)] + [f"st:SS1:{i}" for i in range(NT)]
        S.op("vector", lambda e: e.tensor_scalar_mul(M8, S8, float(1.0 / D)),
             reads=rdall, writes=["st:M8"])
        S.op("vector", lambda e: e.tensor_mul(R8, M8, M8),
             reads=["st:M8"], writes=["st:R8t"])
        S.op("vector", lambda e: e.scalar_tensor_tensor(
            R8, SS8, float(1.0 / D), R8, ALU.mult, ALU.subtract),
            reads=rdall + ["st:R8t"], writes=["st:R8t"])
        S.op("scalar", lambda e: e.activation(R8, R8, AF.Ln,
                                              bias=EPSLN[:, 0:1]),
             reads=["st:R8t", "EPSLN"], writes=["st:R8t"])
        S.op("scalar", lambda e: e.activation(R8, R8, AF.Exp, scale=-0.5),
             reads=["st:R8t"], writes=["st:R8"])
        S.op("vector", lambda e: e.tensor_mul(B8, M8, R8),
             reads=["st:M8", "st:R8"], writes=["st:B8t"])
        S.op("vector", lambda e: e.tensor_scalar_mul(B8, B8, -1.0),
             reads=["st:B8t"], writes=["st:B8"])
        for i in range(NT):
            TRp, K_TRp = post_slices(i)["TRASH"]
            UUp, K_UUp = post_slices(i)["UU"]
            S.op("vector", lambda e, i=i, UUp=UUp: e.tensor_scalar_mul(
                UUp, XT[:, i * D:(i + 1) * D], STAT[:, ST_R1 + i: ST_R1 + i + 1]),
                reads=[f"XT:{i}", "st:R8"], writes=K_UUp)
            S.op("vector", lambda e, i=i, UUp=UUp: e.tensor_scalar_add(
                XNB[:, i * D:(i + 1) * D], UUp,
                STAT[:, ST_B1 + i: ST_B1 + i + 1]),
                reads=K_UUp + ["st:B8"], writes=[f"XNB:{i}"])

    def ln1_transposes():
        for i in range(NT):
            par = i % 2
            transpose_tile(i, f"XNB:{i}", PA if par == 0 else PB, 0,
                           "PA01" if par == 0 else "PB01", slot=i)

    def ln1_phase():
        for t in range(NT + 1):
            if 0 <= t - 1 < NT:
                i = t - 1
                par = i % 2
                transpose_tile(i, f"XNB:{par}", PA if par == 0 else PB, 0,
                               "PA01" if par == 0 else "PB01")
            if t < NT:
                ln_tile(t, XNB[:, (t % 2) * D:(t % 2) * D + D], f"XNB:{t % 2}")

    def do_half(h, do_ln1=True):
        if do_ln1:
            ln1_chains()
            ln1_transposes()

        def mmnyq(e):
            last = None
            for c in range(ND):
                for sgm in range(2):
                    last = e.matmul(
                        PB[0:3, 1024 + sgm * 512: 1024 + (sgm + 1) * 512],
                        FWN[:, c * 3:(c + 1) * 3],
                        XNT[:, c * H + sgm * 512: c * H + (sgm + 1) * 512],
                        start=(c == 0), stop=(c == ND - 1))
            return last
        S.op("tensor", mmnyq,
             reads=[f"FWN:{w},{c}" for w in range(3) for c in range(ND)] + XNTKEYS,
             writes=["PB23"])
        NYb = 4096
        NYT = SCR[0:3, NYb:NYb + 1024]
        NYT32 = SCR[0:32, NYb:NYb + 1024]
        NYA = SCR[0:2, NYb + 1024:NYb + 2048]
        SHA = SCR[0:32, NYb + 2048:NYb + 3072]
        SHBWS = SCR[0:32, NYb + 3072:NYb + 4096]
        K_NYT = scrk(NYb, NYb + 1024)
        K_NYA = scrk(NYb + 1024, NYb + 2048)
        K_SHA = scrk(NYb + 2048, NYb + 3072)
        K_SHB = scrk(NYb + 3072, NYb + 4096)
        S.op("vector", lambda e: e.memset(NYT32, 0.0), reads=[], writes=K_NYT)
        S.op("scalar", lambda e: e.copy(NYT, PB[0:3, 1024:2048]),
             reads=["PB23"] + K_NYT, writes=K_NYT)
        S.op("scalar", lambda e: e.activation(NYA, SCR[0:2, NYb:NYb + 1024], AF.Abs),
             reads=K_NYT, writes=K_NYA)
        S.op("scalar", lambda e: e.activation(NYA, NYA, AF.Ln,
                                              bias=EPS8[0:2, 0:1]),
             reads=K_NYA + ["EPS8"], writes=K_NYA)
        S.op("scalar", lambda e: e.activation(NYA, NYA, AF.Exp, scale=-1.0),
             reads=K_NYA, writes=K_NYA)
        S.op("vector", lambda e: e.tensor_mul(
            SCR[0:2, NYb:NYb + 1024], SCR[0:2, NYb:NYb + 1024], NYA),
            reads=K_NYT + K_NYA, writes=K_NYT)
        mka = list(range(32)); mka[0] = 1
        mkb = list(range(32)); mkb[0] = 2
        S.op("vector", lambda e: e.stream_shuffle(SHA, NYT32, mka),
             reads=K_NYT, writes=K_SHA)
        S.op("vector", lambda e: e.stream_shuffle(SHBWS, NYT32, mkb),
             reads=K_NYT, writes=K_SHB)
        S.op("vector", lambda e: e.tensor_mul(
            SCR[0:1, NYb + 1024:NYb + 2048], SHA[0:1, :], SHBWS[0:1, :]),
            reads=K_SHA + K_SHB, writes=K_NYA)
        inin = 0.0 if h == 0 else CAR[0:1, 9:10]
        rdn = K_NYA + ([] if h == 0 else ["car:n"])
        S.op("vector", lambda e, inin=inin: e.tensor_tensor_scan(
            SCR[0:1, NYb + 3072:NYb + 4096], SCR[0:1, NYb + 1024:NYb + 2048],
            SCR[0:1, NYb + 1024:NYb + 2048], inin, ALU.add, ALU.bypass),
            reads=rdn, writes=K_SHB)
        S.op("vector", lambda e: e.tensor_copy(
            CAR[0:1, 9:10], SCR[0:1, NYb + 4095:NYb + 4096]),
            reads=K_SHB, writes=["car:n"])
        S.op("vector", lambda e: e.tensor_mul(
            NVAL[0:1, :], SCR[0:1, NYb + 3072:NYb + 4096],
            SCR[0:1, NYb:NYb + 1024]),
            reads=K_SHB + K_NYT, writes=["NVAL"])

        PSLOT = [(PA, 0, "PA01"), (PA, 1024, "PA23"),
                 (PB, 0, "PB01"), (PB, 1024, "PB23")]

        def pr_slot(it, plane):
            return PSLOT[(3 * it + plane) % 4]

        def pr_mm(it):
            q, j = it // NJ, it % NJ
            xnt_r = [f"XNT:{4 * q + t}" for t in range(4)]

            def mmpl(e, w=0, dst=None, dstoff=0, j=j, q=q):
                last = None
                for p in range(2):
                    for c in range(ND):
                        lhsT = FW[:, (w * ND + c) * D + p * F + j * P:
                                  (w * ND + c) * D + p * F + j * P + P]
                        rhs = XNT[:, c * H + q * QW: c * H + (q + 1) * QW]
                        last = e.matmul(
                            dst[:, dstoff + p * F: dstoff + (p + 1) * F],
                            lhsT, rhs, start=(c == 0), stop=(c == ND - 1))
                return last
            for w in range(3):
                dst, off, key = pr_slot(it, w)
                S.op("tensor", lambda e, f=mmpl, w=w, dst=dst, off=off: f(
                    e, w=w, dst=dst, dstoff=off),
                    reads=[f"FW:{w},{c}" for c in range(ND)] + xnt_r,
                    writes=[key])

        def pr_norm(it):
            par = it % 2
            PS = pairs_slices(par)
            BS = bch_slices(par)
            FQN, K_FQN = BS["FQN"]; FKN, K_FKN = BS["FKN"]; TVB, K_TVB = BS["TVB"]
            qd, qo, qk = pr_slot(it, 0)
            kd, ko, kk = pr_slot(it, 1)
            vd, vo, vk = pr_slot(it, 2)
            S.op("scalar", lambda e, TVB=TVB, vd=vd, vo=vo: e.copy(
                TVB, vd[:, vo:vo + 1024]), reads=[vk], writes=K_TVB)
            QSpec = (qd[:, qo:qo + 1024], qk, "SQQ", "M2Q", "RQ", FQN, K_FQN)
            KSpec = (kd[:, ko:ko + 1024], kk, "SQK", "M2K", "RK", FKN, K_FKN)
            sl = {}
            for tag, (psl, pk, SQnm, M2nm, Rnm, Onm, Okey) in (("q", QSpec), ("k", KSpec)):
                sl[tag] = (psl, pk, PS[SQnm], PS[M2nm], PS[Rnm], Onm, Okey)
            for tag in ("q", "k"):
                psl, pk, (SQ, K_SQ), _, _, _, _ = sl[tag]
                S.op("scalar", lambda e, SQ=SQ, psl=psl: e.activation(
                    SQ, psl, AF.Square), reads=[pk], writes=K_SQ)
            for tag in ("q", "k"):
                _, _, (SQ, K_SQ), (M2, K_M2), _, _, _ = sl[tag]
                S.op("vector", lambda e, M2=M2, SQ=SQ: e.tensor_add(
                    M2, SQ[:, 0:512], SQ[:, 512:1024]), reads=K_SQ, writes=K_M2)
            M2QK, K_M2QK = PS["M2QK"]
            RQK, K_RQK = PS["RQK"]
            S.op("scalar", lambda e, M2QK=M2QK: e.activation(M2QK, M2QK, AF.Ln),
                 reads=K_M2QK, writes=K_M2QK)
            S.op("scalar", lambda e, M2QK=M2QK: e.activation(
                M2QK, M2QK, AF.Exp, scale=0.5), reads=K_M2QK, writes=K_M2QK)
            S.op("scalar", lambda e, M2QK=M2QK: e.activation(
                M2QK, M2QK, AF.Ln, bias=EPS8[:, 0:1]),
                reads=K_M2QK + ["EPS8"], writes=K_M2QK)
            S.op("scalar", lambda e, RQK=RQK, M2QK=M2QK: e.activation(
                RQK, M2QK, AF.Exp, scale=-1.0), reads=K_M2QK, writes=K_RQK)
            for tag in ("q", "k"):
                psl, pk, _, _, (RR, K_RR), Onm, Okey = sl[tag]
                S.op("vector", lambda e, Onm=Onm, psl=psl, RR=RR: e.tensor_tensor(
                    Onm, psl, bass.AP(RR.tensor, RR.offset, [[8192, P], [0, 2], [1, 512]]),
                    ALU.mult), reads=[pk] + K_RR, writes=Okey)

        def pr_tail(it):
            q, j = it // NJ, it % NJ
            gq = h * NQ + q
            par = it % 2
            BS = bch_slices(par)
            FQN, K_FQN = BS["FQN"]; FKN, K_FKN = BS["FKN"]; TVB, K_TVB = BS["TVB"]
            PRD, K_PRD = BS["PRD"]; CSC, K_CSC = BS["CSC"]
            K_C0, K_C1 = [K_CSC[0]], [K_CSC[1]]
            K_P0, K_P1 = [K_PRD[0]], [K_PRD[1]]
            S.op("vector", lambda e, CSC=CSC, FKN=FKN, TVB=TVB: e.tensor_mul(
                CSC[:, 0:512], FKN[:, 0:512], TVB[:, 0:512]),
                reads=K_FKN + K_TVB, writes=K_C0)
            S.op("vector", lambda e, CSC=CSC, FKN=FKN, TVB=TVB: e.tensor_mul(
                CSC[:, 512:1024], FKN[:, 512:1024], TVB[:, 512:1024]),
                reads=K_FKN + K_TVB, writes=K_C1)
            S.op("vector", lambda e, PRD=PRD, CSC=CSC: e.tensor_sub(
                PRD[:, 0:512], CSC[:, 0:512], CSC[:, 512:1024]),
                reads=K_C0 + K_C1, writes=K_P0)
            S.op("vector", lambda e, CSC=CSC, FKN=FKN, TVB=TVB: e.tensor_mul(
                CSC[:, 0:512], FKN[:, 0:512], TVB[:, 512:1024]),
                reads=K_FKN + K_TVB, writes=K_C0)
            S.op("vector", lambda e, CSC=CSC, FKN=FKN, TVB=TVB: e.tensor_mul(
                CSC[:, 512:1024], FKN[:, 512:1024], TVB[:, 0:512]),
                reads=K_FKN + K_TVB, writes=K_C1)
            S.op("vector", lambda e, PRD=PRD, CSC=CSC: e.tensor_add(
                PRD[:, 512:1024], CSC[:, 0:512], CSC[:, 512:1024]),
                reads=K_C0 + K_C1, writes=K_P1)
            for pl in range(2):
                car_ap = CAR[:, 2 * j + pl: 2 * j + pl + 1]
                ini = 0.0 if gq == 0 else car_ap
                rd = [K_PRD[pl]] + ([] if gq == 0 else [f"car:{j},{pl}"])
                S.op("vector", lambda e, pl=pl, ini=ini, CSC=CSC, PRD=PRD:
                     e.tensor_tensor_scan(
                         CSC[:, pl * F:(pl + 1) * F], PRD[:, pl * F:(pl + 1) * F],
                         PRD[:, pl * F:(pl + 1) * F], ini, ALU.add, ALU.bypass),
                     reads=rd, writes=[K_CSC[pl]])
                S.op("vector", lambda e, pl=pl, car_ap=car_ap, CSC=CSC: e.tensor_copy(
                    car_ap, CSC[:, (pl + 1) * F - 1:(pl + 1) * F]),
                    reads=[K_CSC[pl]], writes=[f"car:{j},{pl}"])
            vr = VAL[:, j * H + q * QW: j * H + (q + 1) * QW]
            vi = VAL[:, (4 + j) * H + q * QW: (4 + j) * H + (q + 1) * QW]
            S.op("vector", lambda e, PRD=PRD, CSC=CSC, FQN=FQN: e.tensor_mul(
                PRD[:, 0:512], CSC[:, 0:512], FQN[:, 0:512]),
                reads=K_C0 + K_FQN, writes=K_P0)
            S.op("vector", lambda e, PRD=PRD, CSC=CSC, FQN=FQN: e.tensor_mul(
                PRD[:, 512:1024], CSC[:, 512:1024], FQN[:, 512:1024]),
                reads=K_C1 + K_FQN, writes=K_P1)
            S.op("vector", lambda e, vr=vr, PRD=PRD: e.tensor_add(
                vr, PRD[:, 0:512], PRD[:, 512:1024]),
                reads=K_P0 + K_P1, writes=[f"VAL:{j},{q}"])
            S.op("vector", lambda e, PRD=PRD, CSC=CSC, FQN=FQN: e.tensor_mul(
                PRD[:, 0:512], CSC[:, 512:1024], FQN[:, 0:512]),
                reads=K_C1 + K_FQN, writes=K_P0)
            S.op("vector", lambda e, PRD=PRD, CSC=CSC, FQN=FQN: e.tensor_mul(
                PRD[:, 512:1024], CSC[:, 0:512], FQN[:, 512:1024]),
                reads=K_C0 + K_FQN, writes=K_P1)
            S.op("vector", lambda e, vi=vi, PRD=PRD: e.tensor_sub(
                vi, PRD[:, 0:512], PRD[:, 512:1024]),
                reads=K_P0 + K_P1, writes=[f"VAL:{4 + j},{q}"])

        NIT = NQ * NJ
        for t in range(NIT + 2):
            if 0 <= t - 2 < NIT:
                pr_tail(t - 2)
            if 0 <= t - 1 < NIT:
                pr_norm(t - 1)
            if t < NIT:
                pr_mm(t)

        def po_mm(i):
            par = i % 2
            vh = PA[:, par * 1024: par * 1024 + 1024]
            vhk = "PA01" if par == 0 else "PA23"
            q = i // 4

            def mmir(e, i=i, vh=vh):
                last = None
                for g in range(8):
                    lhsT = VAL[:, g * H + i * P: g * H + (i + 1) * P]
                    for sgm in range(2):
                        last = e.matmul(
                            vh[:, sgm * 512:(sgm + 1) * 512], lhsT,
                            TAB[:, g * D + sgm * 512: g * D + (sgm + 1) * 512],
                            start=(g == 0), stop=False)
                for sgm in range(2):
                    last = e.matmul(vh[:, sgm * 512:(sgm + 1) * 512],
                                    NVAL[0:1, i * P:(i + 1) * P],
                                    GNYQ[0:1, sgm * 512:(sgm + 1) * 512],
                                    start=False, stop=True)
                return last
            S.op("tensor", mmir,
                 reads=[f"VAL:{g},{q}" for g in range(8)] + TABKEYS
                 + ["NVAL", "GNYQ"], writes=[vhk])

        def po_chain(i):
            vh = PA[:, (i % 2) * 1024: (i % 2) * 1024 + 1024]
            vhk = "PA01" if i % 2 == 0 else "PA23"
            par = i % 4
            PO = post_slices(par)
            UU, K_UU = PO["UU"]
            TRASH, K_TRASH = PO["TRASH"]; LNU, K_LNU = PO["LNU"]
            S.op("vector", lambda e, vh=vh, par=par, i=i, UU=UU:
                 e.scalar_tensor_tensor(
                     UU, XNB[:, i * D:(i + 1) * D], 1.0, vh, ALU.mult, ALU.add,
                     accum_out=stc(par, 0)),
                 reads=[f"XNB:{i}", vhk], writes=K_UU + [f"st:S{par}"])
            S.op("scalar", lambda e, par=par, UU=UU, TRASH=TRASH: e.activation(
                TRASH, UU, AF.Square, accum_out=stc(par, 1)),
                reads=K_UU, writes=[f"st:SS{par}"] + K_TRASH)
            fin(par, stc(par, 0), stc(par, 1), stc(par, 5), stc(par, 6),
                f"st:S{par}", f"st:SS{par}", f"st:R{par}", f"st:B{par}")
            S.op("scalar", lambda e, par=par, UU=UU, LNU=LNU: e.activation(
                LNU, UU, AF.Identity, bias=stc(par, 6), scale=stc(par, 5)),
                reads=K_UU + [f"st:R{par}", f"st:B{par}"], writes=K_LNU)
            S.op("vector", lambda e, i=i, LNU=LNU: e.scalar_tensor_tensor(
                XT[:, i * D:(i + 1) * D], LNU, 1.0, XT[:, i * D:(i + 1) * D],
                ALU.mult, ALU.add,
                accum_out=STAT[:, ST_S1 + i: ST_S1 + i + 1]),
                reads=[f"XT:{i}"] + K_LNU,
                writes=[f"XT:{i}", f"st:S1g:{i}"])

        def po_ln3(i):
            par = i % 2
            ln_tile(i, XNB[:, i * D:(i + 1) * D], f"XNB:{i}",
                    s_pre=f"st:S1g:{i}")
            transpose_tile(i, f"XNB:{i}", PB, par * 1024,
                           "PB01" if par == 0 else "PB23", slot=i)

        for t in range(NT + 2):
            if 0 <= t - 2 < NT:
                po_ln3(t - 2)
            if 0 <= t - 1 < NT:
                po_chain(t - 1)
            if t < NT:
                po_mm(t)

        w1zone = [f"FW:0,{c}" for c in range(ND)]
        w2zone = [f"FW:1,{c}" for c in range(ND)]
        for g in range(NG):
            dma("gpsimd", "dW1", FW[:, 0:8192],
                AP(w1_d, g * 1024, [[DF, P], [DF * P, ND], [1, 1024]]),
                [], w1zone)
            dma("gpsimd", "dW2", FW[:, 8192:16384],
                AP(w2_d, g * 1024 * D, [[D, P], [D * P, ND], [1, D]]),
                [], w2zone)
            if h == 0 and g == 1:
                dma("gpsimd", "dXB", XNB[:, :],
                    AP(x_d, H * D, [[D, P], [D * P, NT], [1, D]]),
                    [f"XNB:{i}" for i in range(NT)],
                    [f"XNB:{i}" for i in range(NT)])
            for fc in range(ND):
                Fg = g * ND + fc
                par = fc % 2
                hp = PA[:, par * 1024: par * 1024 + 1024]
                hk = "PA01" if par == 0 else "PA23"
                HS = ffn_h_slices(par)
                S_U, K_SU = HS["S_U"]; C_U, K_CU = HS["C_U"]

                def mmh(e, fc=fc, hp=hp):
                    last = None
                    for c in range(ND):
                        lhsT = FW[:, c * 1024 + fc * P:
                                   c * 1024 + (fc + 1) * P]
                        for sgm in range(2):
                            last = e.matmul(
                                hp[:, sgm * 512:(sgm + 1) * 512], lhsT,
                                XNT[:, c * H + sgm * 512: c * H + (sgm + 1) * 512],
                                start=(c == 0), stop=(c == ND - 1))
                    return last
                S.op("tensor", mmh, reads=w1zone + XNTKEYS, writes=[hk])
                S.op("scalar", lambda e, hp=hp, Fg=Fg, S_U=S_U: e.activation(
                    S_U, hp, AF.Sin, bias=SB1[:, Fg:Fg + 1], scale=0.1),
                    reads=[hk, "QC:SB1"], writes=K_SU)
                S.op("scalar", lambda e, hp=hp, Fg=Fg, C_U=C_U: e.activation(
                    C_U, hp, AF.Sin, bias=SC1[:, Fg:Fg + 1], scale=0.1),
                    reads=[hk, "QC:SC1"], writes=K_CU)
                S.op("vector", lambda e, hp=hp, Fg=Fg, C_U=C_U: e.scalar_tensor_tensor(
                    C_U, C_U, Ac[:, Fg:Fg + 1], hp, ALU.mult, ALU.add),
                    reads=K_CU + [hk, "QC:A"], writes=K_CU)
                S.op("vector", lambda e, Fg=Fg, S_U=S_U, C_U=C_U: e.scalar_tensor_tensor(
                    S_U, S_U, Bc[:, Fg:Fg + 1], C_U, ALU.mult, ALU.add),
                    reads=K_SU + K_CU + ["QC:B"], writes=K_SU)
                S.op("vector", lambda e, Fg=Fg, S_U=S_U: e.tensor_scalar_add(
                    S_U, S_U, B1c[:, Fg:Fg + 1]),
                    reads=K_SU + ["QC:b1:0"], writes=K_SU)
                S.op("vector", lambda e, fc=fc, S_U=S_U: e.tensor_relu(
                    VAL[:, fc * H:(fc + 1) * H], S_U),
                    reads=K_SU, writes=[f"VAL:{fc},0", f"VAL:{fc},1"])
            def emit_ymms(tc):
                par = tc % 2
                yp = PB[:, par * 1024: par * 1024 + 1024]
                yk = "PB01" if par == 0 else "PB23"

                def mmy(e, g=g, tc=tc, yp=yp, fclo=0, fchi=ND):
                    last = None
                    first = (fclo == 0)
                    if g == 0 and fclo == 0:
                        for sgm in range(2):
                            last = e.matmul(yp[:, sgm * 512:(sgm + 1) * 512],
                                            ONE1[0:1, :],
                                            B2R[0:1, sgm * 512:(sgm + 1) * 512],
                                            start=True, stop=False)
                        first = False
                    for fc in range(fclo, fchi):
                        lhsT = VAL[:, fc * H + tc * P: fc * H + (tc + 1) * P]
                        for sgm in range(2):
                            last = e.matmul(
                                yp[:, sgm * 512:(sgm + 1) * 512], lhsT,
                                FW[:, 8192 + fc * 1024 + sgm * 512:
                                   8192 + fc * 1024 + (sgm + 1) * 512],
                                start=(first and fc == fclo),
                                stop=(fc == ND - 1))
                    return last
                rd0 = [f"VAL:{fc},{tc // 4}" for fc in range(4)] + w2zone
                if g == 0:
                    rd0 += ["ONE1", "B2R"]
                S.op("tensor", lambda e, f=mmy: f(e, fclo=0, fchi=4),
                     reads=rd0, writes=[yk])
                S.op("tensor", lambda e, f=mmy: f(e, fclo=4, fchi=ND),
                     reads=[f"VAL:{fc},{tc // 4}" for fc in range(4, ND)] + w2zone,
                     writes=[yk])
                return yp, yk

            if g == 0:
                for tc in range(NT):
                    yp, yk = emit_ymms(tc)
                    S.op("vector", lambda e, tc=tc, yp=yp: e.tensor_copy(
                        FW[:, 16384 + tc * 1024:16384 + (tc + 1) * 1024], yp),
                        reads=[yk], writes=[f"FW:2,{tc}"])
            elif g < NG - 1:
                for tc in range(NT):
                    yp, yk = emit_ymms(tc)
                    S.op("vector", lambda e, tc=tc, yp=yp: e.scalar_tensor_tensor(
                        FW[:, 16384 + tc * 1024:16384 + (tc + 1) * 1024], yp, 1.0,
                        FW[:, 16384 + tc * 1024:16384 + (tc + 1) * 1024],
                        ALU.mult, ALU.add),
                        reads=[yk, f"FW:2,{tc}"], writes=[f"FW:2,{tc}"])
            else:
                if h == 0:
                    dma("gpsimd", "dFWL", FW[:, 0:8192],
                        AP(fws_d, 0, [[24576, P], [1, 8192]]),
                        ["FWSPILL"], [f"FW:0,{c}" for c in range(ND)])

                def y_a(tc):
                    par = tc % 2
                    yp = PB[:, par * 1024: par * 1024 + 1024]
                    yk = "PB01" if par == 0 else "PB23"
                    YS = ffn_y_slices(par)
                    U2, K_U2 = YS["U2"]; TR4, K_TR4 = YS["TR4"]
                    S.op("vector", lambda e, tc=tc, yp=yp, par=par, U2=U2:
                         e.scalar_tensor_tensor(
                             U2, yp, 1.0,
                             FW[:, 16384 + tc * 1024:16384 + (tc + 1) * 1024],
                             ALU.mult, ALU.add, accum_out=stc(par, 0)),
                         reads=[yk, f"FW:2,{tc}"], writes=K_U2 + [f"st:S{par}"])
                    S.op("scalar", lambda e, par=par, U2=U2, TR4=TR4: e.activation(
                        TR4, U2, AF.Square, accum_out=stc(par, 1)),
                        reads=K_U2, writes=[f"st:SS{par}"] + K_TR4)

                def y_b(tc):
                    par = tc % 2
                    fin(par, stc(par, 0), stc(par, 1), stc(par, 5), stc(par, 6),
                        f"st:S{par}", f"st:SS{par}", f"st:R{par}", f"st:B{par}")

                def y_c(tc):
                    par = tc % 2
                    YS = ffn_y_slices(par)
                    U2, K_U2 = YS["U2"]; LNY, K_LNY = YS["LNY"]
                    S.op("scalar", lambda e, par=par, U2=U2, LNY=LNY: e.activation(
                        LNY, U2, AF.Identity, bias=stc(par, 6), scale=stc(par, 5)),
                        reads=K_U2 + [f"st:R{par}", f"st:B{par}"], writes=K_LNY)
                    OUTS, K_OUTS = mkslice(par * 1024, 0, 1024)
                    S.op("gpsimd", lambda e, tc=tc, LNY=LNY, OUTS=OUTS:
                         e.tensor_add(OUTS, XT[:, tc * D:(tc + 1) * D], LNY),
                         reads=[f"XT:{tc}"] + K_LNY, writes=K_OUTS)
                    dma("sync", f"dST{h}",
                        AP(out_d, (h * H + tc * P) * D, [[D, P], [1, D]]),
                        OUTS, K_OUTS, [])
                    if h == 0 and tc >= 1:
                        dma("scalar", "dX1", XT[:, (tc - 1) * D: tc * D],
                            AP(x_d, (H + (tc - 1) * P) * D, [[D, P], [1, D]]),
                            [], [f"XT:{tc - 1}"])

                def prep_stat(i):
                    sa = STAT[:, ST_S1 + i: ST_S1 + i + 1]
                    ssa = STAT[:, ST_SS1 + i: ST_SS1 + i + 1]
                    tb = 5120 + (i % 2) * 1024
                    TRb = BCH[:, tb: tb + 1024]
                    K_TRb = bchk(tb, tb + 1024)
                    S.op("vector", lambda e, i=i, sa=sa: e.reduce_sum(
                        sa, XNB[:, i * D:(i + 1) * D], AX.X),
                        reads=[f"XNB:{i}"], writes=[f"st:S1:{i}"])
                    S.op("scalar", lambda e, i=i, ssa=ssa, TRb=TRb:
                         e.activation(TRb, XNB[:, i * D:(i + 1) * D],
                                      AF.Square, accum_out=ssa),
                         reads=[f"XNB:{i}"], writes=[f"st:SS1:{i}"] + K_TRb)

                for t in range(NT + 3):
                    if 0 <= t - 3 < NT:
                        y_c(t - 3)
                    if 0 <= t - 2 < NT:
                        y_b(t - 2)
                        if h == 0:
                            prep_stat(t - 2)
                    if 0 <= t - 1 < NT:
                        y_a(t - 1)
                    if t < NT:
                        emit_ymms(t)
                    if h == 0 and t == NT:
                        dma("gpsimd", "dFWL", FW[:, 8192:16384],
                            AP(fws_d, 8192, [[24576, P], [1, 8192]]),
                            ["FWSPILL"], [f"FW:1,{c}" for c in range(ND)])
                    if h == 0 and t == NT + 2:
                        dma("gpsimd", "dFWL", FW[:, 16384:24576],
                            AP(fws_d, 16384, [[24576, P], [1, 8192]]),
                            ["FWSPILL"], [f"FW:2,{c}" for c in range(ND)])
                if h == 0:
                    dma("scalar", "dX1", XT[:, (NT - 1) * D: NT * D],
                        AP(x_d, (H + (NT - 1) * P) * D, [[D, P], [1, D]]),
                        [], [f"XT:{NT - 1}"])
                    S8 = STAT[:, ST_S1:ST_S1 + 8]
                    SS8 = STAT[:, ST_SS1:ST_SS1 + 8]
                    M8 = STAT[:, 64:72]
                    R8 = STAT[:, ST_R1:ST_R1 + 8]
                    B8 = STAT[:, ST_B1:ST_B1 + 8]
                    rdall = ([f"st:S1:{i}" for i in range(NT)]
                             + [f"st:SS1:{i}" for i in range(NT)])
                    S.op("vector", lambda e, M8=M8, S8=S8: e.tensor_scalar_mul(
                        M8, S8, float(1.0 / D)), reads=rdall, writes=["st:M8"])
                    S.op("vector", lambda e, M8=M8, R8=R8: e.tensor_mul(
                        R8, M8, M8), reads=["st:M8"], writes=["st:R8t"])
                    S.op("vector", lambda e, R8=R8, SS8=SS8:
                         e.scalar_tensor_tensor(
                             R8, SS8, float(1.0 / D), R8, ALU.mult,
                             ALU.subtract),
                         reads=rdall + ["st:R8t"], writes=["st:R8t"])
                    S.op("scalar", lambda e, R8=R8: e.activation(
                        R8, R8, AF.Ln, bias=EPSLN[:, 0:1]),
                        reads=["st:R8t", "EPSLN"], writes=["st:R8t"])
                    S.op("scalar", lambda e, R8=R8: e.activation(
                        R8, R8, AF.Exp, scale=-0.5),
                        reads=["st:R8t"], writes=["st:R8"])
                    S.op("vector", lambda e, B8=B8, M8=M8, R8=R8: e.tensor_mul(
                        B8, M8, R8), reads=["st:M8", "st:R8"],
                        writes=["st:B8t"])
                    S.op("vector", lambda e, B8=B8: e.tensor_scalar_mul(
                        B8, B8, -1.0), reads=["st:B8t"], writes=["st:B8"])
                    for i in range(NT):
                        S.op("scalar", lambda e, i=i: e.activation(
                            XNB[:, i * D:(i + 1) * D],
                            XNB[:, i * D:(i + 1) * D], AF.Identity,
                            bias=STAT[:, ST_B1 + i: ST_B1 + i + 1],
                            scale=STAT[:, ST_R1 + i: ST_R1 + i + 1]),
                            reads=[f"XNB:{i}", "st:R8", "st:B8"],
                            writes=[f"XNB:{i}"])
                        transpose_tile(i, f"XNB:{i}",
                                       PA if i % 2 == 0 else PB, 0,
                                       "PA01" if i % 2 == 0 else "PB01",
                                       slot=i, evac_eng="vector")

    fold_one(1, XNTKEYS, XNT)
    ln1_chains()
    fold_one(2, [f"FW:0,{c}" for c in range(ND)], FW)
    fold_one(0, VKEYS, VAL)
    ln1_transposes()
    dma("scalar", "dSPL", fws_d.ap(), FW[:, :], FWKEYS, ["FWSPILL"])
    dma("gpsimd", "dG", TAB[:, 0:8192],
        AP(g_d, 0, [[D, P], [D * P, ND], [1, D]]), TABKEYS, TABKEYS)
    dma("gpsimd", "dLOb0", TAB[:, 0:8192],
        AP(glo_d, 0, [[D, P], [D * P, ND], [1, D]]), TABKEYS, TABKEYS,
        accum=True)
    do_half(0, do_ln1=False)
    do_half(1, do_ln1=False)

    prog = S.build_prog()

    with es:
        blk = es.enter_context(nc.Block())

        def run_eng(engobj, name):
            for wl, emit, inc in prog[name]:
                for semname, val in wl:
                    engobj.wait_ge(sems[semname], val)
                inst = emit(engobj)
                if inc is not None:
                    inst.then_inc(sems[inc[0]], 1 if inc[0].startswith("E")
                                  else 16)
            if name == "sync":
                engobj.wait_ge(sems["dST0"], 16 * NT)
                engobj.wait_ge(sems["dST1"], 16 * NT)

        @blk.sync
        def _(e):
            run_eng(e, "sync")

        @blk.gpsimd
        def _(e):
            run_eng(e, "gpsimd")

        @blk.tensor
        def _(e):
            run_eng(e, "tensor")

        @blk.vector
        def _(e):
            run_eng(e, "vector")

        @blk.scalar
        def _(e):
            run_eng(e, "scalar")

    return nc


_NC_CACHE = [None]


def kernel(L0_attn_x, L0_attn_Wq, L0_attn_Wk, L0_attn_Wv, L0_ffn_W1, L0_ffn_b1,
           L0_ffn_quantum_phases, L0_ffn_quantum_weights_real,
           L0_ffn_quantum_weights_imag, L0_ffn_W2, L0_ffn_b2):
    cs, cslo, cslo2, g, glo, ident = make_tables()
    x = np.ascontiguousarray(np.asarray(L0_attn_x, dtype=np.float32))
    shared = {
        "wq": np.ascontiguousarray(np.asarray(L0_attn_Wq, np.float32)),
        "wk": np.ascontiguousarray(np.asarray(L0_attn_Wk, np.float32)),
        "wv": np.ascontiguousarray(np.asarray(L0_attn_Wv, np.float32)),
        "w1": np.ascontiguousarray(np.asarray(L0_ffn_W1, np.float32)),
        "w2": np.ascontiguousarray(np.asarray(L0_ffn_W2, np.float32)),
        "b1": np.ascontiguousarray(np.asarray(L0_ffn_b1, np.float32)),
        "b2": np.ascontiguousarray(np.asarray(L0_ffn_b2, np.float32)),
        "qth": np.ascontiguousarray(np.asarray(L0_ffn_quantum_phases, np.float32)),
        "qwr": np.ascontiguousarray(
            np.asarray(L0_ffn_quantum_weights_real, np.float32)),
        "qwi": np.ascontiguousarray(
            np.asarray(L0_ffn_quantum_weights_imag, np.float32)),
        "cs": cs, "cslo": cslo, "cslo2": cslo2, "gt": g, "gtlo": glo,
        "identc": ident,
    }
    if _NC_CACHE[0] is None:
        _NC_CACHE[0] = build()
    nc = _NC_CACHE[0]
    in_maps = [dict(shared, x=np.ascontiguousarray(x[b])) for b in range(8)]
    res = run_bass_kernel_spmd(nc, in_maps, core_ids=list(range(8)))
    return np.stack([res.results[b]["out"] for b in range(8)], axis=0)


if __name__ == "__main__":
    import reference
    inp = reference.setup_inputs(0)
    out = kernel(**{k: np.asarray(v) for k, v in inp.items()})
    print(out.shape, out.dtype)


# version 43
# speedup vs baseline: 1.0090x; 1.0090x over previous
import contextlib
import ml_dtypes
import numpy as np
import concourse.bass as bass
import concourse.mybir as mybir
from concourse.bass_utils import run_bass_kernel_spmd

F32 = mybir.dt.float32
I32 = mybir.dt.int32
BF16 = mybir.dt.bfloat16
AF = mybir.ActivationFunctionType
ALU = mybir.AluOpType
AX = mybir.AxisListType

P = 128
D = 1024
ND = 8
T = 2048
H = 1024
NT = 8
F = 512
NJ = 4
NQ = 2
QW = 512
DF = 4096
NG = 4
EPS_FFT = 1e-8
EPS_LN = 1e-5

ENGS = ("sync", "gpsimd", "tensor", "vector", "scalar")


def make_tables():
    e = np.arange(D, dtype=np.float64)
    f = np.arange(F, dtype=np.float64)
    ang = 2.0 * np.pi * np.outer(e, f) / D
    cs64 = np.concatenate(
        [np.cos(ang), -np.sin(ang), np.cos(np.pi * e)[:, None]], axis=1)
    cs = cs64.astype(np.float32)
    cslo = (cs64 - cs.astype(np.float64)).astype(ml_dtypes.bfloat16)
    cslo2 = (cs64 - cs.astype(np.float64)
             - cslo.astype(np.float64)).astype(ml_dtypes.bfloat16)
    d = np.arange(D, dtype=np.float64)
    cf = np.where(f == 0, 1.0, 2.0)
    angG = 2.0 * np.pi * np.outer(f, d) / D
    g64 = np.concatenate(
        [cf[:, None] * np.cos(angG) / D,
         -cf[:, None] * np.sin(angG) / D,
         (np.cos(np.pi * d) / D)[None, :]], axis=0)
    g = g64.astype(np.float32)
    glo = (g64 - g.astype(np.float64)).astype(ml_dtypes.bfloat16)
    ident = np.eye(P, dtype=np.float32)
    return cs, cslo, cslo2, g, glo, ident


class Op:
    __slots__ = ("eng", "emit", "reads", "writes", "dma", "opid", "inc")

    def __init__(self, eng, emit, reads, writes, dma):
        self.eng, self.emit = eng, emit
        self.reads, self.writes = list(reads), list(writes)
        self.dma = dma
        self.inc = None


class Sched:
    def __init__(self):
        self.ops = []
        self.writer = {}
        self.readers = {}
        self.deps = []
        self.dma_mode = {}
        self.dma_last = {}
        self.batch_members = {}

    def dma_sem(self, name, mode):
        self.dma_mode[name] = mode
        self.batch_members[name] = []

    def op(self, eng, emit, reads=(), writes=(), dma=None):
        o = Op(eng, emit, reads, writes, dma)
        o.opid = len(self.ops)
        dep = set()
        for k in o.reads:
            w = self.writer.get(k)
            if w is not None:
                dep.add(w)
        for k in o.writes:
            w = self.writer.get(k)
            if w is not None:
                dep.add(w)
            for r in self.readers.get(k, ()):
                dep.add(r)
        if dma is not None:
            if self.dma_mode[dma] == "serial" and self.dma_last.get(dma) is not None:
                dep.add(self.dma_last[dma])
            self.dma_last[dma] = o.opid
            self.batch_members[dma].append(o.opid)
        dep.discard(o.opid)
        self.ops.append(o)
        self.deps.append(dep)
        for k in o.reads:
            self.readers.setdefault(k, []).append(o.opid)
        for k in o.writes:
            self.writer[k] = o.opid
            self.readers[k] = []
        return o.opid

    def build_prog(self):
        needed = set()
        for dep in self.deps:
            needed.update(dep)
        eng_count = {e: 0 for e in ENGS}
        for o in self.ops:
            if o.dma is not None:
                idx = self.batch_members[o.dma].index(o.opid) + 1
                o.inc = (o.dma, 16 * idx)
            elif o.opid in needed:
                eng_count[o.eng] += 1
                o.inc = ("E" + o.eng, eng_count[o.eng])
        prog = {e: [] for e in ENGS}
        watermark = {e: {} for e in ENGS}
        for o in self.ops:
            waits = {}
            for dd in sorted(self.deps[o.opid]):
                p = self.ops[dd]
                assert p.inc is not None, f"dep {dd} of {o.opid} has no inc"
                sem, val = p.inc
                if p.dma is not None and self.dma_mode[p.dma] == "batch":
                    val = 16 * len(self.batch_members[p.dma])
                waits[sem] = max(waits.get(sem, 0), val)
            wl = []
            for sem, val in waits.items():
                if watermark[o.eng].get(sem, 0) < val:
                    watermark[o.eng][sem] = val
                    wl.append((sem, val))
            prog[o.eng].append((wl, o.emit, o.inc))
        return prog


def build():
    nc = bass.Bass("TRN2", target_bir_lowering=False)
    AP = bass.AP

    x_d = nc.dram_tensor("x", [T, D], F32, kind="ExternalInput")
    wq_d = nc.dram_tensor("wq", [D, D], F32, kind="ExternalInput")
    wk_d = nc.dram_tensor("wk", [D, D], F32, kind="ExternalInput")
    wv_d = nc.dram_tensor("wv", [D, D], F32, kind="ExternalInput")
    w1_d = nc.dram_tensor("w1", [D, DF], F32, kind="ExternalInput")
    w2_d = nc.dram_tensor("w2", [DF, D], F32, kind="ExternalInput")
    b1_d = nc.dram_tensor("b1", [DF], F32, kind="ExternalInput")
    b2_d = nc.dram_tensor("b2", [D], F32, kind="ExternalInput")
    th_d = nc.dram_tensor("qth", [DF], F32, kind="ExternalInput")
    wr_d = nc.dram_tensor("qwr", [DF], F32, kind="ExternalInput")
    wi_d = nc.dram_tensor("qwi", [DF], F32, kind="ExternalInput")
    cs_d = nc.dram_tensor("cs", [D, 2 * F + 1], F32, kind="ExternalInput")
    cslo_d = nc.dram_tensor("cslo", [D, 2 * F + 1], BF16, kind="ExternalInput")
    cslo2_d = nc.dram_tensor("cslo2", [D, 2 * F + 1], BF16, kind="ExternalInput")
    g_d = nc.dram_tensor("gt", [2 * F + 1, D], F32, kind="ExternalInput")
    glo_d = nc.dram_tensor("gtlo", [2 * F + 1, D], BF16, kind="ExternalInput")
    id_d = nc.dram_tensor("identc", [P, P], F32, kind="ExternalInput")
    out_d = nc.dram_tensor("out", [T, D], F32, kind="ExternalOutput")
    fws_d = nc.dram_tensor("fwspill", [P, 24576], BF16)

    es = contextlib.ExitStack()
    sbuf = lambda n, s, dt=F32: es.enter_context(nc.sbuf_tensor(n, s, dt))
    XT = sbuf("XT", [P, NT * D])
    XNT = sbuf("XNT", [P, ND * H], BF16)
    FW = sbuf("FW", [P, 24576], BF16)
    VAL = sbuf("VAL", [P, 8 * H], BF16)
    TAB = sbuf("TAB", [P, ND * 1025], BF16)
    FWN = sbuf("FWN", [P, 24], BF16)
    SCR = sbuf("SCR", [P, 8192])
    BCH = sbuf("BCH", [P, 10240], BF16)
    XNB = sbuf("XNB", [P, 8 * D], BF16)
    CAR = sbuf("CAR", [P, 16])
    STAT = sbuf("STAT", [P, 96])
    QC = sbuf("QC", [P, 256])
    QIb = sbuf("QIb", [P, 32], I32)
    NVAL = sbuf("NVAL", [1, H], BF16)
    GNYQ = sbuf("GNYQ", [1, D], BF16)
    B2R = sbuf("B2R", [1, D], BF16)
    ONE1 = sbuf("ONE1", [1, P], BF16)
    EPSC = sbuf("EPSC", [P, 1])
    EPSLN = sbuf("EPSLN", [P, 1])
    EPS8 = sbuf("EPS8", [P, 1])
    IDN = sbuf("IDN", [P, P], BF16)
    IDNF = sbuf("IDNF", [P, P])
    PA = es.enter_context(nc.psum_tensor("PA", [P, 2048], F32))
    PB = es.enter_context(nc.psum_tensor("PB", [P, 2048], F32))

    semnames = ["Esync", "Egpsimd", "Etensor", "Evector", "Escalar",
                "dPRE", "dPREa", "dPS", "dG", "dW1", "dW2", "dX1", "dSPL", "dFWL",
                "dST0", "dST1", "dXB", "dLOa0", "dLOa1", "dLOa2", "dLOa3",
                "dLOb0", "dLOb1", "dLOb2", "dLOb3"]
    sems = {s: es.enter_context(nc.semaphore(s)) for s in semnames}

    S = Sched()
    for s, m in (("dPRE", "batch"), ("dPREa", "batch"), ("dPS", "batch"), ("dG", "batch"),
                 ("dW1", "serial"), ("dW2", "serial"), ("dX1", "serial"),
                 ("dSPL", "serial"), ("dFWL", "serial"),
                 ("dST0", "serial"), ("dST1", "serial"),
                 ("dXB", "batch"),
                 ("dLOa0", "batch"), ("dLOa1", "batch"), ("dLOa2", "batch"),
                 ("dLOa3", "batch"), ("dLOb0", "batch"),
                 ("dLOb1", "batch"), ("dLOb2", "batch"), ("dLOb3", "batch")):
        S.dma_sem(s, m)

    def scrk(lo, hi):
        return [f"SCR:{b}" for b in range(lo // 512, (hi + 511) // 512)]

    def mkslice(base, lo, hi):
        return SCR[:, base + lo: base + hi], scrk(base + lo, base + hi)

    def pairs_slices(p):
        b = p * 4096
        d = {}
        for nm, lo, hi in (("SQQ", 0, 1024), ("M2Q", 1024, 1536), ("RQ", 1536, 2048),
                           ("SQK", 2048, 3072), ("M2K", 3072, 3584), ("RK", 3584, 4096)):
            d[nm] = mkslice(b, lo, hi)
        return d

    def bchk(lo, hi):
        return [f"BC:{b}" for b in range(lo // 512, (hi + 511) // 512)]

    def bch_slices(p):
        b = p * 5120
        d = {}
        for nm, lo, hi in (("FQN", 0, 1024), ("FKN", 1024, 2048),
                           ("TVB", 2048, 3072), ("PRD", 3072, 4096),
                           ("CSC", 4096, 5120)):
            d[nm] = (BCH[:, b + lo: b + hi], bchk(b + lo, b + hi))
        return d

    def post_slices(p):
        p = p % 4
        d = {"UU": mkslice(p * 2048, 0, 1024),
             "LNU": mkslice(p * 2048, 1024, 2048)}
        d["TRASH"] = (BCH[:, p * 1024:(p + 1) * 1024],
                      [f"BC:{b}" for b in range(p * 2, p * 2 + 2)])
        return d

    def ffn_h_slices(p):
        b = p * 2048
        return {"S_U": mkslice(b, 0, 1024), "C_U": mkslice(b, 1024, 2048)}

    def ffn_y_slices(p):
        return {"U2": mkslice(4096 + p * 1024, 0, 1024),
                "LNY": mkslice(6144 + p * 1024, 0, 1024),
                "TR4": (BCH[:, p * 1024:(p + 1) * 1024], bchk(p * 1024, (p + 1) * 1024))}

    K_CSC = scrk(5120, 6144)
    K_PRD = scrk(4096, 5120)
    TRASH, K_TRASH = SCR[:, 2048:3072], scrk(2048, 3072)

    THc, WRc, WIc, B1c = QC[:, 0:32], QC[:, 32:64], QC[:, 64:96], QC[:, 96:128]
    Ac, Bc, SB1, SC1 = QC[:, 128:160], QC[:, 160:192], QC[:, 192:224], QC[:, 224:256]

    ST_S1, ST_SS1, ST_R1, ST_B1 = 0, 8, 16, 24
    def stc(p, idx):
        return STAT[:, 32 + p * 8 + idx: 32 + p * 8 + idx + 1]

    FWKEYS = [f"FW:{w},{c}" for w in range(3) for c in range(ND)]
    VKEYS = [f"VAL:{g},{q}" for g in range(8) for q in range(NQ)]
    XNTKEYS = [f"XNT:{i}" for i in range(NT)]
    TABKEYS = [f"TAB:{r}" for r in range(ND)]

    def dma(eng, sem, out_ap, in_ap, reads, writes, accum=False):
        if accum:
            S.op(eng, lambda e, o=out_ap, i=in_ap: e.dma_start(
                o, i, accum_op=ALU.add), reads=reads, writes=writes, dma=sem)
        else:
            S.op(eng, lambda e, o=out_ap, i=in_ap: e.dma_start(o, i),
                 reads=reads, writes=writes, dma=sem)

    dma("gpsimd", "dPREa", TAB[:, :],
        AP(cs_d, 0, [[1025, P], [1025 * P, ND], [1, 1025]]), [], TABKEYS)
    dma("gpsimd", "dPREa", XNT[:, 0:8192],
        AP(wk_d, 0, [[D, P], [D * P, ND], [1, D]]), [], XNTKEYS)
    dma("gpsimd", "dPRE", FW[:, 0:8192],
        AP(wv_d, 0, [[D, P], [D * P, ND], [1, D]]),
        [], [f"FW:0,{c}" for c in range(ND)])
    dma("gpsimd", "dPRE", VAL[:, 0:8192],
        AP(wq_d, 0, [[D, P], [D * P, ND], [1, D]]), [], VKEYS)
    dma("gpsimd", "dPRE", IDN[:, :], AP(id_d, 0, [[P, P], [1, P]]), [], ["IDN"])
    dma("gpsimd", "dPRE", B2R[0:1, :], AP(b2_d, 0, [[D, 1], [1, D]]), [], ["B2R"])
    dma("gpsimd", "dPRE", GNYQ[0:1, :],
        AP(g_d, 1024 * D, [[D, 1], [1, D]]), [], ["GNYQ"])
    dma("sync", "dPS", IDNF[:, :], AP(id_d, 0, [[P, P], [1, P]]),
        [], ["IDNF"])
    for o, src in ((0, cslo_d), (1, cslo2_d)):
        for p in range(2):
            kstg = [f"XNB:{i}" for i in range(5)]
            dma("sync", f"dLOa{o * 2 + p}", XNB[:, 0:4100],
                AP(src, p * 4 * 1025 * P,
                   [[1025, P], [1025 * P, 4], [1, 1025]]),
                kstg, kstg)
            S.op("vector", lambda e, p=p: e.tensor_add(
                TAB[:, p * 4100:(p + 1) * 4100],
                TAB[:, p * 4100:(p + 1) * 4100], XNB[:, 0:4100]),
                reads=TABKEYS[p * 4:(p + 1) * 4] + kstg,
                writes=TABKEYS[p * 4:(p + 1) * 4])
    dma("sync", "dPS", XT[:, :],
        AP(x_d, 0, [[D, P], [D * P, NT], [1, D]]),
        [], [f"XT:{i}" for i in range(NT)])
    _QSTG = {"th": (th_d, THc, 4096, "QC:th"), "wr": (wr_d, WRc, 4224, "QC:wr"),
             "wi": (wi_d, WIc, 4352, "QC:wi"), "b1": (b1_d, B1c, 4480, "QC:b1")}
    for _nm, (_src, _dst, _off, _key) in _QSTG.items():
        dma("sync", "dPS", SCR[0:32, _off:_off + 128],
            AP(_src, 0, [[P, 32], [1, P]]), [], [f"qs:{_nm}"])
    for _qi, (_nm, (_src, _dst, _off, _key)) in enumerate(_QSTG.items()):
        S.op("tensor", lambda e, o=_off, qi=_qi: e.matmul(
            PB[:, 1024 + qi * 32: 1024 + (qi + 1) * 32],
            SCR[0:32, o:o + 128], IDNF[0:32, 0:32], start=True, stop=True),
            reads=[f"qs:{_nm}", "IDNF"], writes=[f"qp:{_nm}", "PB23"])
        S.op("scalar", lambda e, d=_dst, qi=_qi: e.copy(
            d, PB[:, 1024 + qi * 32: 1024 + (qi + 1) * 32]),
            reads=[f"qp:{_nm}", "PB23"], writes=[_key + ":0"])


    S.op("vector", lambda e: e.memset(ONE1[0:1, :], 1.0), writes=["ONE1"])
    S.op("vector", lambda e: e.memset(EPSC[:, :], 1e-30), writes=["EPSC"])
    S.op("vector", lambda e: e.memset(EPSLN[:, :], float(EPS_LN)), writes=["EPSLN"])
    S.op("vector", lambda e: e.memset(EPS8[:, :], float(EPS_FFT)), writes=["EPS8"])

    R0 = SCR[:, 5120:5152]
    T0 = SCR[:, 5152:5184]
    DD = SCR[:, 5184:5216]
    CC = SCR[:, 5216:5248]
    STH = SCR[:, 5248:5280]
    CTH = SCR[:, 5280:5312]
    PQ = SCR[:, 5312:5344]

    def rred(shift):
        S.op("vector", lambda e: e.tensor_scalar(
            R0, THc, float(1 / (2 * np.pi)), float(shift), ALU.mult, ALU.add),
            reads=["QC:th:0"], writes=K_CSC)
        S.op("vector", lambda e: e.tensor_copy(QIb[:, :], R0),
             reads=K_CSC, writes=["QIb"])
        S.op("vector", lambda e: e.tensor_copy(T0, QIb[:, :]),
             reads=["QIb"], writes=K_CSC)
        S.op("vector", lambda e: e.tensor_sub(DD, R0, T0),
             reads=K_CSC, writes=K_CSC)
        S.op("vector", lambda e: e.tensor_single_scalar(CC, DD, 0.5, ALU.is_ge),
             reads=K_CSC, writes=K_CSC)
        S.op("vector", lambda e: e.tensor_add(T0, T0, CC),
             reads=K_CSC, writes=K_CSC)
        S.op("vector", lambda e: e.tensor_single_scalar(CC, DD, -0.5, ALU.is_le),
             reads=K_CSC, writes=K_CSC)
        S.op("vector", lambda e: e.tensor_sub(T0, T0, CC),
             reads=K_CSC, writes=K_CSC)
        S.op("vector", lambda e: e.scalar_tensor_tensor(
            R0, T0, float(-2 * np.pi), THc, ALU.mult, ALU.add),
            reads=K_CSC + ["QC:th:0"], writes=K_CSC)
        if shift:
            S.op("vector", lambda e: e.tensor_scalar_add(
                R0, R0, float(shift * 2 * np.pi)), reads=K_CSC, writes=K_CSC)

    rred(0.0)
    S.op("scalar", lambda e: e.activation(STH, R0, AF.Sin),
         reads=K_CSC, writes=["q:sth"])
    rred(0.25)
    S.op("scalar", lambda e: e.activation(CTH, R0, AF.Sin),
         reads=K_CSC, writes=["q:cth"])
    S.op("vector", lambda e: e.tensor_mul(PQ, CTH, WRc),
         reads=["q:cth", "QC:wr:0"], writes=["q:pq"])
    S.op("vector", lambda e: e.tensor_mul(T0, STH, WIc),
         reads=["q:sth", "QC:wi:0", "q:cth"], writes=K_CSC)
    S.op("vector", lambda e: e.tensor_add(PQ, PQ, T0),
         reads=["q:pq"] + K_CSC, writes=["q:pq"])
    S.op("vector", lambda e: e.tensor_mul(T0, PQ, WIc),
         reads=["q:pq", "QC:wi:0"], writes=K_CSC)
    S.op("vector", lambda e: e.tensor_scalar_mul(T0, T0, 0.1),
         reads=K_CSC, writes=K_CSC)
    S.op("vector", lambda e: e.tensor_mul(Ac, T0, STH),
         reads=K_CSC + ["q:sth"], writes=["QC:A"])
    S.op("vector", lambda e: e.tensor_mul(Bc, T0, CTH),
         reads=K_CSC + ["q:cth"], writes=["QC:B"])
    S.op("vector", lambda e: e.tensor_scalar_mul(SB1, B1c, 0.1),
         reads=["QC:b1:0"], writes=["QC:SB1"])
    S.op("vector", lambda e: e.tensor_scalar(
        SC1, B1c, 0.1, float(np.pi / 2), ALU.mult, ALU.add),
        reads=["QC:b1:0"], writes=["QC:SC1"])

    def fin(par, s_ap, ss_ap, r_ap, b_ap, s_key, ss_key, r_key, b_key):
        m_ap, q_ap, v_ap = stc(par, 2), stc(par, 3), stc(par, 4)
        km, kq, kv = f"st:m{par}", f"st:q{par}", f"st:v{par}"
        S.op("vector", lambda e: e.tensor_scalar_mul(m_ap, s_ap, float(1.0 / D)),
             reads=[s_key], writes=[km])
        S.op("vector", lambda e: e.tensor_scalar(q_ap, m_ap, m_ap, None, ALU.mult),
             reads=[km], writes=[kq])
        S.op("vector", lambda e: e.scalar_tensor_tensor(
            v_ap, ss_ap, float(1.0 / D), q_ap, ALU.mult, ALU.subtract),
            reads=[ss_key, kq], writes=[kv])
        S.op("scalar", lambda e: e.activation(v_ap, v_ap, AF.Ln,
                                              bias=EPSLN[:, 0:1]),
             reads=[kv, "EPSLN"], writes=[kv])
        S.op("scalar", lambda e: e.activation(r_ap, v_ap, AF.Exp, scale=-0.5),
             reads=[kv], writes=[r_key])
        S.op("vector", lambda e: e.tensor_scalar(
            b_ap, m_ap, r_ap, -1.0, ALU.mult, ALU.mult),
            reads=[km, r_key], writes=[b_key])

    def ln_tile(i, out_ap, out_key, s_pre=None):
        par = i % 4
        TRp, K_TRp = post_slices(par)["TRASH"]
        sa = STAT[:, ST_S1 + i: ST_S1 + i + 1]
        ssa = STAT[:, ST_SS1 + i: ST_SS1 + i + 1]
        ra = STAT[:, ST_R1 + i: ST_R1 + i + 1]
        ba = STAT[:, ST_B1 + i: ST_B1 + i + 1]
        ks, kss = f"st:S1:{i}", f"st:SS1:{i}"
        kr, kb = f"st:R1:{i}", f"st:B1:{i}"
        if s_pre is None:
            S.op("vector", lambda e: e.reduce_sum(sa, XT[:, i * D:(i + 1) * D], AX.X),
                 reads=[f"XT:{i}"], writes=[ks])
        else:
            ks = s_pre
        S.op("vector", lambda e: e.scalar_tensor_tensor(
            TRp, XT[:, i * D:(i + 1) * D], 1.0, XT[:, i * D:(i + 1) * D],
            ALU.mult, ALU.mult, accum_out=ssa),
            reads=[f"XT:{i}"], writes=[kss] + K_TRp)
        fin(par, sa, ssa, ra, ba, ks, kss, kr, kb)
        S.op("scalar", lambda e: e.activation(
            out_ap, XT[:, i * D:(i + 1) * D], AF.Identity, bias=ba, scale=ra),
            reads=[f"XT:{i}", kr, kb], writes=[out_key])

    def transpose_tile(i, src_key, dst_pt, dst_off, pk, slot=None,
                       evac_eng="scalar"):
        sl = (i % 2) if slot is None else slot
        def mmtr(e, i=i, dst_pt=dst_pt, dst_off=dst_off, sl=sl):
            last = None
            for c in range(ND):
                last = e.matmul(dst_pt[:, dst_off + c * P: dst_off + (c + 1) * P],
                                XNB[:, sl * D + c * P: sl * D + (c + 1) * P],
                                IDN[:, :], start=True, stop=True)
            return last
        S.op("tensor", mmtr, reads=[src_key, "IDN"], writes=[pk])
        if evac_eng == "scalar":
            S.op("scalar", lambda e, i=i: e.copy(
                bass.AP(XNT, i * P, [[ND * H, P], [H, ND], [1, P]]),
                dst_pt[:, dst_off: dst_off + 1024]),
                reads=[pk], writes=[f"XNT:{i}"])
        else:
            S.op("vector", lambda e, i=i: e.tensor_copy(
                bass.AP(XNT, i * P, [[ND * H, P], [H, ND], [1, P]]),
                dst_pt[:, dst_off: dst_off + 1024]),
                reads=[pk], writes=[f"XNT:{i}"])

    def fold_one(w, stage_keys, stage_t):
        for c in range(ND):
            par = (w * ND + c) % 2
            PT = PA if par == 0 else PB
            pk = ["PA01", "PA23"] if par == 0 else ["PB01", "PB23"]

            def mmf(e, PT=PT, st_t=stage_t, c=c, region=0):
                last = None
                for r in range(ND):
                    lhsT = st_t[:, r * D + c * P: r * D + (c + 1) * P]
                    if region == 0:
                        last = e.matmul(PT[:, 0:512], lhsT,
                                        TAB[:, r * 1025: r * 1025 + 512],
                                        start=(r == 0), stop=(r == ND - 1))
                    elif region == 1:
                        last = e.matmul(PT[:, 512:1024], lhsT,
                                        TAB[:, r * 1025 + 512: r * 1025 + 1024],
                                        start=(r == 0), stop=(r == ND - 1))
                    else:
                        last = e.matmul(PT[:, 1024:1025], lhsT,
                                        TAB[:, r * 1025 + 1024: r * 1025 + 1025],
                                        start=(r == 0), stop=(r == ND - 1))
                return last
            S.op("tensor", lambda e, f=mmf: f(e, region=0),
                 reads=stage_keys + TABKEYS, writes=[pk[0]])
            S.op("tensor", lambda e, f=mmf: f(e, region=1),
                 reads=stage_keys + TABKEYS, writes=[pk[0]])
            S.op("tensor", lambda e, f=mmf: f(e, region=2),
                 reads=stage_keys + TABKEYS, writes=[pk[1]])
            S.op("scalar", lambda e, PT=PT, w=w, c=c: e.copy(
                FW[:, (w * ND + c) * D:(w * ND + c + 1) * D], PT[:, 0:1024]),
                reads=[pk[0]], writes=[f"FW:{w},{c}"])
            S.op("scalar", lambda e, PT=PT, w=w, c=c: e.copy(
                FWN[:, c * 3 + w: c * 3 + w + 1], PT[:, 1024:1025]),
                reads=[pk[1]], writes=[f"FWN:{w},{c}"])


    ST_M1 = 40

    def ln1_chains():
        for i in range(NT):
            sa = STAT[:, ST_S1 + i: ST_S1 + i + 1]
            ssa = STAT[:, ST_SS1 + i: ST_SS1 + i + 1]
            TRp, K_TRp = post_slices(i)["TRASH"]
            S.op("vector", lambda e, i=i, sa=sa: e.reduce_sum(
                sa, XT[:, i * D:(i + 1) * D], AX.X),
                reads=[f"XT:{i}"], writes=[f"st:S1:{i}"])
            S.op("vector", lambda e, i=i, ssa=ssa, TRp=TRp: e.scalar_tensor_tensor(
                TRp, XT[:, i * D:(i + 1) * D], 1.0, XT[:, i * D:(i + 1) * D],
                ALU.mult, ALU.mult, accum_out=ssa),
                reads=[f"XT:{i}"], writes=[f"st:SS1:{i}"] + K_TRp)
        S8 = STAT[:, ST_S1:ST_S1 + 8]
        SS8 = STAT[:, ST_SS1:ST_SS1 + 8]
        M8 = STAT[:, ST_M1:ST_M1 + 8]
        R8 = STAT[:, ST_R1:ST_R1 + 8]
        B8 = STAT[:, ST_B1:ST_B1 + 8]
        rdall = [f"st:S1:{i}" for i in range(NT)] + [f"st:SS1:{i}" for i in range(NT)]
        S.op("vector", lambda e: e.tensor_scalar_mul(M8, S8, float(1.0 / D)),
             reads=rdall, writes=["st:M8"])
        S.op("vector", lambda e: e.tensor_mul(R8, M8, M8),
             reads=["st:M8"], writes=["st:R8t"])
        S.op("vector", lambda e: e.scalar_tensor_tensor(
            R8, SS8, float(1.0 / D), R8, ALU.mult, ALU.subtract),
            reads=rdall + ["st:R8t"], writes=["st:R8t"])
        S.op("scalar", lambda e: e.activation(R8, R8, AF.Ln,
                                              bias=EPSLN[:, 0:1]),
             reads=["st:R8t", "EPSLN"], writes=["st:R8t"])
        S.op("scalar", lambda e: e.activation(R8, R8, AF.Exp, scale=-0.5),
             reads=["st:R8t"], writes=["st:R8"])
        S.op("vector", lambda e: e.tensor_mul(B8, M8, R8),
             reads=["st:M8", "st:R8"], writes=["st:B8t"])
        S.op("vector", lambda e: e.tensor_scalar_mul(B8, B8, -1.0),
             reads=["st:B8t"], writes=["st:B8"])
        for i in range(NT):
            TRp, K_TRp = post_slices(i)["TRASH"]
            UUp, K_UUp = post_slices(i)["UU"]
            S.op("vector", lambda e, i=i, UUp=UUp: e.tensor_scalar_mul(
                UUp, XT[:, i * D:(i + 1) * D], STAT[:, ST_R1 + i: ST_R1 + i + 1]),
                reads=[f"XT:{i}", "st:R8"], writes=K_UUp)
            S.op("vector", lambda e, i=i, UUp=UUp: e.tensor_scalar_add(
                XNB[:, i * D:(i + 1) * D], UUp,
                STAT[:, ST_B1 + i: ST_B1 + i + 1]),
                reads=K_UUp + ["st:B8"], writes=[f"XNB:{i}"])

    def ln1_transposes():
        for i in range(NT):
            par = i % 2
            transpose_tile(i, f"XNB:{i}", PA if par == 0 else PB, 0,
                           "PA01" if par == 0 else "PB01", slot=i)

    def ln1_phase():
        for t in range(NT + 1):
            if 0 <= t - 1 < NT:
                i = t - 1
                par = i % 2
                transpose_tile(i, f"XNB:{par}", PA if par == 0 else PB, 0,
                               "PA01" if par == 0 else "PB01")
            if t < NT:
                ln_tile(t, XNB[:, (t % 2) * D:(t % 2) * D + D], f"XNB:{t % 2}")

    def do_half(h, do_ln1=True):
        if do_ln1:
            ln1_chains()
            ln1_transposes()

        def mmnyq(e):
            last = None
            for c in range(ND):
                for sgm in range(2):
                    last = e.matmul(
                        PB[0:3, 1024 + sgm * 512: 1024 + (sgm + 1) * 512],
                        FWN[:, c * 3:(c + 1) * 3],
                        XNT[:, c * H + sgm * 512: c * H + (sgm + 1) * 512],
                        start=(c == 0), stop=(c == ND - 1))
            return last
        S.op("tensor", mmnyq,
             reads=[f"FWN:{w},{c}" for w in range(3) for c in range(ND)] + XNTKEYS,
             writes=["PB23"])
        NYb = 4096
        NYT = SCR[0:3, NYb:NYb + 1024]
        NYT32 = SCR[0:32, NYb:NYb + 1024]
        NYA = SCR[0:2, NYb + 1024:NYb + 2048]
        SHA = SCR[0:32, NYb + 2048:NYb + 3072]
        SHBWS = SCR[0:32, NYb + 3072:NYb + 4096]
        K_NYT = scrk(NYb, NYb + 1024)
        K_NYA = scrk(NYb + 1024, NYb + 2048)
        K_SHA = scrk(NYb + 2048, NYb + 3072)
        K_SHB = scrk(NYb + 3072, NYb + 4096)
        S.op("vector", lambda e: e.memset(NYT32, 0.0), reads=[], writes=K_NYT)
        S.op("scalar", lambda e: e.copy(NYT, PB[0:3, 1024:2048]),
             reads=["PB23"] + K_NYT, writes=K_NYT)
        S.op("scalar", lambda e: e.activation(NYA, SCR[0:2, NYb:NYb + 1024], AF.Abs),
             reads=K_NYT, writes=K_NYA)
        S.op("scalar", lambda e: e.activation(NYA, NYA, AF.Ln,
                                              bias=EPS8[0:2, 0:1]),
             reads=K_NYA + ["EPS8"], writes=K_NYA)
        S.op("scalar", lambda e: e.activation(NYA, NYA, AF.Exp, scale=-1.0),
             reads=K_NYA, writes=K_NYA)
        S.op("vector", lambda e: e.tensor_mul(
            SCR[0:2, NYb:NYb + 1024], SCR[0:2, NYb:NYb + 1024], NYA),
            reads=K_NYT + K_NYA, writes=K_NYT)
        mka = list(range(32)); mka[0] = 1
        mkb = list(range(32)); mkb[0] = 2
        S.op("vector", lambda e: e.stream_shuffle(SHA, NYT32, mka),
             reads=K_NYT, writes=K_SHA)
        S.op("vector", lambda e: e.stream_shuffle(SHBWS, NYT32, mkb),
             reads=K_NYT, writes=K_SHB)
        S.op("vector", lambda e: e.tensor_mul(
            SCR[0:1, NYb + 1024:NYb + 2048], SHA[0:1, :], SHBWS[0:1, :]),
            reads=K_SHA + K_SHB, writes=K_NYA)
        inin = 0.0 if h == 0 else CAR[0:1, 9:10]
        rdn = K_NYA + ([] if h == 0 else ["car:n"])
        S.op("vector", lambda e, inin=inin: e.tensor_tensor_scan(
            SCR[0:1, NYb + 3072:NYb + 4096], SCR[0:1, NYb + 1024:NYb + 2048],
            SCR[0:1, NYb + 1024:NYb + 2048], inin, ALU.add, ALU.bypass),
            reads=rdn, writes=K_SHB)
        S.op("vector", lambda e: e.tensor_copy(
            CAR[0:1, 9:10], SCR[0:1, NYb + 4095:NYb + 4096]),
            reads=K_SHB, writes=["car:n"])
        S.op("vector", lambda e: e.tensor_mul(
            NVAL[0:1, :], SCR[0:1, NYb + 3072:NYb + 4096],
            SCR[0:1, NYb:NYb + 1024]),
            reads=K_SHB + K_NYT, writes=["NVAL"])

        PSLOT = [(PA, 0, "PA01"), (PA, 1024, "PA23"),
                 (PB, 0, "PB01"), (PB, 1024, "PB23")]

        def pr_slot(it, plane):
            return PSLOT[(3 * it + plane) % 4]

        def pr_mm(it):
            q, j = it // NJ, it % NJ
            xnt_r = [f"XNT:{4 * q + t}" for t in range(4)]

            def mmpl(e, w=0, dst=None, dstoff=0, j=j, q=q):
                last = None
                for p in range(2):
                    for c in range(ND):
                        lhsT = FW[:, (w * ND + c) * D + p * F + j * P:
                                  (w * ND + c) * D + p * F + j * P + P]
                        rhs = XNT[:, c * H + q * QW: c * H + (q + 1) * QW]
                        last = e.matmul(
                            dst[:, dstoff + p * F: dstoff + (p + 1) * F],
                            lhsT, rhs, start=(c == 0), stop=(c == ND - 1))
                return last
            for w in range(3):
                dst, off, key = pr_slot(it, w)
                S.op("tensor", lambda e, f=mmpl, w=w, dst=dst, off=off: f(
                    e, w=w, dst=dst, dstoff=off),
                    reads=[f"FW:{w},{c}" for c in range(ND)] + xnt_r,
                    writes=[key])

        def pr_norm(it):
            par = it % 2
            PS = pairs_slices(par)
            BS = bch_slices(par)
            FQN, K_FQN = BS["FQN"]; FKN, K_FKN = BS["FKN"]; TVB, K_TVB = BS["TVB"]
            qd, qo, qk = pr_slot(it, 0)
            kd, ko, kk = pr_slot(it, 1)
            vd, vo, vk = pr_slot(it, 2)
            S.op("scalar", lambda e, TVB=TVB, vd=vd, vo=vo: e.copy(
                TVB, vd[:, vo:vo + 1024]), reads=[vk], writes=K_TVB)
            QSpec = (qd[:, qo:qo + 1024], qk, "SQQ", "M2Q", "RQ", FQN, K_FQN)
            KSpec = (kd[:, ko:ko + 1024], kk, "SQK", "M2K", "RK", FKN, K_FKN)
            sl = {}
            for tag, (psl, pk, SQnm, M2nm, Rnm, Onm, Okey) in (("q", QSpec), ("k", KSpec)):
                sl[tag] = (psl, pk, PS[SQnm], PS[M2nm], PS[Rnm], Onm, Okey)
            for tag in ("q", "k"):
                psl, pk, (SQ, K_SQ), _, _, _, _ = sl[tag]
                S.op("scalar", lambda e, SQ=SQ, psl=psl: e.activation(
                    SQ, psl, AF.Square), reads=[pk], writes=K_SQ)
            for tag in ("q", "k"):
                _, _, (SQ, K_SQ), (M2, K_M2), _, _, _ = sl[tag]
                S.op("vector", lambda e, M2=M2, SQ=SQ: e.tensor_add(
                    M2, SQ[:, 0:512], SQ[:, 512:1024]), reads=K_SQ, writes=K_M2)
            for tag in ("q", "k"):
                _, _, _, (M2, K_M2), _, _, _ = sl[tag]
                S.op("scalar", lambda e, M2=M2: e.activation(
                    M2, M2, AF.Ln), reads=K_M2, writes=K_M2)
            for tag in ("q", "k"):
                _, _, _, (M2, K_M2), _, _, _ = sl[tag]
                S.op("scalar", lambda e, M2=M2: e.activation(
                    M2, M2, AF.Exp, scale=0.5), reads=K_M2, writes=K_M2)
            for tag in ("q", "k"):
                _, _, _, (M2, K_M2), _, _, _ = sl[tag]
                S.op("scalar", lambda e, M2=M2: e.activation(
                    M2, M2, AF.Ln, bias=EPS8[:, 0:1]),
                    reads=K_M2 + ["EPS8"], writes=K_M2)
            for tag in ("q", "k"):
                _, _, _, (M2, K_M2), (RR, K_RR), _, _ = sl[tag]
                S.op("scalar", lambda e, RR=RR, M2=M2: e.activation(
                    RR, M2, AF.Exp, scale=-1.0), reads=K_M2, writes=K_RR)
            for tag in ("q", "k"):
                psl, pk, _, _, (RR, K_RR), Onm, Okey = sl[tag]
                S.op("vector", lambda e, Onm=Onm, psl=psl, RR=RR: e.tensor_tensor(
                    Onm, psl, bass.AP(RR.tensor, RR.offset, [[8192, P], [0, 2], [1, 512]]),
                    ALU.mult), reads=[pk] + K_RR, writes=Okey)

        def pr_tail(it):
            q, j = it // NJ, it % NJ
            gq = h * NQ + q
            par = it % 2
            BS = bch_slices(par)
            FQN, K_FQN = BS["FQN"]; FKN, K_FKN = BS["FKN"]; TVB, K_TVB = BS["TVB"]
            PRD, K_PRD = BS["PRD"]; CSC, K_CSC = BS["CSC"]
            K_C0, K_C1 = [K_CSC[0]], [K_CSC[1]]
            K_P0, K_P1 = [K_PRD[0]], [K_PRD[1]]
            S.op("vector", lambda e, CSC=CSC, FKN=FKN, TVB=TVB: e.tensor_mul(
                CSC[:, 0:512], FKN[:, 0:512], TVB[:, 0:512]),
                reads=K_FKN + K_TVB, writes=K_C0)
            S.op("vector", lambda e, CSC=CSC, FKN=FKN, TVB=TVB: e.tensor_mul(
                CSC[:, 512:1024], FKN[:, 512:1024], TVB[:, 512:1024]),
                reads=K_FKN + K_TVB, writes=K_C1)
            S.op("vector", lambda e, PRD=PRD, CSC=CSC: e.tensor_sub(
                PRD[:, 0:512], CSC[:, 0:512], CSC[:, 512:1024]),
                reads=K_C0 + K_C1, writes=K_P0)
            S.op("vector", lambda e, CSC=CSC, FKN=FKN, TVB=TVB: e.tensor_mul(
                CSC[:, 0:512], FKN[:, 0:512], TVB[:, 512:1024]),
                reads=K_FKN + K_TVB, writes=K_C0)
            S.op("vector", lambda e, CSC=CSC, FKN=FKN, TVB=TVB: e.tensor_mul(
                CSC[:, 512:1024], FKN[:, 512:1024], TVB[:, 0:512]),
                reads=K_FKN + K_TVB, writes=K_C1)
            S.op("vector", lambda e, PRD=PRD, CSC=CSC: e.tensor_add(
                PRD[:, 512:1024], CSC[:, 0:512], CSC[:, 512:1024]),
                reads=K_C0 + K_C1, writes=K_P1)
            for pl in range(2):
                car_ap = CAR[:, 2 * j + pl: 2 * j + pl + 1]
                ini = 0.0 if gq == 0 else car_ap
                rd = [K_PRD[pl]] + ([] if gq == 0 else [f"car:{j},{pl}"])
                S.op("vector", lambda e, pl=pl, ini=ini, CSC=CSC, PRD=PRD:
                     e.tensor_tensor_scan(
                         CSC[:, pl * F:(pl + 1) * F], PRD[:, pl * F:(pl + 1) * F],
                         PRD[:, pl * F:(pl + 1) * F], ini, ALU.add, ALU.bypass),
                     reads=rd, writes=[K_CSC[pl]])
                S.op("vector", lambda e, pl=pl, car_ap=car_ap, CSC=CSC: e.tensor_copy(
                    car_ap, CSC[:, (pl + 1) * F - 1:(pl + 1) * F]),
                    reads=[K_CSC[pl]], writes=[f"car:{j},{pl}"])
            vr = VAL[:, j * H + q * QW: j * H + (q + 1) * QW]
            vi = VAL[:, (4 + j) * H + q * QW: (4 + j) * H + (q + 1) * QW]
            S.op("vector", lambda e, PRD=PRD, CSC=CSC, FQN=FQN: e.tensor_mul(
                PRD[:, 0:512], CSC[:, 0:512], FQN[:, 0:512]),
                reads=K_C0 + K_FQN, writes=K_P0)
            S.op("vector", lambda e, PRD=PRD, CSC=CSC, FQN=FQN: e.tensor_mul(
                PRD[:, 512:1024], CSC[:, 512:1024], FQN[:, 512:1024]),
                reads=K_C1 + K_FQN, writes=K_P1)
            S.op("vector", lambda e, vr=vr, PRD=PRD: e.tensor_add(
                vr, PRD[:, 0:512], PRD[:, 512:1024]),
                reads=K_P0 + K_P1, writes=[f"VAL:{j},{q}"])
            S.op("vector", lambda e, PRD=PRD, CSC=CSC, FQN=FQN: e.tensor_mul(
                PRD[:, 0:512], CSC[:, 512:1024], FQN[:, 0:512]),
                reads=K_C1 + K_FQN, writes=K_P0)
            S.op("vector", lambda e, PRD=PRD, CSC=CSC, FQN=FQN: e.tensor_mul(
                PRD[:, 512:1024], CSC[:, 0:512], FQN[:, 512:1024]),
                reads=K_C0 + K_FQN, writes=K_P1)
            S.op("vector", lambda e, vi=vi, PRD=PRD: e.tensor_sub(
                vi, PRD[:, 0:512], PRD[:, 512:1024]),
                reads=K_P0 + K_P1, writes=[f"VAL:{4 + j},{q}"])

        NIT = NQ * NJ
        for t in range(NIT + 2):
            if 0 <= t - 2 < NIT:
                pr_tail(t - 2)
            if 0 <= t - 1 < NIT:
                pr_norm(t - 1)
            if t < NIT:
                pr_mm(t)

        def po_mm(i):
            par = i % 2
            vh = PA[:, par * 1024: par * 1024 + 1024]
            vhk = "PA01" if par == 0 else "PA23"
            q = i // 4

            def mmir(e, i=i, vh=vh):
                last = None
                for g in range(8):
                    lhsT = VAL[:, g * H + i * P: g * H + (i + 1) * P]
                    for sgm in range(2):
                        last = e.matmul(
                            vh[:, sgm * 512:(sgm + 1) * 512], lhsT,
                            TAB[:, g * D + sgm * 512: g * D + (sgm + 1) * 512],
                            start=(g == 0), stop=False)
                for sgm in range(2):
                    last = e.matmul(vh[:, sgm * 512:(sgm + 1) * 512],
                                    NVAL[0:1, i * P:(i + 1) * P],
                                    GNYQ[0:1, sgm * 512:(sgm + 1) * 512],
                                    start=False, stop=True)
                return last
            S.op("tensor", mmir,
                 reads=[f"VAL:{g},{q}" for g in range(8)] + TABKEYS
                 + ["NVAL", "GNYQ"], writes=[vhk])

        def po_chain(i):
            vh = PA[:, (i % 2) * 1024: (i % 2) * 1024 + 1024]
            vhk = "PA01" if i % 2 == 0 else "PA23"
            par = i % 4
            PO = post_slices(par)
            UU, K_UU = PO["UU"]
            TRASH, K_TRASH = PO["TRASH"]; LNU, K_LNU = PO["LNU"]
            S.op("vector", lambda e, vh=vh, par=par, i=i, UU=UU:
                 e.scalar_tensor_tensor(
                     UU, XNB[:, i * D:(i + 1) * D], 1.0, vh, ALU.mult, ALU.add,
                     accum_out=stc(par, 0)),
                 reads=[f"XNB:{i}", vhk], writes=K_UU + [f"st:S{par}"])
            S.op("scalar", lambda e, par=par, UU=UU, TRASH=TRASH: e.activation(
                TRASH, UU, AF.Square, accum_out=stc(par, 1)),
                reads=K_UU, writes=[f"st:SS{par}"] + K_TRASH)
            fin(par, stc(par, 0), stc(par, 1), stc(par, 5), stc(par, 6),
                f"st:S{par}", f"st:SS{par}", f"st:R{par}", f"st:B{par}")
            S.op("scalar", lambda e, par=par, UU=UU, LNU=LNU: e.activation(
                LNU, UU, AF.Identity, bias=stc(par, 6), scale=stc(par, 5)),
                reads=K_UU + [f"st:R{par}", f"st:B{par}"], writes=K_LNU)
            S.op("vector", lambda e, i=i, LNU=LNU: e.scalar_tensor_tensor(
                XT[:, i * D:(i + 1) * D], LNU, 1.0, XT[:, i * D:(i + 1) * D],
                ALU.mult, ALU.add,
                accum_out=STAT[:, ST_S1 + i: ST_S1 + i + 1]),
                reads=[f"XT:{i}"] + K_LNU,
                writes=[f"XT:{i}", f"st:S1g:{i}"])

        def po_ln3(i):
            par = i % 2
            ln_tile(i, XNB[:, i * D:(i + 1) * D], f"XNB:{i}",
                    s_pre=f"st:S1g:{i}")
            transpose_tile(i, f"XNB:{i}", PB, par * 1024,
                           "PB01" if par == 0 else "PB23", slot=i)

        for t in range(NT + 2):
            if 0 <= t - 2 < NT:
                po_ln3(t - 2)
            if 0 <= t - 1 < NT:
                po_chain(t - 1)
            if t < NT:
                po_mm(t)

        w1zone = [f"FW:0,{c}" for c in range(ND)]
        w2zone = [f"FW:1,{c}" for c in range(ND)]
        for g in range(NG):
            dma("gpsimd", "dW1", FW[:, 0:8192],
                AP(w1_d, g * 1024, [[DF, P], [DF * P, ND], [1, 1024]]),
                [], w1zone)
            dma("gpsimd", "dW2", FW[:, 8192:16384],
                AP(w2_d, g * 1024 * D, [[D, P], [D * P, ND], [1, D]]),
                [], w2zone)
            if h == 0 and g == 1:
                dma("gpsimd", "dXB", XNB[:, :],
                    AP(x_d, H * D, [[D, P], [D * P, NT], [1, D]]),
                    [f"XNB:{i}" for i in range(NT)],
                    [f"XNB:{i}" for i in range(NT)])
            for fc in range(ND):
                Fg = g * ND + fc
                par = fc % 2
                hp = PA[:, par * 1024: par * 1024 + 1024]
                hk = "PA01" if par == 0 else "PA23"
                HS = ffn_h_slices(par)
                S_U, K_SU = HS["S_U"]; C_U, K_CU = HS["C_U"]

                def mmh(e, fc=fc, hp=hp):
                    last = None
                    for c in range(ND):
                        lhsT = FW[:, c * 1024 + fc * P:
                                   c * 1024 + (fc + 1) * P]
                        for sgm in range(2):
                            last = e.matmul(
                                hp[:, sgm * 512:(sgm + 1) * 512], lhsT,
                                XNT[:, c * H + sgm * 512: c * H + (sgm + 1) * 512],
                                start=(c == 0), stop=(c == ND - 1))
                    return last
                S.op("tensor", mmh, reads=w1zone + XNTKEYS, writes=[hk])
                if fc >= ND - 2:
                    hb = 8192 + par * 1024
                    HPC = BCH[:, hb: hb + 1024]
                    K_HPC = bchk(hb, hb + 1024)
                    S.op("scalar", lambda e, HPC=HPC, hp=hp: e.copy(HPC, hp),
                         reads=[hk], writes=K_HPC)
                    hp = HPC
                    hk_r = K_HPC
                else:
                    hk_r = [hk]
                S.op("scalar", lambda e, hp=hp, Fg=Fg, S_U=S_U: e.activation(
                    S_U, hp, AF.Sin, bias=SB1[:, Fg:Fg + 1], scale=0.1),
                    reads=hk_r + ["QC:SB1"], writes=K_SU)
                S.op("scalar", lambda e, hp=hp, Fg=Fg, C_U=C_U: e.activation(
                    C_U, hp, AF.Sin, bias=SC1[:, Fg:Fg + 1], scale=0.1),
                    reads=hk_r + ["QC:SC1"], writes=K_CU)
                S.op("vector", lambda e, hp=hp, Fg=Fg, C_U=C_U: e.scalar_tensor_tensor(
                    C_U, C_U, Ac[:, Fg:Fg + 1], hp, ALU.mult, ALU.add),
                    reads=K_CU + hk_r + ["QC:A"], writes=K_CU)
                S.op("vector", lambda e, Fg=Fg, S_U=S_U, C_U=C_U: e.scalar_tensor_tensor(
                    S_U, S_U, Bc[:, Fg:Fg + 1], C_U, ALU.mult, ALU.add),
                    reads=K_SU + K_CU + ["QC:B"], writes=K_SU)
                S.op("vector", lambda e, Fg=Fg, S_U=S_U: e.tensor_scalar_add(
                    S_U, S_U, B1c[:, Fg:Fg + 1]),
                    reads=K_SU + ["QC:b1:0"], writes=K_SU)
                S.op("vector", lambda e, fc=fc, S_U=S_U: e.tensor_relu(
                    VAL[:, fc * H:(fc + 1) * H], S_U),
                    reads=K_SU, writes=[f"VAL:{fc},0", f"VAL:{fc},1"])
            def emit_ymms(tc):
                par = tc % 2
                yp = PB[:, par * 1024: par * 1024 + 1024]
                yk = "PB01" if par == 0 else "PB23"

                def mmy(e, g=g, tc=tc, yp=yp, fclo=0, fchi=ND):
                    last = None
                    first = (fclo == 0)
                    if g == 0 and fclo == 0:
                        for sgm in range(2):
                            last = e.matmul(yp[:, sgm * 512:(sgm + 1) * 512],
                                            ONE1[0:1, :],
                                            B2R[0:1, sgm * 512:(sgm + 1) * 512],
                                            start=True, stop=False)
                        first = False
                    for fc in range(fclo, fchi):
                        lhsT = VAL[:, fc * H + tc * P: fc * H + (tc + 1) * P]
                        for sgm in range(2):
                            last = e.matmul(
                                yp[:, sgm * 512:(sgm + 1) * 512], lhsT,
                                FW[:, 8192 + fc * 1024 + sgm * 512:
                                   8192 + fc * 1024 + (sgm + 1) * 512],
                                start=(first and fc == fclo),
                                stop=(fc == ND - 1))
                    return last
                rd0 = [f"VAL:{fc},{tc // 4}" for fc in range(4)] + w2zone
                if g == 0:
                    rd0 += ["ONE1", "B2R"]
                S.op("tensor", lambda e, f=mmy: f(e, fclo=0, fchi=4),
                     reads=rd0, writes=[yk])
                S.op("tensor", lambda e, f=mmy: f(e, fclo=4, fchi=ND),
                     reads=[f"VAL:{fc},{tc // 4}" for fc in range(4, ND)] + w2zone,
                     writes=[yk])
                return yp, yk

            if g == 0:
                for tc in range(NT):
                    yp, yk = emit_ymms(tc)
                    S.op("vector", lambda e, tc=tc, yp=yp: e.tensor_copy(
                        FW[:, 16384 + tc * 1024:16384 + (tc + 1) * 1024], yp),
                        reads=[yk], writes=[f"FW:2,{tc}"])
            elif g < NG - 1:
                for tc in range(NT):
                    yp, yk = emit_ymms(tc)
                    S.op("vector", lambda e, tc=tc, yp=yp: e.scalar_tensor_tensor(
                        FW[:, 16384 + tc * 1024:16384 + (tc + 1) * 1024], yp, 1.0,
                        FW[:, 16384 + tc * 1024:16384 + (tc + 1) * 1024],
                        ALU.mult, ALU.add),
                        reads=[yk, f"FW:2,{tc}"], writes=[f"FW:2,{tc}"])
            else:
                if h == 0:
                    dma("gpsimd", "dFWL", FW[:, 0:8192],
                        AP(fws_d, 0, [[24576, P], [1, 8192]]),
                        ["FWSPILL"], [f"FW:0,{c}" for c in range(ND)])

                def y_a(tc):
                    par = tc % 2
                    yp = PB[:, par * 1024: par * 1024 + 1024]
                    yk = "PB01" if par == 0 else "PB23"
                    YS = ffn_y_slices(par)
                    U2, K_U2 = YS["U2"]; TR4, K_TR4 = YS["TR4"]
                    S.op("vector", lambda e, tc=tc, yp=yp, par=par, U2=U2:
                         e.scalar_tensor_tensor(
                             U2, yp, 1.0,
                             FW[:, 16384 + tc * 1024:16384 + (tc + 1) * 1024],
                             ALU.mult, ALU.add, accum_out=stc(par, 0)),
                         reads=[yk, f"FW:2,{tc}"], writes=K_U2 + [f"st:S{par}"])
                    S.op("scalar", lambda e, par=par, U2=U2, TR4=TR4: e.activation(
                        TR4, U2, AF.Square, accum_out=stc(par, 1)),
                        reads=K_U2, writes=[f"st:SS{par}"] + K_TR4)

                def y_b(tc):
                    par = tc % 2
                    fin(par, stc(par, 0), stc(par, 1), stc(par, 5), stc(par, 6),
                        f"st:S{par}", f"st:SS{par}", f"st:R{par}", f"st:B{par}")

                def y_c(tc):
                    par = tc % 2
                    YS = ffn_y_slices(par)
                    U2, K_U2 = YS["U2"]; LNY, K_LNY = YS["LNY"]
                    S.op("scalar", lambda e, par=par, U2=U2, LNY=LNY: e.activation(
                        LNY, U2, AF.Identity, bias=stc(par, 6), scale=stc(par, 5)),
                        reads=K_U2 + [f"st:R{par}", f"st:B{par}"], writes=K_LNY)
                    OUTS, K_OUTS = mkslice(par * 1024, 0, 1024)
                    S.op("gpsimd", lambda e, tc=tc, LNY=LNY, OUTS=OUTS:
                         e.tensor_add(OUTS, XT[:, tc * D:(tc + 1) * D], LNY),
                         reads=[f"XT:{tc}"] + K_LNY, writes=K_OUTS)
                    dma("sync", f"dST{h}",
                        AP(out_d, (h * H + tc * P) * D, [[D, P], [1, D]]),
                        OUTS, K_OUTS, [])
                    if h == 0 and tc >= 1:
                        dma("scalar", "dX1", XT[:, (tc - 1) * D: tc * D],
                            AP(x_d, (H + (tc - 1) * P) * D, [[D, P], [1, D]]),
                            [], [f"XT:{tc - 1}"])

                def prep_stat(i):
                    sa = STAT[:, ST_S1 + i: ST_S1 + i + 1]
                    ssa = STAT[:, ST_SS1 + i: ST_SS1 + i + 1]
                    tb = 5120 + (i % 2) * 1024
                    TRb = BCH[:, tb: tb + 1024]
                    K_TRb = bchk(tb, tb + 1024)
                    S.op("vector", lambda e, i=i, sa=sa: e.reduce_sum(
                        sa, XNB[:, i * D:(i + 1) * D], AX.X),
                        reads=[f"XNB:{i}"], writes=[f"st:S1:{i}"])
                    S.op("scalar", lambda e, i=i, ssa=ssa, TRb=TRb:
                         e.activation(TRb, XNB[:, i * D:(i + 1) * D],
                                      AF.Square, accum_out=ssa),
                         reads=[f"XNB:{i}"], writes=[f"st:SS1:{i}"] + K_TRb)

                for t in range(NT + 3):
                    if 0 <= t - 3 < NT:
                        y_c(t - 3)
                    if 0 <= t - 2 < NT:
                        y_b(t - 2)
                        if h == 0:
                            prep_stat(t - 2)
                    if 0 <= t - 1 < NT:
                        y_a(t - 1)
                    if t < NT:
                        emit_ymms(t)
                    if h == 0 and t == NT:
                        dma("gpsimd", "dFWL", FW[:, 8192:16384],
                            AP(fws_d, 8192, [[24576, P], [1, 8192]]),
                            ["FWSPILL"], [f"FW:1,{c}" for c in range(ND)])
                    if h == 0 and t == NT + 2:
                        dma("gpsimd", "dFWL", FW[:, 16384:24576],
                            AP(fws_d, 16384, [[24576, P], [1, 8192]]),
                            ["FWSPILL"], [f"FW:2,{c}" for c in range(ND)])
                if h == 0:
                    dma("scalar", "dX1", XT[:, (NT - 1) * D: NT * D],
                        AP(x_d, (H + (NT - 1) * P) * D, [[D, P], [1, D]]),
                        [], [f"XT:{NT - 1}"])
                    S8 = STAT[:, ST_S1:ST_S1 + 8]
                    SS8 = STAT[:, ST_SS1:ST_SS1 + 8]
                    M8 = STAT[:, 64:72]
                    R8 = STAT[:, ST_R1:ST_R1 + 8]
                    B8 = STAT[:, ST_B1:ST_B1 + 8]
                    rdall = ([f"st:S1:{i}" for i in range(NT)]
                             + [f"st:SS1:{i}" for i in range(NT)])
                    S.op("vector", lambda e, M8=M8, S8=S8: e.tensor_scalar_mul(
                        M8, S8, float(1.0 / D)), reads=rdall, writes=["st:M8"])
                    S.op("vector", lambda e, M8=M8, R8=R8: e.tensor_mul(
                        R8, M8, M8), reads=["st:M8"], writes=["st:R8t"])
                    S.op("vector", lambda e, R8=R8, SS8=SS8:
                         e.scalar_tensor_tensor(
                             R8, SS8, float(1.0 / D), R8, ALU.mult,
                             ALU.subtract),
                         reads=rdall + ["st:R8t"], writes=["st:R8t"])
                    S.op("scalar", lambda e, R8=R8: e.activation(
                        R8, R8, AF.Ln, bias=EPSLN[:, 0:1]),
                        reads=["st:R8t", "EPSLN"], writes=["st:R8t"])
                    S.op("scalar", lambda e, R8=R8: e.activation(
                        R8, R8, AF.Exp, scale=-0.5),
                        reads=["st:R8t"], writes=["st:R8"])
                    S.op("vector", lambda e, B8=B8, M8=M8, R8=R8: e.tensor_mul(
                        B8, M8, R8), reads=["st:M8", "st:R8"],
                        writes=["st:B8t"])
                    S.op("vector", lambda e, B8=B8: e.tensor_scalar_mul(
                        B8, B8, -1.0), reads=["st:B8t"], writes=["st:B8"])
                    for i in range(NT):
                        S.op("scalar", lambda e, i=i: e.activation(
                            XNB[:, i * D:(i + 1) * D],
                            XNB[:, i * D:(i + 1) * D], AF.Identity,
                            bias=STAT[:, ST_B1 + i: ST_B1 + i + 1],
                            scale=STAT[:, ST_R1 + i: ST_R1 + i + 1]),
                            reads=[f"XNB:{i}", "st:R8", "st:B8"],
                            writes=[f"XNB:{i}"])
                        transpose_tile(i, f"XNB:{i}",
                                       PA if i % 2 == 0 else PB, 0,
                                       "PA01" if i % 2 == 0 else "PB01",
                                       slot=i, evac_eng="vector")

    fold_one(1, XNTKEYS, XNT)
    ln1_chains()
    fold_one(2, [f"FW:0,{c}" for c in range(ND)], FW)
    fold_one(0, VKEYS, VAL)
    ln1_transposes()
    dma("scalar", "dSPL", fws_d.ap(), FW[:, :], FWKEYS, ["FWSPILL"])
    dma("gpsimd", "dG", TAB[:, 0:8192],
        AP(g_d, 0, [[D, P], [D * P, ND], [1, D]]), TABKEYS, TABKEYS)
    dma("gpsimd", "dLOb0", TAB[:, 0:8192],
        AP(glo_d, 0, [[D, P], [D * P, ND], [1, D]]), TABKEYS, TABKEYS,
        accum=True)
    do_half(0, do_ln1=False)
    do_half(1, do_ln1=False)

    prog = S.build_prog()

    with es:
        blk = es.enter_context(nc.Block())

        def run_eng(engobj, name):
            for wl, emit, inc in prog[name]:
                for semname, val in wl:
                    engobj.wait_ge(sems[semname], val)
                inst = emit(engobj)
                if inc is not None:
                    inst.then_inc(sems[inc[0]], 1 if inc[0].startswith("E")
                                  else 16)
            if name == "sync":
                engobj.wait_ge(sems["dST0"], 16 * NT)
                engobj.wait_ge(sems["dST1"], 16 * NT)

        @blk.sync
        def _(e):
            run_eng(e, "sync")

        @blk.gpsimd
        def _(e):
            run_eng(e, "gpsimd")

        @blk.tensor
        def _(e):
            run_eng(e, "tensor")

        @blk.vector
        def _(e):
            run_eng(e, "vector")

        @blk.scalar
        def _(e):
            run_eng(e, "scalar")

    return nc


_NC_CACHE = [None]


def kernel(L0_attn_x, L0_attn_Wq, L0_attn_Wk, L0_attn_Wv, L0_ffn_W1, L0_ffn_b1,
           L0_ffn_quantum_phases, L0_ffn_quantum_weights_real,
           L0_ffn_quantum_weights_imag, L0_ffn_W2, L0_ffn_b2):
    cs, cslo, cslo2, g, glo, ident = make_tables()
    x = np.ascontiguousarray(np.asarray(L0_attn_x, dtype=np.float32))
    shared = {
        "wq": np.ascontiguousarray(np.asarray(L0_attn_Wq, np.float32)),
        "wk": np.ascontiguousarray(np.asarray(L0_attn_Wk, np.float32)),
        "wv": np.ascontiguousarray(np.asarray(L0_attn_Wv, np.float32)),
        "w1": np.ascontiguousarray(np.asarray(L0_ffn_W1, np.float32)),
        "w2": np.ascontiguousarray(np.asarray(L0_ffn_W2, np.float32)),
        "b1": np.ascontiguousarray(np.asarray(L0_ffn_b1, np.float32)),
        "b2": np.ascontiguousarray(np.asarray(L0_ffn_b2, np.float32)),
        "qth": np.ascontiguousarray(np.asarray(L0_ffn_quantum_phases, np.float32)),
        "qwr": np.ascontiguousarray(
            np.asarray(L0_ffn_quantum_weights_real, np.float32)),
        "qwi": np.ascontiguousarray(
            np.asarray(L0_ffn_quantum_weights_imag, np.float32)),
        "cs": cs, "cslo": cslo, "cslo2": cslo2, "gt": g, "gtlo": glo,
        "identc": ident,
    }
    if _NC_CACHE[0] is None:
        _NC_CACHE[0] = build()
    nc = _NC_CACHE[0]
    in_maps = [dict(shared, x=np.ascontiguousarray(x[b])) for b in range(8)]
    res = run_bass_kernel_spmd(nc, in_maps, core_ids=list(range(8)))
    return np.stack([res.results[b]["out"] for b in range(8)], axis=0)


if __name__ == "__main__":
    import reference
    inp = reference.setup_inputs(0)
    out = kernel(**{k: np.asarray(v) for k, v in inp.items()})
    print(out.shape, out.dtype)


# version 44
# speedup vs baseline: 1.0270x; 1.0179x over previous
import contextlib
import ml_dtypes
import numpy as np
import concourse.bass as bass
import concourse.mybir as mybir
from concourse.bass_utils import run_bass_kernel_spmd

F32 = mybir.dt.float32
I32 = mybir.dt.int32
BF16 = mybir.dt.bfloat16
AF = mybir.ActivationFunctionType
ALU = mybir.AluOpType
AX = mybir.AxisListType

P = 128
D = 1024
ND = 8
T = 2048
H = 1024
NT = 8
F = 512
NJ = 4
NQ = 2
QW = 512
DF = 4096
NG = 4
EPS_FFT = 1e-8
EPS_LN = 1e-5

ENGS = ("sync", "gpsimd", "tensor", "vector", "scalar")


def make_tables():
    e = np.arange(D, dtype=np.float64)
    f = np.arange(F, dtype=np.float64)
    ang = 2.0 * np.pi * np.outer(e, f) / D
    cs64 = np.concatenate(
        [np.cos(ang), -np.sin(ang), np.cos(np.pi * e)[:, None]], axis=1)
    cs = cs64.astype(np.float32)
    cslo = (cs64 - cs.astype(np.float64)).astype(ml_dtypes.bfloat16)
    cslo2 = (cs64 - cs.astype(np.float64)
             - cslo.astype(np.float64)).astype(ml_dtypes.bfloat16)
    d = np.arange(D, dtype=np.float64)
    cf = np.where(f == 0, 1.0, 2.0)
    angG = 2.0 * np.pi * np.outer(f, d) / D
    g64 = np.concatenate(
        [cf[:, None] * np.cos(angG) / D,
         -cf[:, None] * np.sin(angG) / D,
         (np.cos(np.pi * d) / D)[None, :]], axis=0)
    g = g64.astype(np.float32)
    glo = (g64 - g.astype(np.float64)).astype(ml_dtypes.bfloat16)
    ident = np.eye(P, dtype=np.float32)
    return cs, cslo, cslo2, g, glo, ident


class Op:
    __slots__ = ("eng", "emit", "reads", "writes", "dma", "opid", "inc")

    def __init__(self, eng, emit, reads, writes, dma):
        self.eng, self.emit = eng, emit
        self.reads, self.writes = list(reads), list(writes)
        self.dma = dma
        self.inc = None


class Sched:
    def __init__(self):
        self.ops = []
        self.writer = {}
        self.readers = {}
        self.deps = []
        self.dma_mode = {}
        self.dma_last = {}
        self.batch_members = {}

    def dma_sem(self, name, mode):
        self.dma_mode[name] = mode
        self.batch_members[name] = []

    def op(self, eng, emit, reads=(), writes=(), dma=None):
        o = Op(eng, emit, reads, writes, dma)
        o.opid = len(self.ops)
        dep = set()
        for k in o.reads:
            w = self.writer.get(k)
            if w is not None:
                dep.add(w)
        for k in o.writes:
            w = self.writer.get(k)
            if w is not None:
                dep.add(w)
            for r in self.readers.get(k, ()):
                dep.add(r)
        if dma is not None:
            if self.dma_mode[dma] == "serial" and self.dma_last.get(dma) is not None:
                dep.add(self.dma_last[dma])
            self.dma_last[dma] = o.opid
            self.batch_members[dma].append(o.opid)
        dep.discard(o.opid)
        self.ops.append(o)
        self.deps.append(dep)
        for k in o.reads:
            self.readers.setdefault(k, []).append(o.opid)
        for k in o.writes:
            self.writer[k] = o.opid
            self.readers[k] = []
        return o.opid

    def build_prog(self):
        needed = set()
        for dep in self.deps:
            needed.update(dep)
        eng_count = {e: 0 for e in ENGS}
        for o in self.ops:
            if o.dma is not None:
                idx = self.batch_members[o.dma].index(o.opid) + 1
                o.inc = (o.dma, 16 * idx)
            elif o.opid in needed:
                eng_count[o.eng] += 1
                o.inc = ("E" + o.eng, eng_count[o.eng])
        prog = {e: [] for e in ENGS}
        watermark = {e: {} for e in ENGS}
        for o in self.ops:
            waits = {}
            for dd in sorted(self.deps[o.opid]):
                p = self.ops[dd]
                assert p.inc is not None, f"dep {dd} of {o.opid} has no inc"
                sem, val = p.inc
                if p.dma is not None and self.dma_mode[p.dma] == "batch":
                    val = 16 * len(self.batch_members[p.dma])
                waits[sem] = max(waits.get(sem, 0), val)
            wl = []
            for sem, val in waits.items():
                if watermark[o.eng].get(sem, 0) < val:
                    watermark[o.eng][sem] = val
                    wl.append((sem, val))
            prog[o.eng].append((wl, o.emit, o.inc))
        return prog


def build():
    nc = bass.Bass("TRN2", target_bir_lowering=False)
    AP = bass.AP

    x_d = nc.dram_tensor("x", [T, D], F32, kind="ExternalInput")
    wq_d = nc.dram_tensor("wq", [D, D], F32, kind="ExternalInput")
    wk_d = nc.dram_tensor("wk", [D, D], F32, kind="ExternalInput")
    wv_d = nc.dram_tensor("wv", [D, D], F32, kind="ExternalInput")
    w1_d = nc.dram_tensor("w1", [D, DF], F32, kind="ExternalInput")
    w2_d = nc.dram_tensor("w2", [DF, D], F32, kind="ExternalInput")
    b1_d = nc.dram_tensor("b1", [DF], F32, kind="ExternalInput")
    b2_d = nc.dram_tensor("b2", [D], F32, kind="ExternalInput")
    th_d = nc.dram_tensor("qth", [DF], F32, kind="ExternalInput")
    wr_d = nc.dram_tensor("qwr", [DF], F32, kind="ExternalInput")
    wi_d = nc.dram_tensor("qwi", [DF], F32, kind="ExternalInput")
    cs_d = nc.dram_tensor("cs", [D, 2 * F + 1], F32, kind="ExternalInput")
    cslo_d = nc.dram_tensor("cslo", [D, 2 * F + 1], BF16, kind="ExternalInput")
    cslo2_d = nc.dram_tensor("cslo2", [D, 2 * F + 1], BF16, kind="ExternalInput")
    g_d = nc.dram_tensor("gt", [2 * F + 1, D], F32, kind="ExternalInput")
    glo_d = nc.dram_tensor("gtlo", [2 * F + 1, D], BF16, kind="ExternalInput")
    id_d = nc.dram_tensor("identc", [P, P], F32, kind="ExternalInput")
    out_d = nc.dram_tensor("out", [T, D], F32, kind="ExternalOutput")
    fws_d = nc.dram_tensor("fwspill", [P, 24576], BF16)

    es = contextlib.ExitStack()
    sbuf = lambda n, s, dt=F32: es.enter_context(nc.sbuf_tensor(n, s, dt))
    XT = sbuf("XT", [P, NT * D])
    XNT = sbuf("XNT", [P, ND * H], BF16)
    FW = sbuf("FW", [P, 24576], BF16)
    VAL = sbuf("VAL", [P, 8 * H], BF16)
    TAB = sbuf("TAB", [P, ND * 1025], BF16)
    FWN = sbuf("FWN", [P, 24], BF16)
    SCR = sbuf("SCR", [P, 8192])
    BCH = sbuf("BCH", [P, 10240], BF16)
    XNB = sbuf("XNB", [P, 8 * D], BF16)
    CAR = sbuf("CAR", [P, 16])
    STAT = sbuf("STAT", [P, 96])
    QC = sbuf("QC", [P, 256])
    QIb = sbuf("QIb", [P, 32], I32)
    NVAL = sbuf("NVAL", [1, H], BF16)
    GNYQ = sbuf("GNYQ", [1, D], BF16)
    B2R = sbuf("B2R", [1, D], BF16)
    ONE1 = sbuf("ONE1", [1, P], BF16)
    EPSC = sbuf("EPSC", [P, 1])
    EPSLN = sbuf("EPSLN", [P, 1])
    EPS8 = sbuf("EPS8", [P, 1])
    IDN = sbuf("IDN", [P, P], BF16)
    IDNF = sbuf("IDNF", [P, P])
    PA = es.enter_context(nc.psum_tensor("PA", [P, 2048], F32))
    PB = es.enter_context(nc.psum_tensor("PB", [P, 2048], F32))

    semnames = ["Esync", "Egpsimd", "Etensor", "Evector", "Escalar",
                "dPRE", "dPREa", "dPS", "dG", "dW1", "dW2", "dX1", "dSPL", "dFWL",
                "dST0", "dST1", "dXB", "dLOa0", "dLOa1", "dLOa2", "dLOa3",
                "dLOb0", "dLOb1", "dLOb2", "dLOb3"]
    sems = {s: es.enter_context(nc.semaphore(s)) for s in semnames}

    S = Sched()
    for s, m in (("dPRE", "batch"), ("dPREa", "batch"), ("dPS", "batch"), ("dG", "batch"),
                 ("dW1", "serial"), ("dW2", "serial"), ("dX1", "serial"),
                 ("dSPL", "serial"), ("dFWL", "serial"),
                 ("dST0", "serial"), ("dST1", "serial"),
                 ("dXB", "batch"),
                 ("dLOa0", "batch"), ("dLOa1", "batch"), ("dLOa2", "batch"),
                 ("dLOa3", "batch"), ("dLOb0", "batch"),
                 ("dLOb1", "batch"), ("dLOb2", "batch"), ("dLOb3", "batch")):
        S.dma_sem(s, m)

    def scrk(lo, hi):
        return [f"SCR:{b}" for b in range(lo // 512, (hi + 511) // 512)]

    def mkslice(base, lo, hi):
        return SCR[:, base + lo: base + hi], scrk(base + lo, base + hi)

    def pairs_slices(p):
        b = p * 4096
        d = {}
        for nm, lo, hi in (("SQQ", 0, 1024), ("M2Q", 1024, 1536), ("RQ", 1536, 2048),
                           ("SQK", 2048, 3072), ("M2K", 3072, 3584), ("RK", 3584, 4096)):
            d[nm] = mkslice(b, lo, hi)
        return d

    def bchk(lo, hi):
        return [f"BC:{b}" for b in range(lo // 512, (hi + 511) // 512)]

    def bch_slices(p):
        b = p * 5120
        d = {}
        for nm, lo, hi in (("FQN", 0, 1024), ("FKN", 1024, 2048),
                           ("TVB", 2048, 3072), ("PRD", 3072, 4096),
                           ("CSC", 4096, 5120)):
            d[nm] = (BCH[:, b + lo: b + hi], bchk(b + lo, b + hi))
        return d

    def post_slices(p):
        p = p % 4
        d = {"UU": mkslice(p * 2048, 0, 1024),
             "LNU": mkslice(p * 2048, 1024, 2048)}
        d["TRASH"] = (BCH[:, p * 1024:(p + 1) * 1024],
                      [f"BC:{b}" for b in range(p * 2, p * 2 + 2)])
        return d

    def ffn_h_slices(p):
        b = p * 2048
        return {"S_U": mkslice(b, 0, 1024), "C_U": mkslice(b, 1024, 2048)}

    def ffn_y_slices(p):
        return {"U2": mkslice(4096 + p * 1024, 0, 1024),
                "LNY": mkslice(6144 + p * 1024, 0, 1024),
                "TR4": (BCH[:, p * 1024:(p + 1) * 1024], bchk(p * 1024, (p + 1) * 1024))}

    K_CSC = scrk(5120, 6144)
    K_PRD = scrk(4096, 5120)
    TRASH, K_TRASH = SCR[:, 2048:3072], scrk(2048, 3072)

    THc, WRc, WIc, B1c = QC[:, 0:32], QC[:, 32:64], QC[:, 64:96], QC[:, 96:128]
    Ac, Bc, SB1, SC1 = QC[:, 128:160], QC[:, 160:192], QC[:, 192:224], QC[:, 224:256]

    ST_S1, ST_SS1, ST_R1, ST_B1 = 0, 8, 16, 24
    def stc(p, idx):
        return STAT[:, 32 + p * 8 + idx: 32 + p * 8 + idx + 1]

    FWKEYS = [f"FW:{w},{c}" for w in range(3) for c in range(ND)]
    VKEYS = [f"VAL:{g},{q}" for g in range(8) for q in range(NQ)]
    XNTKEYS = [f"XNT:{i}" for i in range(NT)]
    TABKEYS = [f"TAB:{r}" for r in range(ND)]

    def dma(eng, sem, out_ap, in_ap, reads, writes, accum=False):
        if accum:
            S.op(eng, lambda e, o=out_ap, i=in_ap: e.dma_start(
                o, i, accum_op=ALU.add), reads=reads, writes=writes, dma=sem)
        else:
            S.op(eng, lambda e, o=out_ap, i=in_ap: e.dma_start(o, i),
                 reads=reads, writes=writes, dma=sem)

    dma("gpsimd", "dPREa", TAB[:, :],
        AP(cs_d, 0, [[1025, P], [1025 * P, ND], [1, 1025]]), [], TABKEYS)
    dma("gpsimd", "dPREa", XNT[:, 0:8192],
        AP(wk_d, 0, [[D, P], [D * P, ND], [1, D]]), [], XNTKEYS)
    dma("gpsimd", "dPRE", FW[:, 0:8192],
        AP(wv_d, 0, [[D, P], [D * P, ND], [1, D]]),
        [], [f"FW:0,{c}" for c in range(ND)])
    dma("gpsimd", "dPRE", VAL[:, 0:8192],
        AP(wq_d, 0, [[D, P], [D * P, ND], [1, D]]), [], VKEYS)
    dma("gpsimd", "dPRE", IDN[:, :], AP(id_d, 0, [[P, P], [1, P]]), [], ["IDN"])
    dma("gpsimd", "dPRE", B2R[0:1, :], AP(b2_d, 0, [[D, 1], [1, D]]), [], ["B2R"])
    dma("gpsimd", "dPRE", GNYQ[0:1, :],
        AP(g_d, 1024 * D, [[D, 1], [1, D]]), [], ["GNYQ"])
    dma("sync", "dPS", IDNF[:, :], AP(id_d, 0, [[P, P], [1, P]]),
        [], ["IDNF"])
    for o, src in ((0, cslo_d), (1, cslo2_d)):
        for p in range(2):
            kstg = [f"XNB:{i}" for i in range(5)]
            dma("sync", f"dLOa{o * 2 + p}", XNB[:, 0:4100],
                AP(src, p * 4 * 1025 * P,
                   [[1025, P], [1025 * P, 4], [1, 1025]]),
                kstg, kstg)
            S.op("vector", lambda e, p=p: e.tensor_add(
                TAB[:, p * 4100:(p + 1) * 4100],
                TAB[:, p * 4100:(p + 1) * 4100], XNB[:, 0:4100]),
                reads=TABKEYS[p * 4:(p + 1) * 4] + kstg,
                writes=TABKEYS[p * 4:(p + 1) * 4])
    dma("sync", "dPS", XT[:, :],
        AP(x_d, 0, [[D, P], [D * P, NT], [1, D]]),
        [], [f"XT:{i}" for i in range(NT)])
    _QSTG = {"th": (th_d, THc, 4096, "QC:th"), "wr": (wr_d, WRc, 4224, "QC:wr"),
             "wi": (wi_d, WIc, 4352, "QC:wi"), "b1": (b1_d, B1c, 4480, "QC:b1")}
    for _nm, (_src, _dst, _off, _key) in _QSTG.items():
        dma("sync", "dPS", SCR[0:32, _off:_off + 128],
            AP(_src, 0, [[P, 32], [1, P]]), [], [f"qs:{_nm}"])
    for _qi, (_nm, (_src, _dst, _off, _key)) in enumerate(_QSTG.items()):
        S.op("tensor", lambda e, o=_off, qi=_qi: e.matmul(
            PB[:, 1024 + qi * 32: 1024 + (qi + 1) * 32],
            SCR[0:32, o:o + 128], IDNF[0:32, 0:32], start=True, stop=True),
            reads=[f"qs:{_nm}", "IDNF"], writes=[f"qp:{_nm}", "PB23"])
        S.op("scalar", lambda e, d=_dst, qi=_qi: e.copy(
            d, PB[:, 1024 + qi * 32: 1024 + (qi + 1) * 32]),
            reads=[f"qp:{_nm}", "PB23"], writes=[_key + ":0"])


    S.op("vector", lambda e: e.memset(ONE1[0:1, :], 1.0), writes=["ONE1"])
    S.op("vector", lambda e: e.memset(EPSC[:, :], 1e-30), writes=["EPSC"])
    S.op("vector", lambda e: e.memset(EPSLN[:, :], float(EPS_LN)), writes=["EPSLN"])
    S.op("vector", lambda e: e.memset(EPS8[:, :], float(EPS_FFT)), writes=["EPS8"])

    R0 = SCR[:, 5120:5152]
    T0 = SCR[:, 5152:5184]
    DD = SCR[:, 5184:5216]
    CC = SCR[:, 5216:5248]
    STH = SCR[:, 5248:5280]
    CTH = SCR[:, 5280:5312]
    PQ = SCR[:, 5312:5344]

    def rred(shift):
        S.op("vector", lambda e: e.tensor_scalar(
            R0, THc, float(1 / (2 * np.pi)), float(shift), ALU.mult, ALU.add),
            reads=["QC:th:0"], writes=K_CSC)
        S.op("vector", lambda e: e.tensor_copy(QIb[:, :], R0),
             reads=K_CSC, writes=["QIb"])
        S.op("vector", lambda e: e.tensor_copy(T0, QIb[:, :]),
             reads=["QIb"], writes=K_CSC)
        S.op("vector", lambda e: e.tensor_sub(DD, R0, T0),
             reads=K_CSC, writes=K_CSC)
        S.op("vector", lambda e: e.tensor_single_scalar(CC, DD, 0.5, ALU.is_ge),
             reads=K_CSC, writes=K_CSC)
        S.op("vector", lambda e: e.tensor_add(T0, T0, CC),
             reads=K_CSC, writes=K_CSC)
        S.op("vector", lambda e: e.tensor_single_scalar(CC, DD, -0.5, ALU.is_le),
             reads=K_CSC, writes=K_CSC)
        S.op("vector", lambda e: e.tensor_sub(T0, T0, CC),
             reads=K_CSC, writes=K_CSC)
        S.op("vector", lambda e: e.scalar_tensor_tensor(
            R0, T0, float(-2 * np.pi), THc, ALU.mult, ALU.add),
            reads=K_CSC + ["QC:th:0"], writes=K_CSC)
        if shift:
            S.op("vector", lambda e: e.tensor_scalar_add(
                R0, R0, float(shift * 2 * np.pi)), reads=K_CSC, writes=K_CSC)

    rred(0.0)
    S.op("scalar", lambda e: e.activation(STH, R0, AF.Sin),
         reads=K_CSC, writes=["q:sth"])
    rred(0.25)
    S.op("scalar", lambda e: e.activation(CTH, R0, AF.Sin),
         reads=K_CSC, writes=["q:cth"])
    S.op("vector", lambda e: e.tensor_mul(PQ, CTH, WRc),
         reads=["q:cth", "QC:wr:0"], writes=["q:pq"])
    S.op("vector", lambda e: e.tensor_mul(T0, STH, WIc),
         reads=["q:sth", "QC:wi:0", "q:cth"], writes=K_CSC)
    S.op("vector", lambda e: e.tensor_add(PQ, PQ, T0),
         reads=["q:pq"] + K_CSC, writes=["q:pq"])
    S.op("vector", lambda e: e.tensor_mul(T0, PQ, WIc),
         reads=["q:pq", "QC:wi:0"], writes=K_CSC)
    S.op("vector", lambda e: e.tensor_scalar_mul(T0, T0, 0.1),
         reads=K_CSC, writes=K_CSC)
    S.op("vector", lambda e: e.tensor_mul(Ac, T0, STH),
         reads=K_CSC + ["q:sth"], writes=["QC:A"])
    S.op("vector", lambda e: e.tensor_mul(Bc, T0, CTH),
         reads=K_CSC + ["q:cth"], writes=["QC:B"])
    S.op("vector", lambda e: e.tensor_scalar_mul(SB1, B1c, 0.1),
         reads=["QC:b1:0"], writes=["QC:SB1"])
    S.op("vector", lambda e: e.tensor_scalar(
        SC1, B1c, 0.1, float(np.pi / 2), ALU.mult, ALU.add),
        reads=["QC:b1:0"], writes=["QC:SC1"])

    def fin(par, s_ap, ss_ap, r_ap, b_ap, s_key, ss_key, r_key, b_key):
        m_ap, q_ap, v_ap = stc(par, 2), stc(par, 3), stc(par, 4)
        km, kq, kv = f"st:m{par}", f"st:q{par}", f"st:v{par}"
        S.op("vector", lambda e: e.tensor_scalar_mul(m_ap, s_ap, float(1.0 / D)),
             reads=[s_key], writes=[km])
        S.op("vector", lambda e: e.tensor_scalar(q_ap, m_ap, m_ap, None, ALU.mult),
             reads=[km], writes=[kq])
        S.op("vector", lambda e: e.scalar_tensor_tensor(
            v_ap, ss_ap, float(1.0 / D), q_ap, ALU.mult, ALU.subtract),
            reads=[ss_key, kq], writes=[kv])
        S.op("scalar", lambda e: e.activation(v_ap, v_ap, AF.Ln,
                                              bias=EPSLN[:, 0:1]),
             reads=[kv, "EPSLN"], writes=[kv])
        S.op("scalar", lambda e: e.activation(r_ap, v_ap, AF.Exp, scale=-0.5),
             reads=[kv], writes=[r_key])
        S.op("vector", lambda e: e.tensor_scalar(
            b_ap, m_ap, r_ap, -1.0, ALU.mult, ALU.mult),
            reads=[km, r_key], writes=[b_key])

    def ln_tile(i, out_ap, out_key, s_pre=None):
        par = i % 4
        TRp, K_TRp = post_slices(par)["TRASH"]
        sa = STAT[:, ST_S1 + i: ST_S1 + i + 1]
        ssa = STAT[:, ST_SS1 + i: ST_SS1 + i + 1]
        ra = STAT[:, ST_R1 + i: ST_R1 + i + 1]
        ba = STAT[:, ST_B1 + i: ST_B1 + i + 1]
        ks, kss = f"st:S1:{i}", f"st:SS1:{i}"
        kr, kb = f"st:R1:{i}", f"st:B1:{i}"
        if s_pre is None:
            S.op("vector", lambda e: e.reduce_sum(sa, XT[:, i * D:(i + 1) * D], AX.X),
                 reads=[f"XT:{i}"], writes=[ks])
        else:
            ks = s_pre
        S.op("vector", lambda e: e.scalar_tensor_tensor(
            TRp, XT[:, i * D:(i + 1) * D], 1.0, XT[:, i * D:(i + 1) * D],
            ALU.mult, ALU.mult, accum_out=ssa),
            reads=[f"XT:{i}"], writes=[kss] + K_TRp)
        fin(par, sa, ssa, ra, ba, ks, kss, kr, kb)
        S.op("scalar", lambda e: e.activation(
            out_ap, XT[:, i * D:(i + 1) * D], AF.Identity, bias=ba, scale=ra),
            reads=[f"XT:{i}", kr, kb], writes=[out_key])

    def transpose_tile(i, src_key, dst_pt, dst_off, pk, slot=None,
                       evac_eng="scalar"):
        sl = (i % 2) if slot is None else slot
        def mmtr(e, i=i, dst_pt=dst_pt, dst_off=dst_off, sl=sl):
            last = None
            for c in range(ND):
                last = e.matmul(dst_pt[:, dst_off + c * P: dst_off + (c + 1) * P],
                                XNB[:, sl * D + c * P: sl * D + (c + 1) * P],
                                IDN[:, :], start=True, stop=True)
            return last
        S.op("tensor", mmtr, reads=[src_key, "IDN"], writes=[pk])
        if evac_eng == "scalar":
            S.op("scalar", lambda e, i=i: e.copy(
                bass.AP(XNT, i * P, [[ND * H, P], [H, ND], [1, P]]),
                dst_pt[:, dst_off: dst_off + 1024]),
                reads=[pk], writes=[f"XNT:{i}"])
        else:
            S.op("vector", lambda e, i=i: e.tensor_copy(
                bass.AP(XNT, i * P, [[ND * H, P], [H, ND], [1, P]]),
                dst_pt[:, dst_off: dst_off + 1024]),
                reads=[pk], writes=[f"XNT:{i}"])

    def fold_one(w, stage_keys, stage_t):
        for c in range(ND):
            par = (w * ND + c) % 2
            PT = PA if par == 0 else PB
            pk = ["PA01", "PA23"] if par == 0 else ["PB01", "PB23"]

            def mmf(e, PT=PT, st_t=stage_t, c=c, region=0):
                last = None
                for r in range(ND):
                    lhsT = st_t[:, r * D + c * P: r * D + (c + 1) * P]
                    if region == 0:
                        last = e.matmul(PT[:, 0:512], lhsT,
                                        TAB[:, r * 1025: r * 1025 + 512],
                                        start=(r == 0), stop=(r == ND - 1))
                    elif region == 1:
                        last = e.matmul(PT[:, 512:1024], lhsT,
                                        TAB[:, r * 1025 + 512: r * 1025 + 1024],
                                        start=(r == 0), stop=(r == ND - 1))
                    else:
                        last = e.matmul(PT[:, 1024:1025], lhsT,
                                        TAB[:, r * 1025 + 1024: r * 1025 + 1025],
                                        start=(r == 0), stop=(r == ND - 1))
                return last
            S.op("tensor", lambda e, f=mmf: f(e, region=0),
                 reads=stage_keys + TABKEYS, writes=[pk[0]])
            S.op("tensor", lambda e, f=mmf: f(e, region=1),
                 reads=stage_keys + TABKEYS, writes=[pk[0]])
            S.op("tensor", lambda e, f=mmf: f(e, region=2),
                 reads=stage_keys + TABKEYS, writes=[pk[1]])
            S.op("scalar", lambda e, PT=PT, w=w, c=c: e.copy(
                FW[:, (w * ND + c) * D:(w * ND + c + 1) * D], PT[:, 0:1024]),
                reads=[pk[0]], writes=[f"FW:{w},{c}"])
            S.op("scalar", lambda e, PT=PT, w=w, c=c: e.copy(
                FWN[:, c * 3 + w: c * 3 + w + 1], PT[:, 1024:1025]),
                reads=[pk[1]], writes=[f"FWN:{w},{c}"])


    ST_M1 = 40

    def ln1_chains():
        for i in range(NT):
            sa = STAT[:, ST_S1 + i: ST_S1 + i + 1]
            ssa = STAT[:, ST_SS1 + i: ST_SS1 + i + 1]
            TRp, K_TRp = post_slices(i)["TRASH"]
            S.op("vector", lambda e, i=i, sa=sa: e.reduce_sum(
                sa, XT[:, i * D:(i + 1) * D], AX.X),
                reads=[f"XT:{i}"], writes=[f"st:S1:{i}"])
            S.op("vector", lambda e, i=i, ssa=ssa, TRp=TRp: e.scalar_tensor_tensor(
                TRp, XT[:, i * D:(i + 1) * D], 1.0, XT[:, i * D:(i + 1) * D],
                ALU.mult, ALU.mult, accum_out=ssa),
                reads=[f"XT:{i}"], writes=[f"st:SS1:{i}"] + K_TRp)
        S8 = STAT[:, ST_S1:ST_S1 + 8]
        SS8 = STAT[:, ST_SS1:ST_SS1 + 8]
        M8 = STAT[:, ST_M1:ST_M1 + 8]
        R8 = STAT[:, ST_R1:ST_R1 + 8]
        B8 = STAT[:, ST_B1:ST_B1 + 8]
        rdall = [f"st:S1:{i}" for i in range(NT)] + [f"st:SS1:{i}" for i in range(NT)]
        S.op("vector", lambda e: e.tensor_scalar_mul(M8, S8, float(1.0 / D)),
             reads=rdall, writes=["st:M8"])
        S.op("vector", lambda e: e.tensor_mul(R8, M8, M8),
             reads=["st:M8"], writes=["st:R8t"])
        S.op("vector", lambda e: e.scalar_tensor_tensor(
            R8, SS8, float(1.0 / D), R8, ALU.mult, ALU.subtract),
            reads=rdall + ["st:R8t"], writes=["st:R8t"])
        S.op("scalar", lambda e: e.activation(R8, R8, AF.Ln,
                                              bias=EPSLN[:, 0:1]),
             reads=["st:R8t", "EPSLN"], writes=["st:R8t"])
        S.op("scalar", lambda e: e.activation(R8, R8, AF.Exp, scale=-0.5),
             reads=["st:R8t"], writes=["st:R8"])
        S.op("vector", lambda e: e.tensor_mul(B8, M8, R8),
             reads=["st:M8", "st:R8"], writes=["st:B8t"])
        S.op("vector", lambda e: e.tensor_scalar_mul(B8, B8, -1.0),
             reads=["st:B8t"], writes=["st:B8"])
        for i in range(NT):
            TRp, K_TRp = post_slices(i)["TRASH"]
            UUp, K_UUp = post_slices(i)["UU"]
            S.op("vector", lambda e, i=i, UUp=UUp: e.tensor_scalar_mul(
                UUp, XT[:, i * D:(i + 1) * D], STAT[:, ST_R1 + i: ST_R1 + i + 1]),
                reads=[f"XT:{i}", "st:R8"], writes=K_UUp)
            S.op("vector", lambda e, i=i, UUp=UUp: e.tensor_scalar_add(
                XNB[:, i * D:(i + 1) * D], UUp,
                STAT[:, ST_B1 + i: ST_B1 + i + 1]),
                reads=K_UUp + ["st:B8"], writes=[f"XNB:{i}"])

    def ln1_transposes():
        for i in range(NT):
            par = i % 2
            transpose_tile(i, f"XNB:{i}", PA if par == 0 else PB, 0,
                           "PA01" if par == 0 else "PB01", slot=i)

    def ln1_phase():
        for t in range(NT + 1):
            if 0 <= t - 1 < NT:
                i = t - 1
                par = i % 2
                transpose_tile(i, f"XNB:{par}", PA if par == 0 else PB, 0,
                               "PA01" if par == 0 else "PB01")
            if t < NT:
                ln_tile(t, XNB[:, (t % 2) * D:(t % 2) * D + D], f"XNB:{t % 2}")

    def do_half(h, do_ln1=True):
        if do_ln1:
            ln1_chains()
            ln1_transposes()

        def mmnyq(e):
            last = None
            for c in range(ND):
                for sgm in range(2):
                    last = e.matmul(
                        PB[0:3, 1024 + sgm * 512: 1024 + (sgm + 1) * 512],
                        FWN[:, c * 3:(c + 1) * 3],
                        XNT[:, c * H + sgm * 512: c * H + (sgm + 1) * 512],
                        start=(c == 0), stop=(c == ND - 1))
            return last
        S.op("tensor", mmnyq,
             reads=[f"FWN:{w},{c}" for w in range(3) for c in range(ND)] + XNTKEYS,
             writes=["PB23"])
        NYb = 4096
        NYT = SCR[0:3, NYb:NYb + 1024]
        NYT32 = SCR[0:32, NYb:NYb + 1024]
        NYA = SCR[0:2, NYb + 1024:NYb + 2048]
        SHA = SCR[0:32, NYb + 2048:NYb + 3072]
        SHBWS = SCR[0:32, NYb + 3072:NYb + 4096]
        K_NYT = scrk(NYb, NYb + 1024)
        K_NYA = scrk(NYb + 1024, NYb + 2048)
        K_SHA = scrk(NYb + 2048, NYb + 3072)
        K_SHB = scrk(NYb + 3072, NYb + 4096)
        S.op("vector", lambda e: e.memset(NYT32, 0.0), reads=[], writes=K_NYT)
        S.op("scalar", lambda e: e.copy(NYT, PB[0:3, 1024:2048]),
             reads=["PB23"] + K_NYT, writes=K_NYT)
        S.op("scalar", lambda e: e.activation(NYA, SCR[0:2, NYb:NYb + 1024], AF.Abs),
             reads=K_NYT, writes=K_NYA)
        S.op("scalar", lambda e: e.activation(NYA, NYA, AF.Ln,
                                              bias=EPS8[0:2, 0:1]),
             reads=K_NYA + ["EPS8"], writes=K_NYA)
        S.op("scalar", lambda e: e.activation(NYA, NYA, AF.Exp, scale=-1.0),
             reads=K_NYA, writes=K_NYA)
        S.op("vector", lambda e: e.tensor_mul(
            SCR[0:2, NYb:NYb + 1024], SCR[0:2, NYb:NYb + 1024], NYA),
            reads=K_NYT + K_NYA, writes=K_NYT)
        mka = list(range(32)); mka[0] = 1
        mkb = list(range(32)); mkb[0] = 2
        S.op("vector", lambda e: e.stream_shuffle(SHA, NYT32, mka),
             reads=K_NYT, writes=K_SHA)
        S.op("vector", lambda e: e.stream_shuffle(SHBWS, NYT32, mkb),
             reads=K_NYT, writes=K_SHB)
        S.op("vector", lambda e: e.tensor_mul(
            SCR[0:1, NYb + 1024:NYb + 2048], SHA[0:1, :], SHBWS[0:1, :]),
            reads=K_SHA + K_SHB, writes=K_NYA)
        inin = 0.0 if h == 0 else CAR[0:1, 9:10]
        rdn = K_NYA + ([] if h == 0 else ["car:n"])
        S.op("vector", lambda e, inin=inin: e.tensor_tensor_scan(
            SCR[0:1, NYb + 3072:NYb + 4096], SCR[0:1, NYb + 1024:NYb + 2048],
            SCR[0:1, NYb + 1024:NYb + 2048], inin, ALU.add, ALU.bypass),
            reads=rdn, writes=K_SHB)
        S.op("vector", lambda e: e.tensor_copy(
            CAR[0:1, 9:10], SCR[0:1, NYb + 4095:NYb + 4096]),
            reads=K_SHB, writes=["car:n"])
        S.op("vector", lambda e: e.tensor_mul(
            NVAL[0:1, :], SCR[0:1, NYb + 3072:NYb + 4096],
            SCR[0:1, NYb:NYb + 1024]),
            reads=K_SHB + K_NYT, writes=["NVAL"])

        PSLOT = [(PA, 0, "PA01"), (PA, 1024, "PA23"),
                 (PB, 0, "PB01"), (PB, 1024, "PB23")]

        def pr_slot(it, plane):
            return PSLOT[(3 * it + plane) % 4]

        def pr_mm(it):
            q, j = it // NJ, it % NJ
            xnt_r = [f"XNT:{4 * q + t}" for t in range(4)]

            def mmpl(e, w=0, dst=None, dstoff=0, j=j, q=q):
                last = None
                for p in range(2):
                    for c in range(ND):
                        lhsT = FW[:, (w * ND + c) * D + p * F + j * P:
                                  (w * ND + c) * D + p * F + j * P + P]
                        rhs = XNT[:, c * H + q * QW: c * H + (q + 1) * QW]
                        last = e.matmul(
                            dst[:, dstoff + p * F: dstoff + (p + 1) * F],
                            lhsT, rhs, start=(c == 0), stop=(c == ND - 1))
                return last
            for w in range(3):
                dst, off, key = pr_slot(it, w)
                S.op("tensor", lambda e, f=mmpl, w=w, dst=dst, off=off: f(
                    e, w=w, dst=dst, dstoff=off),
                    reads=[f"FW:{w},{c}" for c in range(ND)] + xnt_r,
                    writes=[key])

        def pr_norm(it):
            par = it % 2
            PS = pairs_slices(par)
            BS = bch_slices(par)
            FQN, K_FQN = BS["FQN"]; FKN, K_FKN = BS["FKN"]; TVB, K_TVB = BS["TVB"]
            qd, qo, qk = pr_slot(it, 0)
            kd, ko, kk = pr_slot(it, 1)
            vd, vo, vk = pr_slot(it, 2)
            QSpec = (qd[:, qo:qo + 1024], qk, "SQQ", "M2Q", "RQ", FQN, K_FQN)
            KSpec = (kd[:, ko:ko + 1024], kk, "SQK", "M2K", "RK", FKN, K_FKN)
            sl = {}
            for tag, (psl, pk, SQnm, M2nm, Rnm, Onm, Okey) in (("q", QSpec), ("k", KSpec)):
                sl[tag] = (psl, pk, PS[SQnm], PS[M2nm], PS[Rnm], Onm, Okey)
            for tag in ("q", "k"):
                psl, pk, (SQ, K_SQ), _, _, _, _ = sl[tag]
                S.op("scalar", lambda e, SQ=SQ, psl=psl: e.activation(
                    SQ, psl, AF.Square), reads=[pk], writes=K_SQ)
            S.op("scalar", lambda e, TVB=TVB, vd=vd, vo=vo: e.copy(
                TVB, vd[:, vo:vo + 1024]), reads=[vk], writes=K_TVB)
            for tag in ("q", "k"):
                _, _, (SQ, K_SQ), (M2, K_M2), _, _, _ = sl[tag]
                S.op("vector", lambda e, M2=M2, SQ=SQ: e.tensor_add(
                    M2, SQ[:, 0:512], SQ[:, 512:1024]), reads=K_SQ, writes=K_M2)
            for tag in ("q", "k"):
                _, _, _, (M2, K_M2), _, _, _ = sl[tag]
                S.op("scalar", lambda e, M2=M2: e.activation(
                    M2, M2, AF.Ln), reads=K_M2, writes=K_M2)
            for tag in ("q", "k"):
                _, _, _, (M2, K_M2), _, _, _ = sl[tag]
                S.op("scalar", lambda e, M2=M2: e.activation(
                    M2, M2, AF.Exp, scale=0.5), reads=K_M2, writes=K_M2)
            for tag in ("q", "k"):
                _, _, _, (M2, K_M2), _, _, _ = sl[tag]
                S.op("scalar", lambda e, M2=M2: e.activation(
                    M2, M2, AF.Ln, bias=EPS8[:, 0:1]),
                    reads=K_M2 + ["EPS8"], writes=K_M2)
            for tag in ("q", "k"):
                _, _, _, (M2, K_M2), (RR, K_RR), _, _ = sl[tag]
                S.op("scalar", lambda e, RR=RR, M2=M2: e.activation(
                    RR, M2, AF.Exp, scale=-1.0), reads=K_M2, writes=K_RR)
            for tag in ("q", "k"):
                psl, pk, _, _, (RR, K_RR), Onm, Okey = sl[tag]
                S.op("vector", lambda e, Onm=Onm, psl=psl, RR=RR: e.tensor_tensor(
                    Onm, psl, bass.AP(RR.tensor, RR.offset, [[8192, P], [0, 2], [1, 512]]),
                    ALU.mult), reads=[pk] + K_RR, writes=Okey)

        def pr_tail(it):
            q, j = it // NJ, it % NJ
            gq = h * NQ + q
            par = it % 2
            BS = bch_slices(par)
            FQN, K_FQN = BS["FQN"]; FKN, K_FKN = BS["FKN"]; TVB, K_TVB = BS["TVB"]
            PRD, K_PRD = BS["PRD"]; CSC, K_CSC = BS["CSC"]
            K_C0, K_C1 = [K_CSC[0]], [K_CSC[1]]
            K_P0, K_P1 = [K_PRD[0]], [K_PRD[1]]
            S.op("vector", lambda e, CSC=CSC, FKN=FKN, TVB=TVB: e.tensor_mul(
                CSC[:, 0:512], FKN[:, 0:512], TVB[:, 0:512]),
                reads=K_FKN + K_TVB, writes=K_C0)
            S.op("vector", lambda e, CSC=CSC, FKN=FKN, TVB=TVB: e.tensor_mul(
                CSC[:, 512:1024], FKN[:, 512:1024], TVB[:, 512:1024]),
                reads=K_FKN + K_TVB, writes=K_C1)
            S.op("vector", lambda e, PRD=PRD, CSC=CSC: e.tensor_sub(
                PRD[:, 0:512], CSC[:, 0:512], CSC[:, 512:1024]),
                reads=K_C0 + K_C1, writes=K_P0)
            S.op("vector", lambda e, CSC=CSC, FKN=FKN, TVB=TVB: e.tensor_mul(
                CSC[:, 0:512], FKN[:, 0:512], TVB[:, 512:1024]),
                reads=K_FKN + K_TVB, writes=K_C0)
            S.op("vector", lambda e, CSC=CSC, FKN=FKN, TVB=TVB: e.tensor_mul(
                CSC[:, 512:1024], FKN[:, 512:1024], TVB[:, 0:512]),
                reads=K_FKN + K_TVB, writes=K_C1)
            S.op("vector", lambda e, PRD=PRD, CSC=CSC: e.tensor_add(
                PRD[:, 512:1024], CSC[:, 0:512], CSC[:, 512:1024]),
                reads=K_C0 + K_C1, writes=K_P1)
            for pl in range(2):
                car_ap = CAR[:, 2 * j + pl: 2 * j + pl + 1]
                ini = 0.0 if gq == 0 else car_ap
                rd = [K_PRD[pl]] + ([] if gq == 0 else [f"car:{j},{pl}"])
                S.op("vector", lambda e, pl=pl, ini=ini, CSC=CSC, PRD=PRD:
                     e.tensor_tensor_scan(
                         CSC[:, pl * F:(pl + 1) * F], PRD[:, pl * F:(pl + 1) * F],
                         PRD[:, pl * F:(pl + 1) * F], ini, ALU.add, ALU.bypass),
                     reads=rd, writes=[K_CSC[pl]])
                S.op("vector", lambda e, pl=pl, car_ap=car_ap, CSC=CSC: e.tensor_copy(
                    car_ap, CSC[:, (pl + 1) * F - 1:(pl + 1) * F]),
                    reads=[K_CSC[pl]], writes=[f"car:{j},{pl}"])
            vr = VAL[:, j * H + q * QW: j * H + (q + 1) * QW]
            vi = VAL[:, (4 + j) * H + q * QW: (4 + j) * H + (q + 1) * QW]
            S.op("vector", lambda e, PRD=PRD, CSC=CSC, FQN=FQN: e.tensor_mul(
                PRD[:, 0:512], CSC[:, 0:512], FQN[:, 0:512]),
                reads=K_C0 + K_FQN, writes=K_P0)
            S.op("vector", lambda e, PRD=PRD, CSC=CSC, FQN=FQN: e.tensor_mul(
                PRD[:, 512:1024], CSC[:, 512:1024], FQN[:, 512:1024]),
                reads=K_C1 + K_FQN, writes=K_P1)
            S.op("vector", lambda e, vr=vr, PRD=PRD: e.tensor_add(
                vr, PRD[:, 0:512], PRD[:, 512:1024]),
                reads=K_P0 + K_P1, writes=[f"VAL:{j},{q}"])
            S.op("vector", lambda e, PRD=PRD, CSC=CSC, FQN=FQN: e.tensor_mul(
                PRD[:, 0:512], CSC[:, 512:1024], FQN[:, 0:512]),
                reads=K_C1 + K_FQN, writes=K_P0)
            S.op("vector", lambda e, PRD=PRD, CSC=CSC, FQN=FQN: e.tensor_mul(
                PRD[:, 512:1024], CSC[:, 0:512], FQN[:, 512:1024]),
                reads=K_C0 + K_FQN, writes=K_P1)
            S.op("vector", lambda e, vi=vi, PRD=PRD: e.tensor_sub(
                vi, PRD[:, 0:512], PRD[:, 512:1024]),
                reads=K_P0 + K_P1, writes=[f"VAL:{4 + j},{q}"])

        NIT = NQ * NJ
        for t in range(NIT + 2):
            if 0 <= t - 2 < NIT:
                pr_tail(t - 2)
            if 0 <= t - 1 < NIT:
                pr_norm(t - 1)
            if t < NIT:
                pr_mm(t)

        def po_mm(i):
            par = i % 2
            vh = PA[:, par * 1024: par * 1024 + 1024]
            vhk = "PA01" if par == 0 else "PA23"
            q = i // 4

            def mmir(e, i=i, vh=vh):
                last = None
                for g in range(8):
                    lhsT = VAL[:, g * H + i * P: g * H + (i + 1) * P]
                    for sgm in range(2):
                        last = e.matmul(
                            vh[:, sgm * 512:(sgm + 1) * 512], lhsT,
                            TAB[:, g * D + sgm * 512: g * D + (sgm + 1) * 512],
                            start=(g == 0), stop=False)
                for sgm in range(2):
                    last = e.matmul(vh[:, sgm * 512:(sgm + 1) * 512],
                                    NVAL[0:1, i * P:(i + 1) * P],
                                    GNYQ[0:1, sgm * 512:(sgm + 1) * 512],
                                    start=False, stop=True)
                return last
            S.op("tensor", mmir,
                 reads=[f"VAL:{g},{q}" for g in range(8)] + TABKEYS
                 + ["NVAL", "GNYQ"], writes=[vhk])

        def po_chain(i):
            vh = PA[:, (i % 2) * 1024: (i % 2) * 1024 + 1024]
            vhk = "PA01" if i % 2 == 0 else "PA23"
            par = i % 4
            PO = post_slices(par)
            UU, K_UU = PO["UU"]
            TRASH, K_TRASH = PO["TRASH"]; LNU, K_LNU = PO["LNU"]
            S.op("vector", lambda e, vh=vh, par=par, i=i, UU=UU:
                 e.scalar_tensor_tensor(
                     UU, XNB[:, i * D:(i + 1) * D], 1.0, vh, ALU.mult, ALU.add,
                     accum_out=stc(par, 0)),
                 reads=[f"XNB:{i}", vhk], writes=K_UU + [f"st:S{par}"])
            S.op("scalar", lambda e, par=par, UU=UU, TRASH=TRASH: e.activation(
                TRASH, UU, AF.Square, accum_out=stc(par, 1)),
                reads=K_UU, writes=[f"st:SS{par}"] + K_TRASH)
            fin(par, stc(par, 0), stc(par, 1), stc(par, 5), stc(par, 6),
                f"st:S{par}", f"st:SS{par}", f"st:R{par}", f"st:B{par}")
            S.op("scalar", lambda e, par=par, UU=UU, LNU=LNU: e.activation(
                LNU, UU, AF.Identity, bias=stc(par, 6), scale=stc(par, 5)),
                reads=K_UU + [f"st:R{par}", f"st:B{par}"], writes=K_LNU)
            S.op("vector", lambda e, i=i, LNU=LNU: e.scalar_tensor_tensor(
                XT[:, i * D:(i + 1) * D], LNU, 1.0, XT[:, i * D:(i + 1) * D],
                ALU.mult, ALU.add,
                accum_out=STAT[:, ST_S1 + i: ST_S1 + i + 1]),
                reads=[f"XT:{i}"] + K_LNU,
                writes=[f"XT:{i}", f"st:S1g:{i}"])

        def po_ln3(i):
            par = i % 2
            ln_tile(i, XNB[:, i * D:(i + 1) * D], f"XNB:{i}",
                    s_pre=f"st:S1g:{i}")
            transpose_tile(i, f"XNB:{i}", PB, par * 1024,
                           "PB01" if par == 0 else "PB23", slot=i)

        for t in range(NT + 2):
            if 0 <= t - 2 < NT:
                po_ln3(t - 2)
            if 0 <= t - 1 < NT:
                po_chain(t - 1)
            if t < NT:
                po_mm(t)

        w1zone = [f"FW:0,{c}" for c in range(ND)]
        w2zone = [f"FW:1,{c}" for c in range(ND)]
        for g in range(NG):
            dma("gpsimd", "dW1", FW[:, 0:8192],
                AP(w1_d, g * 1024, [[DF, P], [DF * P, ND], [1, 1024]]),
                [], w1zone)
            dma("gpsimd", "dW2", FW[:, 8192:16384],
                AP(w2_d, g * 1024 * D, [[D, P], [D * P, ND], [1, D]]),
                [], w2zone)
            if h == 0 and g == 1:
                dma("gpsimd", "dXB", XNB[:, :],
                    AP(x_d, H * D, [[D, P], [D * P, NT], [1, D]]),
                    [f"XNB:{i}" for i in range(NT)],
                    [f"XNB:{i}" for i in range(NT)])
            for fc in range(ND):
                Fg = g * ND + fc
                par = fc % 2
                hp = PA[:, par * 1024: par * 1024 + 1024]
                hk = "PA01" if par == 0 else "PA23"
                HS = ffn_h_slices(par)
                S_U, K_SU = HS["S_U"]; C_U, K_CU = HS["C_U"]

                def mmh(e, fc=fc, hp=hp):
                    last = None
                    for c in range(ND):
                        lhsT = FW[:, c * 1024 + fc * P:
                                   c * 1024 + (fc + 1) * P]
                        for sgm in range(2):
                            last = e.matmul(
                                hp[:, sgm * 512:(sgm + 1) * 512], lhsT,
                                XNT[:, c * H + sgm * 512: c * H + (sgm + 1) * 512],
                                start=(c == 0), stop=(c == ND - 1))
                    return last
                S.op("tensor", mmh, reads=w1zone + XNTKEYS, writes=[hk])
                S.op("scalar", lambda e, hp=hp, Fg=Fg, S_U=S_U: e.activation(
                    S_U, hp, AF.Sin, bias=SB1[:, Fg:Fg + 1], scale=0.1),
                    reads=[hk, "QC:SB1"], writes=K_SU)
                S.op("scalar", lambda e, hp=hp, Fg=Fg, C_U=C_U: e.activation(
                    C_U, hp, AF.Sin, bias=SC1[:, Fg:Fg + 1], scale=0.1),
                    reads=[hk, "QC:SC1"], writes=K_CU)
                S.op("vector", lambda e, hp=hp, Fg=Fg, C_U=C_U: e.scalar_tensor_tensor(
                    C_U, C_U, Ac[:, Fg:Fg + 1], hp, ALU.mult, ALU.add),
                    reads=K_CU + [hk, "QC:A"], writes=K_CU)
                S.op("vector", lambda e, Fg=Fg, S_U=S_U, C_U=C_U: e.scalar_tensor_tensor(
                    S_U, S_U, Bc[:, Fg:Fg + 1], C_U, ALU.mult, ALU.add),
                    reads=K_SU + K_CU + ["QC:B"], writes=K_SU)
                S.op("vector", lambda e, Fg=Fg, S_U=S_U: e.tensor_scalar_add(
                    S_U, S_U, B1c[:, Fg:Fg + 1]),
                    reads=K_SU + ["QC:b1:0"], writes=K_SU)
                S.op("vector", lambda e, fc=fc, S_U=S_U: e.tensor_relu(
                    VAL[:, fc * H:(fc + 1) * H], S_U),
                    reads=K_SU, writes=[f"VAL:{fc},0", f"VAL:{fc},1"])
            def emit_ymms(tc):
                par = tc % 2
                yp = PB[:, par * 1024: par * 1024 + 1024]
                yk = "PB01" if par == 0 else "PB23"

                def mmy(e, g=g, tc=tc, yp=yp, fclo=0, fchi=ND):
                    last = None
                    first = (fclo == 0)
                    if g == 0 and fclo == 0:
                        for sgm in range(2):
                            last = e.matmul(yp[:, sgm * 512:(sgm + 1) * 512],
                                            ONE1[0:1, :],
                                            B2R[0:1, sgm * 512:(sgm + 1) * 512],
                                            start=True, stop=False)
                        first = False
                    for fc in range(fclo, fchi):
                        lhsT = VAL[:, fc * H + tc * P: fc * H + (tc + 1) * P]
                        for sgm in range(2):
                            last = e.matmul(
                                yp[:, sgm * 512:(sgm + 1) * 512], lhsT,
                                FW[:, 8192 + fc * 1024 + sgm * 512:
                                   8192 + fc * 1024 + (sgm + 1) * 512],
                                start=(first and fc == fclo),
                                stop=(fc == ND - 1))
                    return last
                rd0 = [f"VAL:{fc},{tc // 4}" for fc in range(4)] + w2zone
                if g == 0:
                    rd0 += ["ONE1", "B2R"]
                S.op("tensor", lambda e, f=mmy: f(e, fclo=0, fchi=4),
                     reads=rd0, writes=[yk])
                S.op("tensor", lambda e, f=mmy: f(e, fclo=4, fchi=ND),
                     reads=[f"VAL:{fc},{tc // 4}" for fc in range(4, ND)] + w2zone,
                     writes=[yk])
                return yp, yk

            if g == 0:
                for tc in range(NT):
                    yp, yk = emit_ymms(tc)
                    S.op("vector", lambda e, tc=tc, yp=yp: e.tensor_copy(
                        FW[:, 16384 + tc * 1024:16384 + (tc + 1) * 1024], yp),
                        reads=[yk], writes=[f"FW:2,{tc}"])
            elif g < NG - 1:
                for tc in range(NT):
                    yp, yk = emit_ymms(tc)
                    S.op("vector", lambda e, tc=tc, yp=yp: e.scalar_tensor_tensor(
                        FW[:, 16384 + tc * 1024:16384 + (tc + 1) * 1024], yp, 1.0,
                        FW[:, 16384 + tc * 1024:16384 + (tc + 1) * 1024],
                        ALU.mult, ALU.add),
                        reads=[yk, f"FW:2,{tc}"], writes=[f"FW:2,{tc}"])
            else:
                if h == 0:
                    dma("gpsimd", "dFWL", FW[:, 0:8192],
                        AP(fws_d, 0, [[24576, P], [1, 8192]]),
                        ["FWSPILL"], [f"FW:0,{c}" for c in range(ND)])

                def y_a(tc):
                    par = tc % 2
                    yp = PB[:, par * 1024: par * 1024 + 1024]
                    yk = "PB01" if par == 0 else "PB23"
                    YS = ffn_y_slices(par)
                    U2, K_U2 = YS["U2"]; TR4, K_TR4 = YS["TR4"]
                    S.op("vector", lambda e, tc=tc, yp=yp, par=par, U2=U2:
                         e.scalar_tensor_tensor(
                             U2, yp, 1.0,
                             FW[:, 16384 + tc * 1024:16384 + (tc + 1) * 1024],
                             ALU.mult, ALU.add, accum_out=stc(par, 0)),
                         reads=[yk, f"FW:2,{tc}"], writes=K_U2 + [f"st:S{par}"])
                    S.op("scalar", lambda e, par=par, U2=U2, TR4=TR4: e.activation(
                        TR4, U2, AF.Square, accum_out=stc(par, 1)),
                        reads=K_U2, writes=[f"st:SS{par}"] + K_TR4)

                def y_b(tc):
                    par = tc % 2
                    fin(par, stc(par, 0), stc(par, 1), stc(par, 5), stc(par, 6),
                        f"st:S{par}", f"st:SS{par}", f"st:R{par}", f"st:B{par}")

                def y_c(tc):
                    par = tc % 2
                    YS = ffn_y_slices(par)
                    U2, K_U2 = YS["U2"]; LNY, K_LNY = YS["LNY"]
                    S.op("scalar", lambda e, par=par, U2=U2, LNY=LNY: e.activation(
                        LNY, U2, AF.Identity, bias=stc(par, 6), scale=stc(par, 5)),
                        reads=K_U2 + [f"st:R{par}", f"st:B{par}"], writes=K_LNY)
                    OUTS, K_OUTS = mkslice(par * 1024, 0, 1024)
                    S.op("gpsimd", lambda e, tc=tc, LNY=LNY, OUTS=OUTS:
                         e.tensor_add(OUTS, XT[:, tc * D:(tc + 1) * D], LNY),
                         reads=[f"XT:{tc}"] + K_LNY, writes=K_OUTS)
                    dma("sync", f"dST{h}",
                        AP(out_d, (h * H + tc * P) * D, [[D, P], [1, D]]),
                        OUTS, K_OUTS, [])
                    if h == 0 and tc >= 1:
                        dma("scalar", "dX1", XT[:, (tc - 1) * D: tc * D],
                            AP(x_d, (H + (tc - 1) * P) * D, [[D, P], [1, D]]),
                            [], [f"XT:{tc - 1}"])

                def prep_stat(i):
                    sa = STAT[:, ST_S1 + i: ST_S1 + i + 1]
                    ssa = STAT[:, ST_SS1 + i: ST_SS1 + i + 1]
                    tb = 5120 + (i % 2) * 1024
                    TRb = BCH[:, tb: tb + 1024]
                    K_TRb = bchk(tb, tb + 1024)
                    S.op("vector", lambda e, i=i, sa=sa: e.reduce_sum(
                        sa, XNB[:, i * D:(i + 1) * D], AX.X),
                        reads=[f"XNB:{i}"], writes=[f"st:S1:{i}"])
                    S.op("scalar", lambda e, i=i, ssa=ssa, TRb=TRb:
                         e.activation(TRb, XNB[:, i * D:(i + 1) * D],
                                      AF.Square, accum_out=ssa),
                         reads=[f"XNB:{i}"], writes=[f"st:SS1:{i}"] + K_TRb)

                for t in range(NT + 3):
                    if 0 <= t - 3 < NT:
                        y_c(t - 3)
                    if 0 <= t - 2 < NT:
                        y_b(t - 2)
                        if h == 0:
                            prep_stat(t - 2)
                    if 0 <= t - 1 < NT:
                        y_a(t - 1)
                    if t < NT:
                        emit_ymms(t)
                    if h == 0 and t == NT:
                        dma("gpsimd", "dFWL", FW[:, 8192:16384],
                            AP(fws_d, 8192, [[24576, P], [1, 8192]]),
                            ["FWSPILL"], [f"FW:1,{c}" for c in range(ND)])
                    if h == 0 and t == NT + 2:
                        dma("gpsimd", "dFWL", FW[:, 16384:24576],
                            AP(fws_d, 16384, [[24576, P], [1, 8192]]),
                            ["FWSPILL"], [f"FW:2,{c}" for c in range(ND)])
                if h == 0:
                    dma("scalar", "dX1", XT[:, (NT - 1) * D: NT * D],
                        AP(x_d, (H + (NT - 1) * P) * D, [[D, P], [1, D]]),
                        [], [f"XT:{NT - 1}"])
                    S8 = STAT[:, ST_S1:ST_S1 + 8]
                    SS8 = STAT[:, ST_SS1:ST_SS1 + 8]
                    M8 = STAT[:, 64:72]
                    R8 = STAT[:, ST_R1:ST_R1 + 8]
                    B8 = STAT[:, ST_B1:ST_B1 + 8]
                    rdall = ([f"st:S1:{i}" for i in range(NT)]
                             + [f"st:SS1:{i}" for i in range(NT)])
                    S.op("vector", lambda e, M8=M8, S8=S8: e.tensor_scalar_mul(
                        M8, S8, float(1.0 / D)), reads=rdall, writes=["st:M8"])
                    S.op("vector", lambda e, M8=M8, R8=R8: e.tensor_mul(
                        R8, M8, M8), reads=["st:M8"], writes=["st:R8t"])
                    S.op("vector", lambda e, R8=R8, SS8=SS8:
                         e.scalar_tensor_tensor(
                             R8, SS8, float(1.0 / D), R8, ALU.mult,
                             ALU.subtract),
                         reads=rdall + ["st:R8t"], writes=["st:R8t"])
                    S.op("scalar", lambda e, R8=R8: e.activation(
                        R8, R8, AF.Ln, bias=EPSLN[:, 0:1]),
                        reads=["st:R8t", "EPSLN"], writes=["st:R8t"])
                    S.op("scalar", lambda e, R8=R8: e.activation(
                        R8, R8, AF.Exp, scale=-0.5),
                        reads=["st:R8t"], writes=["st:R8"])
                    S.op("vector", lambda e, B8=B8, M8=M8, R8=R8: e.tensor_mul(
                        B8, M8, R8), reads=["st:M8", "st:R8"],
                        writes=["st:B8t"])
                    S.op("vector", lambda e, B8=B8: e.tensor_scalar_mul(
                        B8, B8, -1.0), reads=["st:B8t"], writes=["st:B8"])
                    for i in range(NT):
                        S.op("scalar", lambda e, i=i: e.activation(
                            XNB[:, i * D:(i + 1) * D],
                            XNB[:, i * D:(i + 1) * D], AF.Identity,
                            bias=STAT[:, ST_B1 + i: ST_B1 + i + 1],
                            scale=STAT[:, ST_R1 + i: ST_R1 + i + 1]),
                            reads=[f"XNB:{i}", "st:R8", "st:B8"],
                            writes=[f"XNB:{i}"])
                        transpose_tile(i, f"XNB:{i}",
                                       PA if i % 2 == 0 else PB, 0,
                                       "PA01" if i % 2 == 0 else "PB01",
                                       slot=i, evac_eng="vector")

    fold_one(1, XNTKEYS, XNT)
    ln1_chains()
    fold_one(2, [f"FW:0,{c}" for c in range(ND)], FW)
    fold_one(0, VKEYS, VAL)
    ln1_transposes()
    dma("scalar", "dSPL", fws_d.ap(), FW[:, :], FWKEYS, ["FWSPILL"])
    dma("gpsimd", "dG", TAB[:, 0:8192],
        AP(g_d, 0, [[D, P], [D * P, ND], [1, D]]), TABKEYS, TABKEYS)
    dma("gpsimd", "dLOb0", TAB[:, 0:8192],
        AP(glo_d, 0, [[D, P], [D * P, ND], [1, D]]), TABKEYS, TABKEYS,
        accum=True)
    do_half(0, do_ln1=False)
    do_half(1, do_ln1=False)

    prog = S.build_prog()

    with es:
        blk = es.enter_context(nc.Block())

        def run_eng(engobj, name):
            for wl, emit, inc in prog[name]:
                for semname, val in wl:
                    engobj.wait_ge(sems[semname], val)
                inst = emit(engobj)
                if inc is not None:
                    inst.then_inc(sems[inc[0]], 1 if inc[0].startswith("E")
                                  else 16)
            if name == "sync":
                engobj.wait_ge(sems["dST0"], 16 * NT)
                engobj.wait_ge(sems["dST1"], 16 * NT)

        @blk.sync
        def _(e):
            run_eng(e, "sync")

        @blk.gpsimd
        def _(e):
            run_eng(e, "gpsimd")

        @blk.tensor
        def _(e):
            run_eng(e, "tensor")

        @blk.vector
        def _(e):
            run_eng(e, "vector")

        @blk.scalar
        def _(e):
            run_eng(e, "scalar")

    return nc


_NC_CACHE = [None]


def kernel(L0_attn_x, L0_attn_Wq, L0_attn_Wk, L0_attn_Wv, L0_ffn_W1, L0_ffn_b1,
           L0_ffn_quantum_phases, L0_ffn_quantum_weights_real,
           L0_ffn_quantum_weights_imag, L0_ffn_W2, L0_ffn_b2):
    cs, cslo, cslo2, g, glo, ident = make_tables()
    x = np.ascontiguousarray(np.asarray(L0_attn_x, dtype=np.float32))
    shared = {
        "wq": np.ascontiguousarray(np.asarray(L0_attn_Wq, np.float32)),
        "wk": np.ascontiguousarray(np.asarray(L0_attn_Wk, np.float32)),
        "wv": np.ascontiguousarray(np.asarray(L0_attn_Wv, np.float32)),
        "w1": np.ascontiguousarray(np.asarray(L0_ffn_W1, np.float32)),
        "w2": np.ascontiguousarray(np.asarray(L0_ffn_W2, np.float32)),
        "b1": np.ascontiguousarray(np.asarray(L0_ffn_b1, np.float32)),
        "b2": np.ascontiguousarray(np.asarray(L0_ffn_b2, np.float32)),
        "qth": np.ascontiguousarray(np.asarray(L0_ffn_quantum_phases, np.float32)),
        "qwr": np.ascontiguousarray(
            np.asarray(L0_ffn_quantum_weights_real, np.float32)),
        "qwi": np.ascontiguousarray(
            np.asarray(L0_ffn_quantum_weights_imag, np.float32)),
        "cs": cs, "cslo": cslo, "cslo2": cslo2, "gt": g, "gtlo": glo,
        "identc": ident,
    }
    if _NC_CACHE[0] is None:
        _NC_CACHE[0] = build()
    nc = _NC_CACHE[0]
    in_maps = [dict(shared, x=np.ascontiguousarray(x[b])) for b in range(8)]
    res = run_bass_kernel_spmd(nc, in_maps, core_ids=list(range(8)))
    return np.stack([res.results[b]["out"] for b in range(8)], axis=0)


if __name__ == "__main__":
    import reference
    inp = reference.setup_inputs(0)
    out = kernel(**{k: np.asarray(v) for k, v in inp.items()})
    print(out.shape, out.dtype)


# version 45
# speedup vs baseline: 1.0335x; 1.0063x over previous
import contextlib
import ml_dtypes
import numpy as np
import concourse.bass as bass
import concourse.mybir as mybir
from concourse.bass_utils import run_bass_kernel_spmd

F32 = mybir.dt.float32
I32 = mybir.dt.int32
BF16 = mybir.dt.bfloat16
AF = mybir.ActivationFunctionType
ALU = mybir.AluOpType
AX = mybir.AxisListType

P = 128
D = 1024
ND = 8
T = 2048
H = 1024
NT = 8
F = 512
NJ = 4
NQ = 2
QW = 512
DF = 4096
NG = 4
EPS_FFT = 1e-8
EPS_LN = 1e-5

ENGS = ("sync", "gpsimd", "tensor", "vector", "scalar")


def make_tables():
    e = np.arange(D, dtype=np.float64)
    f = np.arange(F, dtype=np.float64)
    ang = 2.0 * np.pi * np.outer(e, f) / D
    cs64 = np.concatenate(
        [np.cos(ang), -np.sin(ang), np.cos(np.pi * e)[:, None]], axis=1)
    cs = cs64.astype(np.float32)
    cslo = (cs64 - cs.astype(np.float64)).astype(ml_dtypes.bfloat16)
    cslo2 = (cs64 - cs.astype(np.float64)
             - cslo.astype(np.float64)).astype(ml_dtypes.bfloat16)
    d = np.arange(D, dtype=np.float64)
    cf = np.where(f == 0, 1.0, 2.0)
    angG = 2.0 * np.pi * np.outer(f, d) / D
    g64 = np.concatenate(
        [cf[:, None] * np.cos(angG) / D,
         -cf[:, None] * np.sin(angG) / D,
         (np.cos(np.pi * d) / D)[None, :]], axis=0)
    g = g64.astype(np.float32)
    glo = (g64 - g.astype(np.float64)).astype(ml_dtypes.bfloat16)
    ident = np.eye(P, dtype=np.float32)
    return cs, cslo, cslo2, g, glo, ident


class Op:
    __slots__ = ("eng", "emit", "reads", "writes", "dma", "opid", "inc")

    def __init__(self, eng, emit, reads, writes, dma):
        self.eng, self.emit = eng, emit
        self.reads, self.writes = list(reads), list(writes)
        self.dma = dma
        self.inc = None


class Sched:
    def __init__(self):
        self.ops = []
        self.writer = {}
        self.readers = {}
        self.deps = []
        self.dma_mode = {}
        self.dma_last = {}
        self.batch_members = {}

    def dma_sem(self, name, mode):
        self.dma_mode[name] = mode
        self.batch_members[name] = []

    def op(self, eng, emit, reads=(), writes=(), dma=None):
        o = Op(eng, emit, reads, writes, dma)
        o.opid = len(self.ops)
        dep = set()
        for k in o.reads:
            w = self.writer.get(k)
            if w is not None:
                dep.add(w)
        for k in o.writes:
            w = self.writer.get(k)
            if w is not None:
                dep.add(w)
            for r in self.readers.get(k, ()):
                dep.add(r)
        if dma is not None:
            if self.dma_mode[dma] == "serial" and self.dma_last.get(dma) is not None:
                dep.add(self.dma_last[dma])
            self.dma_last[dma] = o.opid
            self.batch_members[dma].append(o.opid)
        dep.discard(o.opid)
        self.ops.append(o)
        self.deps.append(dep)
        for k in o.reads:
            self.readers.setdefault(k, []).append(o.opid)
        for k in o.writes:
            self.writer[k] = o.opid
            self.readers[k] = []
        return o.opid

    def build_prog(self):
        needed = set()
        for dep in self.deps:
            needed.update(dep)
        eng_count = {e: 0 for e in ENGS}
        for o in self.ops:
            if o.dma is not None:
                idx = self.batch_members[o.dma].index(o.opid) + 1
                o.inc = (o.dma, 16 * idx)
            elif o.opid in needed:
                eng_count[o.eng] += 1
                o.inc = ("E" + o.eng, eng_count[o.eng])
        prog = {e: [] for e in ENGS}
        watermark = {e: {} for e in ENGS}
        for o in self.ops:
            waits = {}
            for dd in sorted(self.deps[o.opid]):
                p = self.ops[dd]
                assert p.inc is not None, f"dep {dd} of {o.opid} has no inc"
                sem, val = p.inc
                if p.dma is not None and self.dma_mode[p.dma] == "batch":
                    val = 16 * len(self.batch_members[p.dma])
                waits[sem] = max(waits.get(sem, 0), val)
            wl = []
            for sem, val in waits.items():
                if watermark[o.eng].get(sem, 0) < val:
                    watermark[o.eng][sem] = val
                    wl.append((sem, val))
            prog[o.eng].append((wl, o.emit, o.inc))
        return prog


def build():
    nc = bass.Bass("TRN2", target_bir_lowering=False)
    AP = bass.AP

    x_d = nc.dram_tensor("x", [T, D], F32, kind="ExternalInput")
    wq_d = nc.dram_tensor("wq", [D, D], F32, kind="ExternalInput")
    wk_d = nc.dram_tensor("wk", [D, D], F32, kind="ExternalInput")
    wv_d = nc.dram_tensor("wv", [D, D], F32, kind="ExternalInput")
    w1_d = nc.dram_tensor("w1", [D, DF], F32, kind="ExternalInput")
    w2_d = nc.dram_tensor("w2", [DF, D], F32, kind="ExternalInput")
    b1_d = nc.dram_tensor("b1", [DF], F32, kind="ExternalInput")
    b2_d = nc.dram_tensor("b2", [D], F32, kind="ExternalInput")
    th_d = nc.dram_tensor("qth", [DF], F32, kind="ExternalInput")
    wr_d = nc.dram_tensor("qwr", [DF], F32, kind="ExternalInput")
    wi_d = nc.dram_tensor("qwi", [DF], F32, kind="ExternalInput")
    cs_d = nc.dram_tensor("cs", [D, 2 * F + 1], F32, kind="ExternalInput")
    cslo_d = nc.dram_tensor("cslo", [D, 2 * F + 1], BF16, kind="ExternalInput")
    cslo2_d = nc.dram_tensor("cslo2", [D, 2 * F + 1], BF16, kind="ExternalInput")
    g_d = nc.dram_tensor("gt", [2 * F + 1, D], F32, kind="ExternalInput")
    glo_d = nc.dram_tensor("gtlo", [2 * F + 1, D], BF16, kind="ExternalInput")
    id_d = nc.dram_tensor("identc", [P, P], F32, kind="ExternalInput")
    out_d = nc.dram_tensor("out", [T, D], F32, kind="ExternalOutput")
    fws_d = nc.dram_tensor("fwspill", [P, 24576], BF16)

    es = contextlib.ExitStack()
    sbuf = lambda n, s, dt=F32: es.enter_context(nc.sbuf_tensor(n, s, dt))
    XT = sbuf("XT", [P, NT * D])
    XNT = sbuf("XNT", [P, ND * H], BF16)
    FW = sbuf("FW", [P, 24576], BF16)
    VAL = sbuf("VAL", [P, 8 * H], BF16)
    TAB = sbuf("TAB", [P, ND * 1025], BF16)
    FWN = sbuf("FWN", [P, 24], BF16)
    SCR = sbuf("SCR", [P, 8192])
    BCH = sbuf("BCH", [P, 10240], BF16)
    XNB = sbuf("XNB", [P, 8 * D], BF16)
    CAR = sbuf("CAR", [P, 16])
    STAT = sbuf("STAT", [P, 96])
    QC = sbuf("QC", [P, 256])
    QIb = sbuf("QIb", [P, 32], I32)
    NVAL = sbuf("NVAL", [1, H], BF16)
    GNYQ = sbuf("GNYQ", [1, D], BF16)
    B2R = sbuf("B2R", [1, D], BF16)
    ONE1 = sbuf("ONE1", [1, P], BF16)
    EPSC = sbuf("EPSC", [P, 1])
    EPSLN = sbuf("EPSLN", [P, 1])
    EPS8 = sbuf("EPS8", [P, 1])
    IDN = sbuf("IDN", [P, P], BF16)
    IDNF = sbuf("IDNF", [P, P])
    PA = es.enter_context(nc.psum_tensor("PA", [P, 2048], F32))
    PB = es.enter_context(nc.psum_tensor("PB", [P, 2048], F32))

    semnames = ["Esync", "Egpsimd", "Etensor", "Evector", "Escalar",
                "dPRE", "dPREa", "dPS", "dG", "dW1", "dW2", "dX1", "dSPL", "dFWL",
                "dST0", "dST1", "dXB", "dLOa0", "dLOa1", "dLOa2", "dLOa3",
                "dLOb0", "dLOb1", "dLOb2", "dLOb3"]
    sems = {s: es.enter_context(nc.semaphore(s)) for s in semnames}

    S = Sched()
    for s, m in (("dPRE", "batch"), ("dPREa", "batch"), ("dPS", "batch"), ("dG", "batch"),
                 ("dW1", "serial"), ("dW2", "serial"), ("dX1", "serial"),
                 ("dSPL", "serial"), ("dFWL", "serial"),
                 ("dST0", "serial"), ("dST1", "serial"),
                 ("dXB", "batch"),
                 ("dLOa0", "batch"), ("dLOa1", "batch"), ("dLOa2", "batch"),
                 ("dLOa3", "batch"), ("dLOb0", "batch"),
                 ("dLOb1", "batch"), ("dLOb2", "batch"), ("dLOb3", "batch")):
        S.dma_sem(s, m)

    def scrk(lo, hi):
        return [f"SCR:{b}" for b in range(lo // 512, (hi + 511) // 512)]

    def mkslice(base, lo, hi):
        return SCR[:, base + lo: base + hi], scrk(base + lo, base + hi)

    def pairs_slices(p):
        b = p * 4096
        d = {}
        for nm, lo, hi in (("SQQ", 0, 1024), ("M2Q", 1024, 1536), ("RQ", 1536, 2048),
                           ("SQK", 2048, 3072), ("M2K", 3072, 3584), ("RK", 3584, 4096)):
            d[nm] = mkslice(b, lo, hi)
        return d

    def bchk(lo, hi):
        return [f"BC:{b}" for b in range(lo // 512, (hi + 511) // 512)]

    def bch_slices(p):
        b = p * 5120
        d = {}
        for nm, lo, hi in (("FQN", 0, 1024), ("FKN", 1024, 2048),
                           ("TVB", 2048, 3072), ("PRD", 3072, 4096),
                           ("CSC", 4096, 5120)):
            d[nm] = (BCH[:, b + lo: b + hi], bchk(b + lo, b + hi))
        return d

    def post_slices(p):
        p = p % 4
        d = {"UU": mkslice(p * 2048, 0, 1024),
             "LNU": mkslice(p * 2048, 1024, 2048)}
        d["TRASH"] = (BCH[:, p * 1024:(p + 1) * 1024],
                      [f"BC:{b}" for b in range(p * 2, p * 2 + 2)])
        return d

    def ffn_h_slices(p):
        b = p * 2048
        return {"S_U": mkslice(b, 0, 1024), "C_U": mkslice(b, 1024, 2048)}

    def ffn_y_slices(p):
        return {"U2": mkslice(4096 + p * 1024, 0, 1024),
                "LNY": mkslice(6144 + p * 1024, 0, 1024),
                "TR4": (BCH[:, p * 1024:(p + 1) * 1024], bchk(p * 1024, (p + 1) * 1024))}

    K_CSC = scrk(5120, 6144)
    K_PRD = scrk(4096, 5120)
    TRASH, K_TRASH = SCR[:, 2048:3072], scrk(2048, 3072)

    THc, WRc, WIc, B1c = QC[:, 0:32], QC[:, 32:64], QC[:, 64:96], QC[:, 96:128]
    Ac, Bc, SB1, SC1 = QC[:, 128:160], QC[:, 160:192], QC[:, 192:224], QC[:, 224:256]

    ST_S1, ST_SS1, ST_R1, ST_B1 = 0, 8, 16, 24
    def stc(p, idx):
        return STAT[:, 32 + p * 8 + idx: 32 + p * 8 + idx + 1]

    FWKEYS = [f"FW:{w},{c}" for w in range(3) for c in range(ND)]
    VKEYS = [f"VAL:{g},{q}" for g in range(8) for q in range(NQ)]
    XNTKEYS = [f"XNT:{i}" for i in range(NT)]
    TABKEYS = [f"TAB:{r}" for r in range(ND)]

    def dma(eng, sem, out_ap, in_ap, reads, writes, accum=False):
        if accum:
            S.op(eng, lambda e, o=out_ap, i=in_ap: e.dma_start(
                o, i, accum_op=ALU.add), reads=reads, writes=writes, dma=sem)
        else:
            S.op(eng, lambda e, o=out_ap, i=in_ap: e.dma_start(o, i),
                 reads=reads, writes=writes, dma=sem)

    dma("gpsimd", "dPREa", TAB[:, :],
        AP(cs_d, 0, [[1025, P], [1025 * P, ND], [1, 1025]]), [], TABKEYS)
    dma("gpsimd", "dPREa", XNT[:, 0:8192],
        AP(wk_d, 0, [[D, P], [D * P, ND], [1, D]]), [], XNTKEYS)
    dma("gpsimd", "dPRE", FW[:, 0:8192],
        AP(wv_d, 0, [[D, P], [D * P, ND], [1, D]]),
        [], [f"FW:0,{c}" for c in range(ND)])
    dma("gpsimd", "dPRE", VAL[:, 0:8192],
        AP(wq_d, 0, [[D, P], [D * P, ND], [1, D]]), [], VKEYS)
    dma("gpsimd", "dPRE", IDN[:, :], AP(id_d, 0, [[P, P], [1, P]]), [], ["IDN"])
    dma("gpsimd", "dPRE", B2R[0:1, :], AP(b2_d, 0, [[D, 1], [1, D]]), [], ["B2R"])
    dma("gpsimd", "dPRE", GNYQ[0:1, :],
        AP(g_d, 1024 * D, [[D, 1], [1, D]]), [], ["GNYQ"])
    dma("sync", "dPS", IDNF[:, :], AP(id_d, 0, [[P, P], [1, P]]),
        [], ["IDNF"])
    for o, src in ((0, cslo_d), (1, cslo2_d)):
        for p in range(2):
            kstg = [f"XNB:{i}" for i in range(5)]
            dma("sync", f"dLOa{o * 2 + p}", XNB[:, 0:4100],
                AP(src, p * 4 * 1025 * P,
                   [[1025, P], [1025 * P, 4], [1, 1025]]),
                kstg, kstg)
            S.op("vector", lambda e, p=p: e.tensor_add(
                TAB[:, p * 4100:(p + 1) * 4100],
                TAB[:, p * 4100:(p + 1) * 4100], XNB[:, 0:4100]),
                reads=TABKEYS[p * 4:(p + 1) * 4] + kstg,
                writes=TABKEYS[p * 4:(p + 1) * 4])
    dma("sync", "dPS", XT[:, :],
        AP(x_d, 0, [[D, P], [D * P, NT], [1, D]]),
        [], [f"XT:{i}" for i in range(NT)])
    _QSTG = {"th": (th_d, THc, 4096, "QC:th"), "wr": (wr_d, WRc, 4224, "QC:wr"),
             "wi": (wi_d, WIc, 4352, "QC:wi"), "b1": (b1_d, B1c, 4480, "QC:b1")}
    for _nm, (_src, _dst, _off, _key) in _QSTG.items():
        dma("sync", "dPS", SCR[0:32, _off:_off + 128],
            AP(_src, 0, [[P, 32], [1, P]]), [], [f"qs:{_nm}"])
    for _qi, (_nm, (_src, _dst, _off, _key)) in enumerate(_QSTG.items()):
        S.op("tensor", lambda e, o=_off, qi=_qi: e.matmul(
            PB[:, 1024 + qi * 32: 1024 + (qi + 1) * 32],
            SCR[0:32, o:o + 128], IDNF[0:32, 0:32], start=True, stop=True),
            reads=[f"qs:{_nm}", "IDNF"], writes=[f"qp:{_nm}", "PB23"])
        S.op("scalar", lambda e, d=_dst, qi=_qi: e.copy(
            d, PB[:, 1024 + qi * 32: 1024 + (qi + 1) * 32]),
            reads=[f"qp:{_nm}", "PB23"], writes=[_key + ":0"])


    S.op("vector", lambda e: e.memset(ONE1[0:1, :], 1.0), writes=["ONE1"])
    S.op("vector", lambda e: e.memset(EPSC[:, :], 1e-30), writes=["EPSC"])
    S.op("vector", lambda e: e.memset(EPSLN[:, :], float(EPS_LN)), writes=["EPSLN"])
    S.op("vector", lambda e: e.memset(EPS8[:, :], float(EPS_FFT)), writes=["EPS8"])

    R0 = SCR[:, 5120:5152]
    T0 = SCR[:, 5152:5184]
    DD = SCR[:, 5184:5216]
    CC = SCR[:, 5216:5248]
    STH = SCR[:, 5248:5280]
    CTH = SCR[:, 5280:5312]
    PQ = SCR[:, 5312:5344]

    def rred(shift):
        S.op("vector", lambda e: e.tensor_scalar(
            R0, THc, float(1 / (2 * np.pi)), float(shift), ALU.mult, ALU.add),
            reads=["QC:th:0"], writes=K_CSC)
        S.op("vector", lambda e: e.tensor_copy(QIb[:, :], R0),
             reads=K_CSC, writes=["QIb"])
        S.op("vector", lambda e: e.tensor_copy(T0, QIb[:, :]),
             reads=["QIb"], writes=K_CSC)
        S.op("vector", lambda e: e.tensor_sub(DD, R0, T0),
             reads=K_CSC, writes=K_CSC)
        S.op("vector", lambda e: e.tensor_single_scalar(CC, DD, 0.5, ALU.is_ge),
             reads=K_CSC, writes=K_CSC)
        S.op("vector", lambda e: e.tensor_add(T0, T0, CC),
             reads=K_CSC, writes=K_CSC)
        S.op("vector", lambda e: e.tensor_single_scalar(CC, DD, -0.5, ALU.is_le),
             reads=K_CSC, writes=K_CSC)
        S.op("vector", lambda e: e.tensor_sub(T0, T0, CC),
             reads=K_CSC, writes=K_CSC)
        S.op("vector", lambda e: e.scalar_tensor_tensor(
            R0, T0, float(-2 * np.pi), THc, ALU.mult, ALU.add),
            reads=K_CSC + ["QC:th:0"], writes=K_CSC)
        if shift:
            S.op("vector", lambda e: e.tensor_scalar_add(
                R0, R0, float(shift * 2 * np.pi)), reads=K_CSC, writes=K_CSC)

    rred(0.0)
    S.op("scalar", lambda e: e.activation(STH, R0, AF.Sin),
         reads=K_CSC, writes=["q:sth"])
    rred(0.25)
    S.op("scalar", lambda e: e.activation(CTH, R0, AF.Sin),
         reads=K_CSC, writes=["q:cth"])
    S.op("vector", lambda e: e.tensor_mul(PQ, CTH, WRc),
         reads=["q:cth", "QC:wr:0"], writes=["q:pq"])
    S.op("vector", lambda e: e.tensor_mul(T0, STH, WIc),
         reads=["q:sth", "QC:wi:0", "q:cth"], writes=K_CSC)
    S.op("vector", lambda e: e.tensor_add(PQ, PQ, T0),
         reads=["q:pq"] + K_CSC, writes=["q:pq"])
    S.op("vector", lambda e: e.tensor_mul(T0, PQ, WIc),
         reads=["q:pq", "QC:wi:0"], writes=K_CSC)
    S.op("vector", lambda e: e.tensor_scalar_mul(T0, T0, 0.1),
         reads=K_CSC, writes=K_CSC)
    S.op("vector", lambda e: e.tensor_mul(Ac, T0, STH),
         reads=K_CSC + ["q:sth"], writes=["QC:A"])
    S.op("vector", lambda e: e.tensor_mul(Bc, T0, CTH),
         reads=K_CSC + ["q:cth"], writes=["QC:B"])
    S.op("vector", lambda e: e.tensor_scalar_mul(SB1, B1c, 0.1),
         reads=["QC:b1:0"], writes=["QC:SB1"])
    S.op("vector", lambda e: e.tensor_scalar(
        SC1, B1c, 0.1, float(np.pi / 2), ALU.mult, ALU.add),
        reads=["QC:b1:0"], writes=["QC:SC1"])

    def fin(par, s_ap, ss_ap, r_ap, b_ap, s_key, ss_key, r_key, b_key):
        m_ap, q_ap, v_ap = stc(par, 2), stc(par, 3), stc(par, 4)
        km, kq, kv = f"st:m{par}", f"st:q{par}", f"st:v{par}"
        S.op("vector", lambda e: e.tensor_scalar_mul(m_ap, s_ap, float(1.0 / D)),
             reads=[s_key], writes=[km])
        S.op("vector", lambda e: e.tensor_scalar(q_ap, m_ap, m_ap, None, ALU.mult),
             reads=[km], writes=[kq])
        S.op("vector", lambda e: e.scalar_tensor_tensor(
            v_ap, ss_ap, float(1.0 / D), q_ap, ALU.mult, ALU.subtract),
            reads=[ss_key, kq], writes=[kv])
        S.op("scalar", lambda e: e.activation(v_ap, v_ap, AF.Ln,
                                              bias=EPSLN[:, 0:1]),
             reads=[kv, "EPSLN"], writes=[kv])
        S.op("scalar", lambda e: e.activation(r_ap, v_ap, AF.Exp, scale=-0.5),
             reads=[kv], writes=[r_key])
        S.op("vector", lambda e: e.tensor_scalar(
            b_ap, m_ap, r_ap, -1.0, ALU.mult, ALU.mult),
            reads=[km, r_key], writes=[b_key])

    def ln_tile(i, out_ap, out_key, s_pre=None):
        par = i % 4
        TRp, K_TRp = post_slices(par)["TRASH"]
        sa = STAT[:, ST_S1 + i: ST_S1 + i + 1]
        ssa = STAT[:, ST_SS1 + i: ST_SS1 + i + 1]
        ra = STAT[:, ST_R1 + i: ST_R1 + i + 1]
        ba = STAT[:, ST_B1 + i: ST_B1 + i + 1]
        ks, kss = f"st:S1:{i}", f"st:SS1:{i}"
        kr, kb = f"st:R1:{i}", f"st:B1:{i}"
        if s_pre is None:
            S.op("vector", lambda e: e.reduce_sum(sa, XT[:, i * D:(i + 1) * D], AX.X),
                 reads=[f"XT:{i}"], writes=[ks])
        else:
            ks = s_pre
        S.op("vector", lambda e: e.scalar_tensor_tensor(
            TRp, XT[:, i * D:(i + 1) * D], 1.0, XT[:, i * D:(i + 1) * D],
            ALU.mult, ALU.mult, accum_out=ssa),
            reads=[f"XT:{i}"], writes=[kss] + K_TRp)
        fin(par, sa, ssa, ra, ba, ks, kss, kr, kb)
        S.op("scalar", lambda e: e.activation(
            out_ap, XT[:, i * D:(i + 1) * D], AF.Identity, bias=ba, scale=ra),
            reads=[f"XT:{i}", kr, kb], writes=[out_key])

    def transpose_tile(i, src_key, dst_pt, dst_off, pk, slot=None,
                       evac_eng="scalar"):
        sl = (i % 2) if slot is None else slot
        def mmtr(e, i=i, dst_pt=dst_pt, dst_off=dst_off, sl=sl):
            last = None
            for c in range(ND):
                last = e.matmul(dst_pt[:, dst_off + c * P: dst_off + (c + 1) * P],
                                XNB[:, sl * D + c * P: sl * D + (c + 1) * P],
                                IDN[:, :], start=True, stop=True)
            return last
        S.op("tensor", mmtr, reads=[src_key, "IDN"], writes=[pk])
        if evac_eng == "scalar":
            S.op("scalar", lambda e, i=i: e.copy(
                bass.AP(XNT, i * P, [[ND * H, P], [H, ND], [1, P]]),
                dst_pt[:, dst_off: dst_off + 1024]),
                reads=[pk], writes=[f"XNT:{i}"])
        else:
            S.op("vector", lambda e, i=i: e.tensor_copy(
                bass.AP(XNT, i * P, [[ND * H, P], [H, ND], [1, P]]),
                dst_pt[:, dst_off: dst_off + 1024]),
                reads=[pk], writes=[f"XNT:{i}"])

    def fold_one(w, stage_keys, stage_t):
        for c in range(ND):
            par = (w * ND + c) % 2
            PT = PA if par == 0 else PB
            pk = ["PA01", "PA23"] if par == 0 else ["PB01", "PB23"]

            def mmf(e, PT=PT, st_t=stage_t, c=c, region=0):
                last = None
                for r in range(ND):
                    lhsT = st_t[:, r * D + c * P: r * D + (c + 1) * P]
                    if region == 0:
                        last = e.matmul(PT[:, 0:512], lhsT,
                                        TAB[:, r * 1025: r * 1025 + 512],
                                        start=(r == 0), stop=(r == ND - 1))
                    elif region == 1:
                        last = e.matmul(PT[:, 512:1024], lhsT,
                                        TAB[:, r * 1025 + 512: r * 1025 + 1024],
                                        start=(r == 0), stop=(r == ND - 1))
                    else:
                        last = e.matmul(PT[:, 1024:1025], lhsT,
                                        TAB[:, r * 1025 + 1024: r * 1025 + 1025],
                                        start=(r == 0), stop=(r == ND - 1))
                return last
            S.op("tensor", lambda e, f=mmf: f(e, region=0),
                 reads=stage_keys + TABKEYS, writes=[pk[0]])
            S.op("tensor", lambda e, f=mmf: f(e, region=1),
                 reads=stage_keys + TABKEYS, writes=[pk[0]])
            S.op("tensor", lambda e, f=mmf: f(e, region=2),
                 reads=stage_keys + TABKEYS, writes=[pk[1]])
            S.op("scalar", lambda e, PT=PT, w=w, c=c: e.copy(
                FW[:, (w * ND + c) * D:(w * ND + c + 1) * D], PT[:, 0:1024]),
                reads=[pk[0]], writes=[f"FW:{w},{c}"])
            S.op("scalar", lambda e, PT=PT, w=w, c=c: e.copy(
                FWN[:, c * 3 + w: c * 3 + w + 1], PT[:, 1024:1025]),
                reads=[pk[1]], writes=[f"FWN:{w},{c}"])


    ST_M1 = 40

    def ln1_chains():
        for i in range(NT):
            sa = STAT[:, ST_S1 + i: ST_S1 + i + 1]
            ssa = STAT[:, ST_SS1 + i: ST_SS1 + i + 1]
            TRp, K_TRp = post_slices(i)["TRASH"]
            S.op("vector", lambda e, i=i, sa=sa: e.reduce_sum(
                sa, XT[:, i * D:(i + 1) * D], AX.X),
                reads=[f"XT:{i}"], writes=[f"st:S1:{i}"])
            S.op("vector", lambda e, i=i, ssa=ssa, TRp=TRp: e.scalar_tensor_tensor(
                TRp, XT[:, i * D:(i + 1) * D], 1.0, XT[:, i * D:(i + 1) * D],
                ALU.mult, ALU.mult, accum_out=ssa),
                reads=[f"XT:{i}"], writes=[f"st:SS1:{i}"] + K_TRp)
        S8 = STAT[:, ST_S1:ST_S1 + 8]
        SS8 = STAT[:, ST_SS1:ST_SS1 + 8]
        M8 = STAT[:, ST_M1:ST_M1 + 8]
        R8 = STAT[:, ST_R1:ST_R1 + 8]
        B8 = STAT[:, ST_B1:ST_B1 + 8]
        rdall = [f"st:S1:{i}" for i in range(NT)] + [f"st:SS1:{i}" for i in range(NT)]
        S.op("vector", lambda e: e.tensor_scalar_mul(M8, S8, float(1.0 / D)),
             reads=rdall, writes=["st:M8"])
        S.op("vector", lambda e: e.tensor_mul(R8, M8, M8),
             reads=["st:M8"], writes=["st:R8t"])
        S.op("vector", lambda e: e.scalar_tensor_tensor(
            R8, SS8, float(1.0 / D), R8, ALU.mult, ALU.subtract),
            reads=rdall + ["st:R8t"], writes=["st:R8t"])
        S.op("scalar", lambda e: e.activation(R8, R8, AF.Ln,
                                              bias=EPSLN[:, 0:1]),
             reads=["st:R8t", "EPSLN"], writes=["st:R8t"])
        S.op("scalar", lambda e: e.activation(R8, R8, AF.Exp, scale=-0.5),
             reads=["st:R8t"], writes=["st:R8"])
        S.op("vector", lambda e: e.tensor_mul(B8, M8, R8),
             reads=["st:M8", "st:R8"], writes=["st:B8t"])
        S.op("vector", lambda e: e.tensor_scalar_mul(B8, B8, -1.0),
             reads=["st:B8t"], writes=["st:B8"])
        for i in range(NT):
            TRp, K_TRp = post_slices(i)["TRASH"]
            UUp, K_UUp = post_slices(i)["UU"]
            S.op("vector", lambda e, i=i, UUp=UUp: e.tensor_scalar_mul(
                UUp, XT[:, i * D:(i + 1) * D], STAT[:, ST_R1 + i: ST_R1 + i + 1]),
                reads=[f"XT:{i}", "st:R8"], writes=K_UUp)
            S.op("vector", lambda e, i=i, UUp=UUp: e.tensor_scalar_add(
                XNB[:, i * D:(i + 1) * D], UUp,
                STAT[:, ST_B1 + i: ST_B1 + i + 1]),
                reads=K_UUp + ["st:B8"], writes=[f"XNB:{i}"])

    def ln1_transposes():
        for i in range(NT):
            par = i % 2
            transpose_tile(i, f"XNB:{i}", PA if par == 0 else PB, 0,
                           "PA01" if par == 0 else "PB01", slot=i)

    def ln1_phase():
        for t in range(NT + 1):
            if 0 <= t - 1 < NT:
                i = t - 1
                par = i % 2
                transpose_tile(i, f"XNB:{par}", PA if par == 0 else PB, 0,
                               "PA01" if par == 0 else "PB01")
            if t < NT:
                ln_tile(t, XNB[:, (t % 2) * D:(t % 2) * D + D], f"XNB:{t % 2}")

    def do_half(h, do_ln1=True):
        if do_ln1:
            ln1_chains()
            ln1_transposes()

        def mmnyq(e):
            last = None
            for c in range(ND):
                for sgm in range(2):
                    last = e.matmul(
                        PB[0:3, 1024 + sgm * 512: 1024 + (sgm + 1) * 512],
                        FWN[:, c * 3:(c + 1) * 3],
                        XNT[:, c * H + sgm * 512: c * H + (sgm + 1) * 512],
                        start=(c == 0), stop=(c == ND - 1))
            return last
        S.op("tensor", mmnyq,
             reads=[f"FWN:{w},{c}" for w in range(3) for c in range(ND)] + XNTKEYS,
             writes=["PB23"])
        NYb = 4096
        NYT = SCR[0:3, NYb:NYb + 1024]
        NYT32 = SCR[0:32, NYb:NYb + 1024]
        NYA = SCR[0:2, NYb + 1024:NYb + 2048]
        SHA = SCR[0:32, NYb + 2048:NYb + 3072]
        SHBWS = SCR[0:32, NYb + 3072:NYb + 4096]
        K_NYT = scrk(NYb, NYb + 1024)
        K_NYA = scrk(NYb + 1024, NYb + 2048)
        K_SHA = scrk(NYb + 2048, NYb + 3072)
        K_SHB = scrk(NYb + 3072, NYb + 4096)
        S.op("vector", lambda e: e.memset(NYT32, 0.0), reads=[], writes=K_NYT)
        S.op("scalar", lambda e: e.copy(NYT, PB[0:3, 1024:2048]),
             reads=["PB23"] + K_NYT, writes=K_NYT)
        S.op("scalar", lambda e: e.activation(NYA, SCR[0:2, NYb:NYb + 1024], AF.Abs),
             reads=K_NYT, writes=K_NYA)
        S.op("scalar", lambda e: e.activation(NYA, NYA, AF.Ln,
                                              bias=EPS8[0:2, 0:1]),
             reads=K_NYA + ["EPS8"], writes=K_NYA)
        S.op("scalar", lambda e: e.activation(NYA, NYA, AF.Exp, scale=-1.0),
             reads=K_NYA, writes=K_NYA)
        S.op("vector", lambda e: e.tensor_mul(
            SCR[0:2, NYb:NYb + 1024], SCR[0:2, NYb:NYb + 1024], NYA),
            reads=K_NYT + K_NYA, writes=K_NYT)
        mka = list(range(32)); mka[0] = 1
        mkb = list(range(32)); mkb[0] = 2
        S.op("vector", lambda e: e.stream_shuffle(SHA, NYT32, mka),
             reads=K_NYT, writes=K_SHA)
        S.op("vector", lambda e: e.stream_shuffle(SHBWS, NYT32, mkb),
             reads=K_NYT, writes=K_SHB)
        S.op("vector", lambda e: e.tensor_mul(
            SCR[0:1, NYb + 1024:NYb + 2048], SHA[0:1, :], SHBWS[0:1, :]),
            reads=K_SHA + K_SHB, writes=K_NYA)
        inin = 0.0 if h == 0 else CAR[0:1, 9:10]
        rdn = K_NYA + ([] if h == 0 else ["car:n"])
        S.op("vector", lambda e, inin=inin: e.tensor_tensor_scan(
            SCR[0:1, NYb + 3072:NYb + 4096], SCR[0:1, NYb + 1024:NYb + 2048],
            SCR[0:1, NYb + 1024:NYb + 2048], inin, ALU.add, ALU.bypass),
            reads=rdn, writes=K_SHB)
        S.op("vector", lambda e: e.tensor_copy(
            CAR[0:1, 9:10], SCR[0:1, NYb + 4095:NYb + 4096]),
            reads=K_SHB, writes=["car:n"])
        S.op("vector", lambda e: e.tensor_mul(
            NVAL[0:1, :], SCR[0:1, NYb + 3072:NYb + 4096],
            SCR[0:1, NYb:NYb + 1024]),
            reads=K_SHB + K_NYT, writes=["NVAL"])

        PSLOT = [(PA, 0, "PA01"), (PA, 1024, "PA23"),
                 (PB, 0, "PB01"), (PB, 1024, "PB23")]

        def pr_slot(it, plane):
            return PSLOT[(3 * it + plane) % 4]

        def pr_mm(it):
            q, j = it // NJ, it % NJ
            xnt_r = [f"XNT:{4 * q + t}" for t in range(4)]

            def mmpl(e, w=0, dst=None, dstoff=0, j=j, q=q):
                last = None
                for p in range(2):
                    for c in range(ND):
                        lhsT = FW[:, (w * ND + c) * D + p * F + j * P:
                                  (w * ND + c) * D + p * F + j * P + P]
                        rhs = XNT[:, c * H + q * QW: c * H + (q + 1) * QW]
                        last = e.matmul(
                            dst[:, dstoff + p * F: dstoff + (p + 1) * F],
                            lhsT, rhs, start=(c == 0), stop=(c == ND - 1))
                return last
            for w in range(3):
                dst, off, key = pr_slot(it, w)
                S.op("tensor", lambda e, f=mmpl, w=w, dst=dst, off=off: f(
                    e, w=w, dst=dst, dstoff=off),
                    reads=[f"FW:{w},{c}" for c in range(ND)] + xnt_r,
                    writes=[key])

        def pr_norm(it):
            par = it % 2
            PS = pairs_slices(par)
            BS = bch_slices(par)
            FQN, K_FQN = BS["FQN"]; FKN, K_FKN = BS["FKN"]; TVB, K_TVB = BS["TVB"]
            qd, qo, qk = pr_slot(it, 0)
            kd, ko, kk = pr_slot(it, 1)
            vd, vo, vk = pr_slot(it, 2)
            QSpec = (qd[:, qo:qo + 1024], qk, "SQQ", "M2Q", "RQ", FQN, K_FQN)
            KSpec = (kd[:, ko:ko + 1024], kk, "SQK", "M2K", "RK", FKN, K_FKN)
            sl = {}
            for tag, (psl, pk, SQnm, M2nm, Rnm, Onm, Okey) in (("q", QSpec), ("k", KSpec)):
                sl[tag] = (psl, pk, PS[SQnm], PS[M2nm], PS[Rnm], Onm, Okey)
            for tag in ("q", "k"):
                psl, pk, (SQ, K_SQ), _, _, _, _ = sl[tag]
                S.op("scalar", lambda e, SQ=SQ, psl=psl: e.activation(
                    SQ, psl, AF.Square), reads=[pk], writes=K_SQ)
            for tag in ("q", "k"):
                _, _, (SQ, K_SQ), (M2, K_M2), _, _, _ = sl[tag]
                S.op("vector", lambda e, M2=M2, SQ=SQ: e.tensor_add(
                    M2, SQ[:, 0:512], SQ[:, 512:1024]), reads=K_SQ, writes=K_M2)
            for tag in ("q", "k"):
                _, _, _, (M2, K_M2), _, _, _ = sl[tag]
                S.op("scalar", lambda e, M2=M2: e.activation(
                    M2, M2, AF.Ln), reads=K_M2, writes=K_M2)
            for tag in ("q", "k"):
                _, _, _, (M2, K_M2), _, _, _ = sl[tag]
                S.op("scalar", lambda e, M2=M2: e.activation(
                    M2, M2, AF.Exp, scale=0.5), reads=K_M2, writes=K_M2)
            for tag in ("q", "k"):
                _, _, _, (M2, K_M2), _, _, _ = sl[tag]
                S.op("scalar", lambda e, M2=M2: e.activation(
                    M2, M2, AF.Ln, bias=EPS8[:, 0:1]),
                    reads=K_M2 + ["EPS8"], writes=K_M2)
            for tag in ("q", "k"):
                _, _, _, (M2, K_M2), (RR, K_RR), _, _ = sl[tag]
                S.op("scalar", lambda e, RR=RR, M2=M2: e.activation(
                    RR, M2, AF.Exp, scale=-1.0), reads=K_M2, writes=K_RR)
            for tag in ("q", "k"):
                psl, pk, _, _, (RR, K_RR), Onm, Okey = sl[tag]
                S.op("vector", lambda e, Onm=Onm, psl=psl, RR=RR: e.tensor_tensor(
                    Onm, psl, bass.AP(RR.tensor, RR.offset, [[8192, P], [0, 2], [1, 512]]),
                    ALU.mult), reads=[pk] + K_RR, writes=Okey)
            S.op("scalar", lambda e, TVB=TVB, vd=vd, vo=vo: e.copy(
                TVB, vd[:, vo:vo + 1024]), reads=[vk], writes=K_TVB)

        def pr_tail(it):
            q, j = it // NJ, it % NJ
            gq = h * NQ + q
            par = it % 2
            BS = bch_slices(par)
            FQN, K_FQN = BS["FQN"]; FKN, K_FKN = BS["FKN"]; TVB, K_TVB = BS["TVB"]
            PRD, K_PRD = BS["PRD"]; CSC, K_CSC = BS["CSC"]
            K_C0, K_C1 = [K_CSC[0]], [K_CSC[1]]
            K_P0, K_P1 = [K_PRD[0]], [K_PRD[1]]
            S.op("vector", lambda e, CSC=CSC, FKN=FKN, TVB=TVB: e.tensor_mul(
                CSC[:, 0:512], FKN[:, 0:512], TVB[:, 0:512]),
                reads=K_FKN + K_TVB, writes=K_C0)
            S.op("vector", lambda e, CSC=CSC, FKN=FKN, TVB=TVB: e.tensor_mul(
                CSC[:, 512:1024], FKN[:, 512:1024], TVB[:, 512:1024]),
                reads=K_FKN + K_TVB, writes=K_C1)
            S.op("vector", lambda e, PRD=PRD, CSC=CSC: e.tensor_sub(
                PRD[:, 0:512], CSC[:, 0:512], CSC[:, 512:1024]),
                reads=K_C0 + K_C1, writes=K_P0)
            S.op("vector", lambda e, CSC=CSC, FKN=FKN, TVB=TVB: e.tensor_mul(
                CSC[:, 0:512], FKN[:, 0:512], TVB[:, 512:1024]),
                reads=K_FKN + K_TVB, writes=K_C0)
            S.op("vector", lambda e, CSC=CSC, FKN=FKN, TVB=TVB: e.tensor_mul(
                CSC[:, 512:1024], FKN[:, 512:1024], TVB[:, 0:512]),
                reads=K_FKN + K_TVB, writes=K_C1)
            S.op("vector", lambda e, PRD=PRD, CSC=CSC: e.tensor_add(
                PRD[:, 512:1024], CSC[:, 0:512], CSC[:, 512:1024]),
                reads=K_C0 + K_C1, writes=K_P1)
            for pl in range(2):
                car_ap = CAR[:, 2 * j + pl: 2 * j + pl + 1]
                ini = 0.0 if gq == 0 else car_ap
                rd = [K_PRD[pl]] + ([] if gq == 0 else [f"car:{j},{pl}"])
                S.op("vector", lambda e, pl=pl, ini=ini, CSC=CSC, PRD=PRD:
                     e.tensor_tensor_scan(
                         CSC[:, pl * F:(pl + 1) * F], PRD[:, pl * F:(pl + 1) * F],
                         PRD[:, pl * F:(pl + 1) * F], ini, ALU.add, ALU.bypass),
                     reads=rd, writes=[K_CSC[pl]])
                S.op("vector", lambda e, pl=pl, car_ap=car_ap, CSC=CSC: e.tensor_copy(
                    car_ap, CSC[:, (pl + 1) * F - 1:(pl + 1) * F]),
                    reads=[K_CSC[pl]], writes=[f"car:{j},{pl}"])
            vr = VAL[:, j * H + q * QW: j * H + (q + 1) * QW]
            vi = VAL[:, (4 + j) * H + q * QW: (4 + j) * H + (q + 1) * QW]
            S.op("vector", lambda e, PRD=PRD, CSC=CSC, FQN=FQN: e.tensor_mul(
                PRD[:, 0:512], CSC[:, 0:512], FQN[:, 0:512]),
                reads=K_C0 + K_FQN, writes=K_P0)
            S.op("vector", lambda e, PRD=PRD, CSC=CSC, FQN=FQN: e.tensor_mul(
                PRD[:, 512:1024], CSC[:, 512:1024], FQN[:, 512:1024]),
                reads=K_C1 + K_FQN, writes=K_P1)
            S.op("vector", lambda e, vr=vr, PRD=PRD: e.tensor_add(
                vr, PRD[:, 0:512], PRD[:, 512:1024]),
                reads=K_P0 + K_P1, writes=[f"VAL:{j},{q}"])
            S.op("vector", lambda e, PRD=PRD, CSC=CSC, FQN=FQN: e.tensor_mul(
                PRD[:, 0:512], CSC[:, 512:1024], FQN[:, 0:512]),
                reads=K_C1 + K_FQN, writes=K_P0)
            S.op("vector", lambda e, PRD=PRD, CSC=CSC, FQN=FQN: e.tensor_mul(
                PRD[:, 512:1024], CSC[:, 0:512], FQN[:, 512:1024]),
                reads=K_C0 + K_FQN, writes=K_P1)
            S.op("vector", lambda e, vi=vi, PRD=PRD: e.tensor_sub(
                vi, PRD[:, 0:512], PRD[:, 512:1024]),
                reads=K_P0 + K_P1, writes=[f"VAL:{4 + j},{q}"])

        NIT = NQ * NJ
        for t in range(NIT + 2):
            if 0 <= t - 2 < NIT:
                pr_tail(t - 2)
            if 0 <= t - 1 < NIT:
                pr_norm(t - 1)
            if t < NIT:
                pr_mm(t)

        def po_mm(i):
            par = i % 2
            vh = PA[:, par * 1024: par * 1024 + 1024]
            vhk = "PA01" if par == 0 else "PA23"
            q = i // 4

            def mmir(e, i=i, vh=vh):
                last = None
                for g in range(8):
                    lhsT = VAL[:, g * H + i * P: g * H + (i + 1) * P]
                    for sgm in range(2):
                        last = e.matmul(
                            vh[:, sgm * 512:(sgm + 1) * 512], lhsT,
                            TAB[:, g * D + sgm * 512: g * D + (sgm + 1) * 512],
                            start=(g == 0), stop=False)
                for sgm in range(2):
                    last = e.matmul(vh[:, sgm * 512:(sgm + 1) * 512],
                                    NVAL[0:1, i * P:(i + 1) * P],
                                    GNYQ[0:1, sgm * 512:(sgm + 1) * 512],
                                    start=False, stop=True)
                return last
            S.op("tensor", mmir,
                 reads=[f"VAL:{g},{q}" for g in range(8)] + TABKEYS
                 + ["NVAL", "GNYQ"], writes=[vhk])

        def po_chain(i):
            vh = PA[:, (i % 2) * 1024: (i % 2) * 1024 + 1024]
            vhk = "PA01" if i % 2 == 0 else "PA23"
            par = i % 4
            PO = post_slices(par)
            UU, K_UU = PO["UU"]
            TRASH, K_TRASH = PO["TRASH"]; LNU, K_LNU = PO["LNU"]
            S.op("vector", lambda e, vh=vh, par=par, i=i, UU=UU:
                 e.scalar_tensor_tensor(
                     UU, XNB[:, i * D:(i + 1) * D], 1.0, vh, ALU.mult, ALU.add,
                     accum_out=stc(par, 0)),
                 reads=[f"XNB:{i}", vhk], writes=K_UU + [f"st:S{par}"])
            S.op("scalar", lambda e, par=par, UU=UU, TRASH=TRASH: e.activation(
                TRASH, UU, AF.Square, accum_out=stc(par, 1)),
                reads=K_UU, writes=[f"st:SS{par}"] + K_TRASH)
            fin(par, stc(par, 0), stc(par, 1), stc(par, 5), stc(par, 6),
                f"st:S{par}", f"st:SS{par}", f"st:R{par}", f"st:B{par}")
            S.op("scalar", lambda e, par=par, UU=UU, LNU=LNU: e.activation(
                LNU, UU, AF.Identity, bias=stc(par, 6), scale=stc(par, 5)),
                reads=K_UU + [f"st:R{par}", f"st:B{par}"], writes=K_LNU)
            S.op("vector", lambda e, i=i, LNU=LNU: e.scalar_tensor_tensor(
                XT[:, i * D:(i + 1) * D], LNU, 1.0, XT[:, i * D:(i + 1) * D],
                ALU.mult, ALU.add,
                accum_out=STAT[:, ST_S1 + i: ST_S1 + i + 1]),
                reads=[f"XT:{i}"] + K_LNU,
                writes=[f"XT:{i}", f"st:S1g:{i}"])

        def po_ln3(i):
            par = i % 2
            ln_tile(i, XNB[:, i * D:(i + 1) * D], f"XNB:{i}",
                    s_pre=f"st:S1g:{i}")
            transpose_tile(i, f"XNB:{i}", PB, par * 1024,
                           "PB01" if par == 0 else "PB23", slot=i)

        for t in range(NT + 2):
            if 0 <= t - 2 < NT:
                po_ln3(t - 2)
            if 0 <= t - 1 < NT:
                po_chain(t - 1)
            if t < NT:
                po_mm(t)

        w1zone = [f"FW:0,{c}" for c in range(ND)]
        w2zone = [f"FW:1,{c}" for c in range(ND)]
        for g in range(NG):
            dma("gpsimd", "dW1", FW[:, 0:8192],
                AP(w1_d, g * 1024, [[DF, P], [DF * P, ND], [1, 1024]]),
                [], w1zone)
            dma("gpsimd", "dW2", FW[:, 8192:16384],
                AP(w2_d, g * 1024 * D, [[D, P], [D * P, ND], [1, D]]),
                [], w2zone)
            if h == 0 and g == 1:
                dma("gpsimd", "dXB", XNB[:, :],
                    AP(x_d, H * D, [[D, P], [D * P, NT], [1, D]]),
                    [f"XNB:{i}" for i in range(NT)],
                    [f"XNB:{i}" for i in range(NT)])
            for fc in range(ND):
                Fg = g * ND + fc
                par = fc % 2
                hp = PA[:, par * 1024: par * 1024 + 1024]
                hk = "PA01" if par == 0 else "PA23"
                HS = ffn_h_slices(par)
                S_U, K_SU = HS["S_U"]; C_U, K_CU = HS["C_U"]

                def mmh(e, fc=fc, hp=hp):
                    last = None
                    for c in range(ND):
                        lhsT = FW[:, c * 1024 + fc * P:
                                   c * 1024 + (fc + 1) * P]
                        for sgm in range(2):
                            last = e.matmul(
                                hp[:, sgm * 512:(sgm + 1) * 512], lhsT,
                                XNT[:, c * H + sgm * 512: c * H + (sgm + 1) * 512],
                                start=(c == 0), stop=(c == ND - 1))
                    return last
                S.op("tensor", mmh, reads=w1zone + XNTKEYS, writes=[hk])
                S.op("scalar", lambda e, hp=hp, Fg=Fg, S_U=S_U: e.activation(
                    S_U, hp, AF.Sin, bias=SB1[:, Fg:Fg + 1], scale=0.1),
                    reads=[hk, "QC:SB1"], writes=K_SU)
                S.op("scalar", lambda e, hp=hp, Fg=Fg, C_U=C_U: e.activation(
                    C_U, hp, AF.Sin, bias=SC1[:, Fg:Fg + 1], scale=0.1),
                    reads=[hk, "QC:SC1"], writes=K_CU)
                S.op("vector", lambda e, hp=hp, Fg=Fg, C_U=C_U: e.scalar_tensor_tensor(
                    C_U, C_U, Ac[:, Fg:Fg + 1], hp, ALU.mult, ALU.add),
                    reads=K_CU + [hk, "QC:A"], writes=K_CU)
                S.op("vector", lambda e, Fg=Fg, S_U=S_U, C_U=C_U: e.scalar_tensor_tensor(
                    S_U, S_U, Bc[:, Fg:Fg + 1], C_U, ALU.mult, ALU.add),
                    reads=K_SU + K_CU + ["QC:B"], writes=K_SU)
                S.op("vector", lambda e, Fg=Fg, S_U=S_U: e.tensor_scalar_add(
                    S_U, S_U, B1c[:, Fg:Fg + 1]),
                    reads=K_SU + ["QC:b1:0"], writes=K_SU)
                S.op("vector", lambda e, fc=fc, S_U=S_U: e.tensor_relu(
                    VAL[:, fc * H:(fc + 1) * H], S_U),
                    reads=K_SU, writes=[f"VAL:{fc},0", f"VAL:{fc},1"])
            def emit_ymms(tc):
                par = tc % 2
                yp = PB[:, par * 1024: par * 1024 + 1024]
                yk = "PB01" if par == 0 else "PB23"

                def mmy(e, g=g, tc=tc, yp=yp, fclo=0, fchi=ND):
                    last = None
                    first = (fclo == 0)
                    if g == 0 and fclo == 0:
                        for sgm in range(2):
                            last = e.matmul(yp[:, sgm * 512:(sgm + 1) * 512],
                                            ONE1[0:1, :],
                                            B2R[0:1, sgm * 512:(sgm + 1) * 512],
                                            start=True, stop=False)
                        first = False
                    for fc in range(fclo, fchi):
                        lhsT = VAL[:, fc * H + tc * P: fc * H + (tc + 1) * P]
                        for sgm in range(2):
                            last = e.matmul(
                                yp[:, sgm * 512:(sgm + 1) * 512], lhsT,
                                FW[:, 8192 + fc * 1024 + sgm * 512:
                                   8192 + fc * 1024 + (sgm + 1) * 512],
                                start=(first and fc == fclo),
                                stop=(fc == ND - 1))
                    return last
                rd0 = [f"VAL:{fc},{tc // 4}" for fc in range(4)] + w2zone
                if g == 0:
                    rd0 += ["ONE1", "B2R"]
                S.op("tensor", lambda e, f=mmy: f(e, fclo=0, fchi=4),
                     reads=rd0, writes=[yk])
                S.op("tensor", lambda e, f=mmy: f(e, fclo=4, fchi=ND),
                     reads=[f"VAL:{fc},{tc // 4}" for fc in range(4, ND)] + w2zone,
                     writes=[yk])
                return yp, yk

            if g == 0:
                for tc in range(NT):
                    yp, yk = emit_ymms(tc)
                    S.op("vector", lambda e, tc=tc, yp=yp: e.tensor_copy(
                        FW[:, 16384 + tc * 1024:16384 + (tc + 1) * 1024], yp),
                        reads=[yk], writes=[f"FW:2,{tc}"])
            elif g < NG - 1:
                for tc in range(NT):
                    yp, yk = emit_ymms(tc)
                    S.op("vector", lambda e, tc=tc, yp=yp: e.scalar_tensor_tensor(
                        FW[:, 16384 + tc * 1024:16384 + (tc + 1) * 1024], yp, 1.0,
                        FW[:, 16384 + tc * 1024:16384 + (tc + 1) * 1024],
                        ALU.mult, ALU.add),
                        reads=[yk, f"FW:2,{tc}"], writes=[f"FW:2,{tc}"])
            else:
                if h == 0:
                    dma("gpsimd", "dFWL", FW[:, 0:8192],
                        AP(fws_d, 0, [[24576, P], [1, 8192]]),
                        ["FWSPILL"], [f"FW:0,{c}" for c in range(ND)])

                def y_a(tc):
                    par = tc % 2
                    yp = PB[:, par * 1024: par * 1024 + 1024]
                    yk = "PB01" if par == 0 else "PB23"
                    YS = ffn_y_slices(par)
                    U2, K_U2 = YS["U2"]; TR4, K_TR4 = YS["TR4"]
                    S.op("vector", lambda e, tc=tc, yp=yp, par=par, U2=U2:
                         e.scalar_tensor_tensor(
                             U2, yp, 1.0,
                             FW[:, 16384 + tc * 1024:16384 + (tc + 1) * 1024],
                             ALU.mult, ALU.add, accum_out=stc(par, 0)),
                         reads=[yk, f"FW:2,{tc}"], writes=K_U2 + [f"st:S{par}"])
                    S.op("scalar", lambda e, par=par, U2=U2, TR4=TR4: e.activation(
                        TR4, U2, AF.Square, accum_out=stc(par, 1)),
                        reads=K_U2, writes=[f"st:SS{par}"] + K_TR4)

                def y_b(tc):
                    par = tc % 2
                    fin(par, stc(par, 0), stc(par, 1), stc(par, 5), stc(par, 6),
                        f"st:S{par}", f"st:SS{par}", f"st:R{par}", f"st:B{par}")

                def y_c(tc):
                    par = tc % 2
                    YS = ffn_y_slices(par)
                    U2, K_U2 = YS["U2"]; LNY, K_LNY = YS["LNY"]
                    S.op("scalar", lambda e, par=par, U2=U2, LNY=LNY: e.activation(
                        LNY, U2, AF.Identity, bias=stc(par, 6), scale=stc(par, 5)),
                        reads=K_U2 + [f"st:R{par}", f"st:B{par}"], writes=K_LNY)
                    OUTS, K_OUTS = mkslice(par * 1024, 0, 1024)
                    S.op("gpsimd", lambda e, tc=tc, LNY=LNY, OUTS=OUTS:
                         e.tensor_add(OUTS, XT[:, tc * D:(tc + 1) * D], LNY),
                         reads=[f"XT:{tc}"] + K_LNY, writes=K_OUTS)
                    dma("sync", f"dST{h}",
                        AP(out_d, (h * H + tc * P) * D, [[D, P], [1, D]]),
                        OUTS, K_OUTS, [])
                    if h == 0 and tc >= 1:
                        dma("scalar", "dX1", XT[:, (tc - 1) * D: tc * D],
                            AP(x_d, (H + (tc - 1) * P) * D, [[D, P], [1, D]]),
                            [], [f"XT:{tc - 1}"])

                def prep_stat(i):
                    sa = STAT[:, ST_S1 + i: ST_S1 + i + 1]
                    ssa = STAT[:, ST_SS1 + i: ST_SS1 + i + 1]
                    tb = 5120 + (i % 2) * 1024
                    TRb = BCH[:, tb: tb + 1024]
                    K_TRb = bchk(tb, tb + 1024)
                    S.op("vector", lambda e, i=i, sa=sa: e.reduce_sum(
                        sa, XNB[:, i * D:(i + 1) * D], AX.X),
                        reads=[f"XNB:{i}"], writes=[f"st:S1:{i}"])
                    S.op("scalar", lambda e, i=i, ssa=ssa, TRb=TRb:
                         e.activation(TRb, XNB[:, i * D:(i + 1) * D],
                                      AF.Square, accum_out=ssa),
                         reads=[f"XNB:{i}"], writes=[f"st:SS1:{i}"] + K_TRb)

                for t in range(NT + 3):
                    if 0 <= t - 3 < NT:
                        y_c(t - 3)
                    if 0 <= t - 2 < NT:
                        y_b(t - 2)
                        if h == 0:
                            prep_stat(t - 2)
                    if 0 <= t - 1 < NT:
                        y_a(t - 1)
                    if t < NT:
                        emit_ymms(t)
                    if h == 0 and t == NT:
                        dma("gpsimd", "dFWL", FW[:, 8192:16384],
                            AP(fws_d, 8192, [[24576, P], [1, 8192]]),
                            ["FWSPILL"], [f"FW:1,{c}" for c in range(ND)])
                    if h == 0 and t == NT + 2:
                        dma("gpsimd", "dFWL", FW[:, 16384:24576],
                            AP(fws_d, 16384, [[24576, P], [1, 8192]]),
                            ["FWSPILL"], [f"FW:2,{c}" for c in range(ND)])
                if h == 0:
                    dma("scalar", "dX1", XT[:, (NT - 1) * D: NT * D],
                        AP(x_d, (H + (NT - 1) * P) * D, [[D, P], [1, D]]),
                        [], [f"XT:{NT - 1}"])
                    S8 = STAT[:, ST_S1:ST_S1 + 8]
                    SS8 = STAT[:, ST_SS1:ST_SS1 + 8]
                    M8 = STAT[:, 64:72]
                    R8 = STAT[:, ST_R1:ST_R1 + 8]
                    B8 = STAT[:, ST_B1:ST_B1 + 8]
                    rdall = ([f"st:S1:{i}" for i in range(NT)]
                             + [f"st:SS1:{i}" for i in range(NT)])
                    S.op("vector", lambda e, M8=M8, S8=S8: e.tensor_scalar_mul(
                        M8, S8, float(1.0 / D)), reads=rdall, writes=["st:M8"])
                    S.op("vector", lambda e, M8=M8, R8=R8: e.tensor_mul(
                        R8, M8, M8), reads=["st:M8"], writes=["st:R8t"])
                    S.op("vector", lambda e, R8=R8, SS8=SS8:
                         e.scalar_tensor_tensor(
                             R8, SS8, float(1.0 / D), R8, ALU.mult,
                             ALU.subtract),
                         reads=rdall + ["st:R8t"], writes=["st:R8t"])
                    S.op("scalar", lambda e, R8=R8: e.activation(
                        R8, R8, AF.Ln, bias=EPSLN[:, 0:1]),
                        reads=["st:R8t", "EPSLN"], writes=["st:R8t"])
                    S.op("scalar", lambda e, R8=R8: e.activation(
                        R8, R8, AF.Exp, scale=-0.5),
                        reads=["st:R8t"], writes=["st:R8"])
                    S.op("vector", lambda e, B8=B8, M8=M8, R8=R8: e.tensor_mul(
                        B8, M8, R8), reads=["st:M8", "st:R8"],
                        writes=["st:B8t"])
                    S.op("vector", lambda e, B8=B8: e.tensor_scalar_mul(
                        B8, B8, -1.0), reads=["st:B8t"], writes=["st:B8"])
                    for i in range(NT):
                        S.op("scalar", lambda e, i=i: e.activation(
                            XNB[:, i * D:(i + 1) * D],
                            XNB[:, i * D:(i + 1) * D], AF.Identity,
                            bias=STAT[:, ST_B1 + i: ST_B1 + i + 1],
                            scale=STAT[:, ST_R1 + i: ST_R1 + i + 1]),
                            reads=[f"XNB:{i}", "st:R8", "st:B8"],
                            writes=[f"XNB:{i}"])
                        transpose_tile(i, f"XNB:{i}",
                                       PA if i % 2 == 0 else PB, 0,
                                       "PA01" if i % 2 == 0 else "PB01",
                                       slot=i, evac_eng="vector")

    fold_one(1, XNTKEYS, XNT)
    ln1_chains()
    fold_one(2, [f"FW:0,{c}" for c in range(ND)], FW)
    fold_one(0, VKEYS, VAL)
    ln1_transposes()
    dma("scalar", "dSPL", fws_d.ap(), FW[:, :], FWKEYS, ["FWSPILL"])
    dma("gpsimd", "dG", TAB[:, 0:8192],
        AP(g_d, 0, [[D, P], [D * P, ND], [1, D]]), TABKEYS, TABKEYS)
    dma("gpsimd", "dLOb0", TAB[:, 0:8192],
        AP(glo_d, 0, [[D, P], [D * P, ND], [1, D]]), TABKEYS, TABKEYS,
        accum=True)
    do_half(0, do_ln1=False)
    do_half(1, do_ln1=False)

    prog = S.build_prog()

    with es:
        blk = es.enter_context(nc.Block())

        def run_eng(engobj, name):
            for wl, emit, inc in prog[name]:
                for semname, val in wl:
                    engobj.wait_ge(sems[semname], val)
                inst = emit(engobj)
                if inc is not None:
                    inst.then_inc(sems[inc[0]], 1 if inc[0].startswith("E")
                                  else 16)
            if name == "sync":
                engobj.wait_ge(sems["dST0"], 16 * NT)
                engobj.wait_ge(sems["dST1"], 16 * NT)

        @blk.sync
        def _(e):
            run_eng(e, "sync")

        @blk.gpsimd
        def _(e):
            run_eng(e, "gpsimd")

        @blk.tensor
        def _(e):
            run_eng(e, "tensor")

        @blk.vector
        def _(e):
            run_eng(e, "vector")

        @blk.scalar
        def _(e):
            run_eng(e, "scalar")

    return nc


_NC_CACHE = [None]


def kernel(L0_attn_x, L0_attn_Wq, L0_attn_Wk, L0_attn_Wv, L0_ffn_W1, L0_ffn_b1,
           L0_ffn_quantum_phases, L0_ffn_quantum_weights_real,
           L0_ffn_quantum_weights_imag, L0_ffn_W2, L0_ffn_b2):
    cs, cslo, cslo2, g, glo, ident = make_tables()
    x = np.ascontiguousarray(np.asarray(L0_attn_x, dtype=np.float32))
    shared = {
        "wq": np.ascontiguousarray(np.asarray(L0_attn_Wq, np.float32)),
        "wk": np.ascontiguousarray(np.asarray(L0_attn_Wk, np.float32)),
        "wv": np.ascontiguousarray(np.asarray(L0_attn_Wv, np.float32)),
        "w1": np.ascontiguousarray(np.asarray(L0_ffn_W1, np.float32)),
        "w2": np.ascontiguousarray(np.asarray(L0_ffn_W2, np.float32)),
        "b1": np.ascontiguousarray(np.asarray(L0_ffn_b1, np.float32)),
        "b2": np.ascontiguousarray(np.asarray(L0_ffn_b2, np.float32)),
        "qth": np.ascontiguousarray(np.asarray(L0_ffn_quantum_phases, np.float32)),
        "qwr": np.ascontiguousarray(
            np.asarray(L0_ffn_quantum_weights_real, np.float32)),
        "qwi": np.ascontiguousarray(
            np.asarray(L0_ffn_quantum_weights_imag, np.float32)),
        "cs": cs, "cslo": cslo, "cslo2": cslo2, "gt": g, "gtlo": glo,
        "identc": ident,
    }
    if _NC_CACHE[0] is None:
        _NC_CACHE[0] = build()
    nc = _NC_CACHE[0]
    in_maps = [dict(shared, x=np.ascontiguousarray(x[b])) for b in range(8)]
    res = run_bass_kernel_spmd(nc, in_maps, core_ids=list(range(8)))
    return np.stack([res.results[b]["out"] for b in range(8)], axis=0)


if __name__ == "__main__":
    import reference
    inp = reference.setup_inputs(0)
    out = kernel(**{k: np.asarray(v) for k, v in inp.items()})
    print(out.shape, out.dtype)


# version 47 (final)
# speedup vs baseline: 1.0516x; 1.0176x over previous
import contextlib
import ml_dtypes
import numpy as np
import concourse.bass as bass
import concourse.mybir as mybir
from concourse.bass_utils import run_bass_kernel_spmd

F32 = mybir.dt.float32
I32 = mybir.dt.int32
BF16 = mybir.dt.bfloat16
AF = mybir.ActivationFunctionType
ALU = mybir.AluOpType
AX = mybir.AxisListType

P = 128
D = 1024
ND = 8
T = 2048
H = 1024
NT = 8
F = 512
NJ = 4
NQ = 2
QW = 512
DF = 4096
NG = 4
EPS_FFT = 1e-8
EPS_LN = 1e-5

ENGS = ("sync", "gpsimd", "tensor", "vector", "scalar")


def make_tables():
    e = np.arange(D, dtype=np.float64)
    f = np.arange(F, dtype=np.float64)
    ang = 2.0 * np.pi * np.outer(e, f) / D
    cs64 = np.concatenate(
        [np.cos(ang), -np.sin(ang), np.cos(np.pi * e)[:, None]], axis=1)
    cs = cs64.astype(np.float32)
    cslo = (cs64 - cs.astype(np.float64)).astype(ml_dtypes.bfloat16)
    cslo2 = (cs64 - cs.astype(np.float64)
             - cslo.astype(np.float64)).astype(ml_dtypes.bfloat16)
    d = np.arange(D, dtype=np.float64)
    cf = np.where(f == 0, 1.0, 2.0)
    angG = 2.0 * np.pi * np.outer(f, d) / D
    g64 = np.concatenate(
        [cf[:, None] * np.cos(angG) / D,
         -cf[:, None] * np.sin(angG) / D,
         (np.cos(np.pi * d) / D)[None, :]], axis=0)
    g = g64.astype(np.float32)
    glo = (g64 - g.astype(np.float64)).astype(ml_dtypes.bfloat16)
    ident = np.eye(P, dtype=np.float32)
    return cs, cslo, cslo2, g, glo, ident


class Op:
    __slots__ = ("eng", "emit", "reads", "writes", "dma", "opid", "inc")

    def __init__(self, eng, emit, reads, writes, dma):
        self.eng, self.emit = eng, emit
        self.reads, self.writes = list(reads), list(writes)
        self.dma = dma
        self.inc = None


class Sched:
    def __init__(self):
        self.ops = []
        self.writer = {}
        self.readers = {}
        self.deps = []
        self.dma_mode = {}
        self.dma_last = {}
        self.batch_members = {}

    def dma_sem(self, name, mode):
        self.dma_mode[name] = mode
        self.batch_members[name] = []

    def op(self, eng, emit, reads=(), writes=(), dma=None):
        o = Op(eng, emit, reads, writes, dma)
        o.opid = len(self.ops)
        dep = set()
        for k in o.reads:
            w = self.writer.get(k)
            if w is not None:
                dep.add(w)
        for k in o.writes:
            w = self.writer.get(k)
            if w is not None:
                dep.add(w)
            for r in self.readers.get(k, ()):
                dep.add(r)
        if dma is not None:
            if self.dma_mode[dma] == "serial" and self.dma_last.get(dma) is not None:
                dep.add(self.dma_last[dma])
            self.dma_last[dma] = o.opid
            self.batch_members[dma].append(o.opid)
        dep.discard(o.opid)
        self.ops.append(o)
        self.deps.append(dep)
        for k in o.reads:
            self.readers.setdefault(k, []).append(o.opid)
        for k in o.writes:
            self.writer[k] = o.opid
            self.readers[k] = []
        return o.opid

    def build_prog(self):
        needed = set()
        for dep in self.deps:
            needed.update(dep)
        eng_count = {e: 0 for e in ENGS}
        for o in self.ops:
            if o.dma is not None:
                idx = self.batch_members[o.dma].index(o.opid) + 1
                o.inc = (o.dma, 16 * idx)
            elif o.opid in needed:
                eng_count[o.eng] += 1
                o.inc = ("E" + o.eng, eng_count[o.eng])
        prog = {e: [] for e in ENGS}
        watermark = {e: {} for e in ENGS}
        for o in self.ops:
            waits = {}
            for dd in sorted(self.deps[o.opid]):
                p = self.ops[dd]
                assert p.inc is not None, f"dep {dd} of {o.opid} has no inc"
                sem, val = p.inc
                if p.dma is not None and self.dma_mode[p.dma] == "batch":
                    val = 16 * len(self.batch_members[p.dma])
                waits[sem] = max(waits.get(sem, 0), val)
            wl = []
            for sem, val in waits.items():
                if watermark[o.eng].get(sem, 0) < val:
                    watermark[o.eng][sem] = val
                    wl.append((sem, val))
            prog[o.eng].append((wl, o.emit, o.inc))
        return prog


def build():
    nc = bass.Bass("TRN2", target_bir_lowering=False)
    AP = bass.AP

    x_d = nc.dram_tensor("x", [T, D], F32, kind="ExternalInput")
    wq_d = nc.dram_tensor("wq", [D, D], F32, kind="ExternalInput")
    wk_d = nc.dram_tensor("wk", [D, D], F32, kind="ExternalInput")
    wv_d = nc.dram_tensor("wv", [D, D], F32, kind="ExternalInput")
    w1_d = nc.dram_tensor("w1", [D, DF], F32, kind="ExternalInput")
    w2_d = nc.dram_tensor("w2", [DF, D], F32, kind="ExternalInput")
    b1_d = nc.dram_tensor("b1", [DF], F32, kind="ExternalInput")
    b2_d = nc.dram_tensor("b2", [D], F32, kind="ExternalInput")
    th_d = nc.dram_tensor("qth", [DF], F32, kind="ExternalInput")
    wr_d = nc.dram_tensor("qwr", [DF], F32, kind="ExternalInput")
    wi_d = nc.dram_tensor("qwi", [DF], F32, kind="ExternalInput")
    cs_d = nc.dram_tensor("cs", [D, 2 * F + 1], F32, kind="ExternalInput")
    cslo_d = nc.dram_tensor("cslo", [D, 2 * F + 1], BF16, kind="ExternalInput")
    cslo2_d = nc.dram_tensor("cslo2", [D, 2 * F + 1], BF16, kind="ExternalInput")
    g_d = nc.dram_tensor("gt", [2 * F + 1, D], F32, kind="ExternalInput")
    glo_d = nc.dram_tensor("gtlo", [2 * F + 1, D], BF16, kind="ExternalInput")
    id_d = nc.dram_tensor("identc", [P, P], F32, kind="ExternalInput")
    out_d = nc.dram_tensor("out", [T, D], F32, kind="ExternalOutput")
    fws_d = nc.dram_tensor("fwspill", [P, 24576], BF16)

    es = contextlib.ExitStack()
    sbuf = lambda n, s, dt=F32: es.enter_context(nc.sbuf_tensor(n, s, dt))
    XT = sbuf("XT", [P, NT * D])
    XNT = sbuf("XNT", [P, ND * H], BF16)
    FW = sbuf("FW", [P, 24576], BF16)
    VAL = sbuf("VAL", [P, 8 * H], BF16)
    TAB = sbuf("TAB", [P, ND * 1025], BF16)
    FWN = sbuf("FWN", [P, 24], BF16)
    SCR = sbuf("SCR", [P, 8192])
    BCH = sbuf("BCH", [P, 10240], BF16)
    XNB = sbuf("XNB", [P, 8 * D], BF16)
    CAR = sbuf("CAR", [P, 16])
    STAT = sbuf("STAT", [P, 96])
    QC = sbuf("QC", [P, 256])
    QIb = sbuf("QIb", [P, 32], I32)
    NVAL = sbuf("NVAL", [1, H], BF16)
    GNYQ = sbuf("GNYQ", [1, D], BF16)
    B2R = sbuf("B2R", [1, D], BF16)
    ONE1 = sbuf("ONE1", [1, P], BF16)
    EPSC = sbuf("EPSC", [P, 1])
    EPSLN = sbuf("EPSLN", [P, 1])
    EPS8 = sbuf("EPS8", [P, 1])
    IDN = sbuf("IDN", [P, P], BF16)
    IDNF = sbuf("IDNF", [P, P])
    PA = es.enter_context(nc.psum_tensor("PA", [P, 2048], F32))
    PB = es.enter_context(nc.psum_tensor("PB", [P, 2048], F32))

    semnames = ["Esync", "Egpsimd", "Etensor", "Evector", "Escalar",
                "dPRE", "dPREa", "dPS", "dG", "dW1", "dW2", "dX1", "dSPL", "dFWL",
                "dST0", "dST1", "dXB", "dLOa0", "dLOa1", "dLOa2", "dLOa3",
                "dLOb0", "dLOb1", "dLOb2", "dLOb3"]
    sems = {s: es.enter_context(nc.semaphore(s)) for s in semnames}

    S = Sched()
    for s, m in (("dPRE", "batch"), ("dPREa", "batch"), ("dPS", "batch"), ("dG", "batch"),
                 ("dW1", "serial"), ("dW2", "serial"), ("dX1", "serial"),
                 ("dSPL", "serial"), ("dFWL", "serial"),
                 ("dST0", "serial"), ("dST1", "serial"),
                 ("dXB", "batch"),
                 ("dLOa0", "batch"), ("dLOa1", "batch"), ("dLOa2", "batch"),
                 ("dLOa3", "batch"), ("dLOb0", "batch"),
                 ("dLOb1", "batch"), ("dLOb2", "batch"), ("dLOb3", "batch")):
        S.dma_sem(s, m)

    def scrk(lo, hi):
        return [f"SCR:{b}" for b in range(lo // 512, (hi + 511) // 512)]

    def mkslice(base, lo, hi):
        return SCR[:, base + lo: base + hi], scrk(base + lo, base + hi)

    def pairs_slices(p):
        b = p * 4096
        d = {}
        for nm, lo, hi in (("SQQ", 0, 1024), ("M2Q", 1024, 1536), ("RQ", 1536, 2048),
                           ("SQK", 2048, 3072), ("M2K", 3072, 3584), ("RK", 3584, 4096)):
            d[nm] = mkslice(b, lo, hi)
        return d

    def bchk(lo, hi):
        return [f"BC:{b}" for b in range(lo // 512, (hi + 511) // 512)]

    def bch_slices(p):
        b = p * 5120
        d = {}
        for nm, lo, hi in (("FQN", 0, 1024), ("FKN", 1024, 2048),
                           ("TVB", 2048, 3072), ("PRD", 3072, 4096),
                           ("CSC", 4096, 5120)):
            d[nm] = (BCH[:, b + lo: b + hi], bchk(b + lo, b + hi))
        return d

    def post_slices(p):
        p = p % 4
        d = {"UU": mkslice(p * 2048, 0, 1024),
             "LNU": mkslice(p * 2048, 1024, 2048)}
        d["TRASH"] = (BCH[:, p * 1024:(p + 1) * 1024],
                      [f"BC:{b}" for b in range(p * 2, p * 2 + 2)])
        return d

    def ffn_h_slices(p):
        b = p * 2048
        return {"S_U": mkslice(b, 0, 1024), "C_U": mkslice(b, 1024, 2048)}

    def ffn_y_slices(p):
        return {"U2": mkslice(4096 + p * 1024, 0, 1024),
                "LNY": mkslice(6144 + p * 1024, 0, 1024),
                "TR4": (BCH[:, p * 1024:(p + 1) * 1024], bchk(p * 1024, (p + 1) * 1024))}

    K_CSC = scrk(5120, 6144)
    K_PRD = scrk(4096, 5120)
    TRASH, K_TRASH = SCR[:, 2048:3072], scrk(2048, 3072)

    THc, WRc, WIc, B1c = QC[:, 0:32], QC[:, 32:64], QC[:, 64:96], QC[:, 96:128]
    Ac, Bc, SB1, SC1 = QC[:, 128:160], QC[:, 160:192], QC[:, 192:224], QC[:, 224:256]

    ST_S1, ST_SS1, ST_R1, ST_B1 = 0, 8, 16, 24
    def stc(p, idx):
        return STAT[:, 32 + p * 8 + idx: 32 + p * 8 + idx + 1]

    FWKEYS = [f"FW:{w},{c}" for w in range(3) for c in range(ND)]
    VKEYS = [f"VAL:{g},{q}" for g in range(8) for q in range(NQ)]
    XNTKEYS = [f"XNT:{i}" for i in range(NT)]
    TABKEYS = [f"TAB:{r}" for r in range(ND)]

    def dma(eng, sem, out_ap, in_ap, reads, writes, accum=False):
        if accum:
            S.op(eng, lambda e, o=out_ap, i=in_ap: e.dma_start(
                o, i, accum_op=ALU.add), reads=reads, writes=writes, dma=sem)
        else:
            S.op(eng, lambda e, o=out_ap, i=in_ap: e.dma_start(o, i),
                 reads=reads, writes=writes, dma=sem)

    dma("gpsimd", "dPREa", TAB[:, :],
        AP(cs_d, 0, [[1025, P], [1025 * P, ND], [1, 1025]]), [], TABKEYS)
    dma("gpsimd", "dPREa", XNT[:, 0:8192],
        AP(wk_d, 0, [[D, P], [D * P, ND], [1, D]]), [], XNTKEYS)
    dma("gpsimd", "dPRE", FW[:, 0:8192],
        AP(wv_d, 0, [[D, P], [D * P, ND], [1, D]]),
        [], [f"FW:0,{c}" for c in range(ND)])
    dma("gpsimd", "dPRE", VAL[:, 0:8192],
        AP(wq_d, 0, [[D, P], [D * P, ND], [1, D]]), [], VKEYS)
    dma("gpsimd", "dPRE", IDN[:, :], AP(id_d, 0, [[P, P], [1, P]]), [], ["IDN"])
    dma("gpsimd", "dPRE", B2R[0:1, :], AP(b2_d, 0, [[D, 1], [1, D]]), [], ["B2R"])
    dma("gpsimd", "dPRE", GNYQ[0:1, :],
        AP(g_d, 1024 * D, [[D, 1], [1, D]]), [], ["GNYQ"])
    dma("sync", "dPS", IDNF[:, :], AP(id_d, 0, [[P, P], [1, P]]),
        [], ["IDNF"])
    for o, src in ((0, cslo_d), (1, cslo2_d)):
        for p in range(2):
            kstg = [f"XNB:{i}" for i in range(5)]
            dma("sync", f"dLOa{o * 2 + p}", XNB[:, 0:4100],
                AP(src, p * 4 * 1025 * P,
                   [[1025, P], [1025 * P, 4], [1, 1025]]),
                kstg, kstg)
            S.op("vector", lambda e, p=p: e.tensor_add(
                TAB[:, p * 4100:(p + 1) * 4100],
                TAB[:, p * 4100:(p + 1) * 4100], XNB[:, 0:4100]),
                reads=TABKEYS[p * 4:(p + 1) * 4] + kstg,
                writes=TABKEYS[p * 4:(p + 1) * 4])
    dma("sync", "dPS", XT[:, :],
        AP(x_d, 0, [[D, P], [D * P, NT], [1, D]]),
        [], [f"XT:{i}" for i in range(NT)])
    _QSTG = {"th": (th_d, THc, 4096, "QC:th"), "wr": (wr_d, WRc, 4224, "QC:wr"),
             "wi": (wi_d, WIc, 4352, "QC:wi"), "b1": (b1_d, B1c, 4480, "QC:b1")}
    for _nm, (_src, _dst, _off, _key) in _QSTG.items():
        dma("sync", "dPS", SCR[0:32, _off:_off + 128],
            AP(_src, 0, [[P, 32], [1, P]]), [], [f"qs:{_nm}"])
    for _qi, (_nm, (_src, _dst, _off, _key)) in enumerate(_QSTG.items()):
        S.op("tensor", lambda e, o=_off, qi=_qi: e.matmul(
            PB[:, 1024 + qi * 32: 1024 + (qi + 1) * 32],
            SCR[0:32, o:o + 128], IDNF[0:32, 0:32], start=True, stop=True),
            reads=[f"qs:{_nm}", "IDNF"], writes=[f"qp:{_nm}", "PB23"])
        S.op("scalar", lambda e, d=_dst, qi=_qi: e.copy(
            d, PB[:, 1024 + qi * 32: 1024 + (qi + 1) * 32]),
            reads=[f"qp:{_nm}", "PB23"], writes=[_key + ":0"])


    S.op("vector", lambda e: e.memset(ONE1[0:1, :], 1.0), writes=["ONE1"])
    S.op("vector", lambda e: e.memset(EPSC[:, :], 1e-30), writes=["EPSC"])
    S.op("vector", lambda e: e.memset(EPSLN[:, :], float(EPS_LN)), writes=["EPSLN"])
    S.op("vector", lambda e: e.memset(EPS8[:, :], float(EPS_FFT)), writes=["EPS8"])

    R0 = SCR[:, 5120:5152]
    T0 = SCR[:, 5152:5184]
    DD = SCR[:, 5184:5216]
    CC = SCR[:, 5216:5248]
    STH = SCR[:, 5248:5280]
    CTH = SCR[:, 5280:5312]
    PQ = SCR[:, 5312:5344]

    def rred(shift):
        S.op("vector", lambda e: e.tensor_scalar(
            R0, THc, float(1 / (2 * np.pi)), float(shift), ALU.mult, ALU.add),
            reads=["QC:th:0"], writes=K_CSC)
        S.op("vector", lambda e: e.tensor_copy(QIb[:, :], R0),
             reads=K_CSC, writes=["QIb"])
        S.op("vector", lambda e: e.tensor_copy(T0, QIb[:, :]),
             reads=["QIb"], writes=K_CSC)
        S.op("vector", lambda e: e.tensor_sub(DD, R0, T0),
             reads=K_CSC, writes=K_CSC)
        S.op("vector", lambda e: e.tensor_single_scalar(CC, DD, 0.5, ALU.is_ge),
             reads=K_CSC, writes=K_CSC)
        S.op("vector", lambda e: e.tensor_add(T0, T0, CC),
             reads=K_CSC, writes=K_CSC)
        S.op("vector", lambda e: e.tensor_single_scalar(CC, DD, -0.5, ALU.is_le),
             reads=K_CSC, writes=K_CSC)
        S.op("vector", lambda e: e.tensor_sub(T0, T0, CC),
             reads=K_CSC, writes=K_CSC)
        S.op("vector", lambda e: e.scalar_tensor_tensor(
            R0, T0, float(-2 * np.pi), THc, ALU.mult, ALU.add),
            reads=K_CSC + ["QC:th:0"], writes=K_CSC)
        if shift:
            S.op("vector", lambda e: e.tensor_scalar_add(
                R0, R0, float(shift * 2 * np.pi)), reads=K_CSC, writes=K_CSC)

    rred(0.0)
    S.op("scalar", lambda e: e.activation(STH, R0, AF.Sin),
         reads=K_CSC, writes=["q:sth"])
    rred(0.25)
    S.op("scalar", lambda e: e.activation(CTH, R0, AF.Sin),
         reads=K_CSC, writes=["q:cth"])
    S.op("vector", lambda e: e.tensor_mul(PQ, CTH, WRc),
         reads=["q:cth", "QC:wr:0"], writes=["q:pq"])
    S.op("vector", lambda e: e.tensor_mul(T0, STH, WIc),
         reads=["q:sth", "QC:wi:0", "q:cth"], writes=K_CSC)
    S.op("vector", lambda e: e.tensor_add(PQ, PQ, T0),
         reads=["q:pq"] + K_CSC, writes=["q:pq"])
    S.op("vector", lambda e: e.tensor_mul(T0, PQ, WIc),
         reads=["q:pq", "QC:wi:0"], writes=K_CSC)
    S.op("vector", lambda e: e.tensor_scalar_mul(T0, T0, 0.1),
         reads=K_CSC, writes=K_CSC)
    S.op("vector", lambda e: e.tensor_mul(Ac, T0, STH),
         reads=K_CSC + ["q:sth"], writes=["QC:A"])
    S.op("vector", lambda e: e.tensor_mul(Bc, T0, CTH),
         reads=K_CSC + ["q:cth"], writes=["QC:B"])
    S.op("vector", lambda e: e.tensor_scalar_mul(SB1, B1c, 0.1),
         reads=["QC:b1:0"], writes=["QC:SB1"])
    S.op("vector", lambda e: e.tensor_scalar(
        SC1, B1c, 0.1, float(np.pi / 2), ALU.mult, ALU.add),
        reads=["QC:b1:0"], writes=["QC:SC1"])

    def fin(par, s_ap, ss_ap, r_ap, b_ap, s_key, ss_key, r_key, b_key):
        m_ap, q_ap, v_ap = stc(par, 2), stc(par, 3), stc(par, 4)
        km, kq, kv = f"st:m{par}", f"st:q{par}", f"st:v{par}"
        S.op("vector", lambda e: e.tensor_scalar_mul(m_ap, s_ap, float(1.0 / D)),
             reads=[s_key], writes=[km])
        S.op("vector", lambda e: e.tensor_scalar(q_ap, m_ap, m_ap, None, ALU.mult),
             reads=[km], writes=[kq])
        S.op("vector", lambda e: e.scalar_tensor_tensor(
            v_ap, ss_ap, float(1.0 / D), q_ap, ALU.mult, ALU.subtract),
            reads=[ss_key, kq], writes=[kv])
        S.op("scalar", lambda e: e.activation(v_ap, v_ap, AF.Ln,
                                              bias=EPSLN[:, 0:1]),
             reads=[kv, "EPSLN"], writes=[kv])
        S.op("scalar", lambda e: e.activation(r_ap, v_ap, AF.Exp, scale=-0.5),
             reads=[kv], writes=[r_key])
        S.op("vector", lambda e: e.tensor_scalar(
            b_ap, m_ap, r_ap, -1.0, ALU.mult, ALU.mult),
            reads=[km, r_key], writes=[b_key])

    def ln_tile(i, out_ap, out_key, s_pre=None):
        par = i % 4
        TRp, K_TRp = post_slices(par)["TRASH"]
        sa = STAT[:, ST_S1 + i: ST_S1 + i + 1]
        ssa = STAT[:, ST_SS1 + i: ST_SS1 + i + 1]
        ra = STAT[:, ST_R1 + i: ST_R1 + i + 1]
        ba = STAT[:, ST_B1 + i: ST_B1 + i + 1]
        ks, kss = f"st:S1:{i}", f"st:SS1:{i}"
        kr, kb = f"st:R1:{i}", f"st:B1:{i}"
        if s_pre is None:
            S.op("vector", lambda e: e.reduce_sum(sa, XT[:, i * D:(i + 1) * D], AX.X),
                 reads=[f"XT:{i}"], writes=[ks])
        else:
            ks = s_pre
        S.op("vector", lambda e: e.scalar_tensor_tensor(
            TRp, XT[:, i * D:(i + 1) * D], 1.0, XT[:, i * D:(i + 1) * D],
            ALU.mult, ALU.mult, accum_out=ssa),
            reads=[f"XT:{i}"], writes=[kss] + K_TRp)
        fin(par, sa, ssa, ra, ba, ks, kss, kr, kb)
        S.op("scalar", lambda e: e.activation(
            out_ap, XT[:, i * D:(i + 1) * D], AF.Identity, bias=ba, scale=ra),
            reads=[f"XT:{i}", kr, kb], writes=[out_key])

    def transpose_tile(i, src_key, dst_pt, dst_off, pk, slot=None,
                       evac_eng="scalar"):
        sl = (i % 2) if slot is None else slot
        def mmtr(e, i=i, dst_pt=dst_pt, dst_off=dst_off, sl=sl):
            last = None
            for c in range(ND):
                last = e.matmul(dst_pt[:, dst_off + c * P: dst_off + (c + 1) * P],
                                XNB[:, sl * D + c * P: sl * D + (c + 1) * P],
                                IDN[:, :], start=True, stop=True)
            return last
        S.op("tensor", mmtr, reads=[src_key, "IDN"], writes=[pk])
        if evac_eng == "scalar":
            S.op("scalar", lambda e, i=i: e.copy(
                bass.AP(XNT, i * P, [[ND * H, P], [H, ND], [1, P]]),
                dst_pt[:, dst_off: dst_off + 1024]),
                reads=[pk], writes=[f"XNT:{i}"])
        else:
            S.op("vector", lambda e, i=i: e.tensor_copy(
                bass.AP(XNT, i * P, [[ND * H, P], [H, ND], [1, P]]),
                dst_pt[:, dst_off: dst_off + 1024]),
                reads=[pk], writes=[f"XNT:{i}"])

    def fold_one(w, stage_keys, stage_t):
        for c in range(ND):
            par = (w * ND + c) % 2
            PT = PA if par == 0 else PB
            pk = ["PA01", "PA23"] if par == 0 else ["PB01", "PB23"]

            def mmf(e, PT=PT, st_t=stage_t, c=c, region=0):
                last = None
                for r in range(ND):
                    lhsT = st_t[:, r * D + c * P: r * D + (c + 1) * P]
                    if region == 0:
                        last = e.matmul(PT[:, 0:512], lhsT,
                                        TAB[:, r * 1025: r * 1025 + 512],
                                        start=(r == 0), stop=(r == ND - 1))
                    elif region == 1:
                        last = e.matmul(PT[:, 512:1024], lhsT,
                                        TAB[:, r * 1025 + 512: r * 1025 + 1024],
                                        start=(r == 0), stop=(r == ND - 1))
                    else:
                        last = e.matmul(PT[:, 1024:1025], lhsT,
                                        TAB[:, r * 1025 + 1024: r * 1025 + 1025],
                                        start=(r == 0), stop=(r == ND - 1))
                return last
            S.op("tensor", lambda e, f=mmf: f(e, region=0),
                 reads=stage_keys + TABKEYS, writes=[pk[0]])
            S.op("tensor", lambda e, f=mmf: f(e, region=1),
                 reads=stage_keys + TABKEYS, writes=[pk[0]])
            S.op("tensor", lambda e, f=mmf: f(e, region=2),
                 reads=stage_keys + TABKEYS, writes=[pk[1]])
            S.op("scalar", lambda e, PT=PT, w=w, c=c: e.copy(
                FW[:, (w * ND + c) * D:(w * ND + c + 1) * D], PT[:, 0:1024]),
                reads=[pk[0]], writes=[f"FW:{w},{c}"])
            S.op("scalar", lambda e, PT=PT, w=w, c=c: e.copy(
                FWN[:, c * 3 + w: c * 3 + w + 1], PT[:, 1024:1025]),
                reads=[pk[1]], writes=[f"FWN:{w},{c}"])


    ST_M1 = 40

    def ln1_chains():
        for i in range(NT):
            sa = STAT[:, ST_S1 + i: ST_S1 + i + 1]
            ssa = STAT[:, ST_SS1 + i: ST_SS1 + i + 1]
            TRp, K_TRp = post_slices(i)["TRASH"]
            S.op("vector", lambda e, i=i, sa=sa: e.reduce_sum(
                sa, XT[:, i * D:(i + 1) * D], AX.X),
                reads=[f"XT:{i}"], writes=[f"st:S1:{i}"])
            S.op("vector", lambda e, i=i, ssa=ssa, TRp=TRp: e.scalar_tensor_tensor(
                TRp, XT[:, i * D:(i + 1) * D], 1.0, XT[:, i * D:(i + 1) * D],
                ALU.mult, ALU.mult, accum_out=ssa),
                reads=[f"XT:{i}"], writes=[f"st:SS1:{i}"] + K_TRp)
        S8 = STAT[:, ST_S1:ST_S1 + 8]
        SS8 = STAT[:, ST_SS1:ST_SS1 + 8]
        M8 = STAT[:, ST_M1:ST_M1 + 8]
        R8 = STAT[:, ST_R1:ST_R1 + 8]
        B8 = STAT[:, ST_B1:ST_B1 + 8]
        rdall = [f"st:S1:{i}" for i in range(NT)] + [f"st:SS1:{i}" for i in range(NT)]
        S.op("vector", lambda e: e.tensor_scalar_mul(M8, S8, float(1.0 / D)),
             reads=rdall, writes=["st:M8"])
        S.op("vector", lambda e: e.tensor_mul(R8, M8, M8),
             reads=["st:M8"], writes=["st:R8t"])
        S.op("vector", lambda e: e.scalar_tensor_tensor(
            R8, SS8, float(1.0 / D), R8, ALU.mult, ALU.subtract),
            reads=rdall + ["st:R8t"], writes=["st:R8t"])
        S.op("scalar", lambda e: e.activation(R8, R8, AF.Ln,
                                              bias=EPSLN[:, 0:1]),
             reads=["st:R8t", "EPSLN"], writes=["st:R8t"])
        S.op("scalar", lambda e: e.activation(R8, R8, AF.Exp, scale=-0.5),
             reads=["st:R8t"], writes=["st:R8"])
        S.op("vector", lambda e: e.tensor_mul(B8, M8, R8),
             reads=["st:M8", "st:R8"], writes=["st:B8t"])
        S.op("vector", lambda e: e.tensor_scalar_mul(B8, B8, -1.0),
             reads=["st:B8t"], writes=["st:B8"])
        for i in range(NT):
            TRp, K_TRp = post_slices(i)["TRASH"]
            UUp, K_UUp = post_slices(i)["UU"]
            S.op("vector", lambda e, i=i, UUp=UUp: e.tensor_scalar_mul(
                UUp, XT[:, i * D:(i + 1) * D], STAT[:, ST_R1 + i: ST_R1 + i + 1]),
                reads=[f"XT:{i}", "st:R8"], writes=K_UUp)
            S.op("vector", lambda e, i=i, UUp=UUp: e.tensor_scalar_add(
                XNB[:, i * D:(i + 1) * D], UUp,
                STAT[:, ST_B1 + i: ST_B1 + i + 1]),
                reads=K_UUp + ["st:B8"], writes=[f"XNB:{i}"])

    def ln1_transposes():
        for i in range(NT):
            par = i % 2
            transpose_tile(i, f"XNB:{i}", PA if par == 0 else PB, 0,
                           "PA01" if par == 0 else "PB01", slot=i)

    def ln1_phase():
        for t in range(NT + 1):
            if 0 <= t - 1 < NT:
                i = t - 1
                par = i % 2
                transpose_tile(i, f"XNB:{par}", PA if par == 0 else PB, 0,
                               "PA01" if par == 0 else "PB01")
            if t < NT:
                ln_tile(t, XNB[:, (t % 2) * D:(t % 2) * D + D], f"XNB:{t % 2}")

    def do_half(h, do_ln1=True):
        if do_ln1:
            ln1_chains()
            ln1_transposes()

        def mmnyq(e):
            last = None
            for c in range(ND):
                for sgm in range(2):
                    last = e.matmul(
                        PB[0:3, 1024 + sgm * 512: 1024 + (sgm + 1) * 512],
                        FWN[:, c * 3:(c + 1) * 3],
                        XNT[:, c * H + sgm * 512: c * H + (sgm + 1) * 512],
                        start=(c == 0), stop=(c == ND - 1))
            return last
        S.op("tensor", mmnyq,
             reads=[f"FWN:{w},{c}" for w in range(3) for c in range(ND)] + XNTKEYS,
             writes=["PB23"])
        NYb = 4096
        NYT = SCR[0:3, NYb:NYb + 1024]
        NYT32 = SCR[0:32, NYb:NYb + 1024]
        NYA = SCR[0:2, NYb + 1024:NYb + 2048]
        SHA = SCR[0:32, NYb + 2048:NYb + 3072]
        SHBWS = SCR[0:32, NYb + 3072:NYb + 4096]
        K_NYT = scrk(NYb, NYb + 1024)
        K_NYA = scrk(NYb + 1024, NYb + 2048)
        K_SHA = scrk(NYb + 2048, NYb + 3072)
        K_SHB = scrk(NYb + 3072, NYb + 4096)
        S.op("vector", lambda e: e.memset(NYT32, 0.0), reads=[], writes=K_NYT)
        S.op("scalar", lambda e: e.copy(NYT, PB[0:3, 1024:2048]),
             reads=["PB23"] + K_NYT, writes=K_NYT)
        S.op("scalar", lambda e: e.activation(NYA, SCR[0:2, NYb:NYb + 1024], AF.Abs),
             reads=K_NYT, writes=K_NYA)
        S.op("scalar", lambda e: e.activation(NYA, NYA, AF.Ln,
                                              bias=EPS8[0:2, 0:1]),
             reads=K_NYA + ["EPS8"], writes=K_NYA)
        S.op("scalar", lambda e: e.activation(NYA, NYA, AF.Exp, scale=-1.0),
             reads=K_NYA, writes=K_NYA)
        S.op("vector", lambda e: e.tensor_mul(
            SCR[0:2, NYb:NYb + 1024], SCR[0:2, NYb:NYb + 1024], NYA),
            reads=K_NYT + K_NYA, writes=K_NYT)
        mka = list(range(32)); mka[0] = 1
        mkb = list(range(32)); mkb[0] = 2
        S.op("vector", lambda e: e.stream_shuffle(SHA, NYT32, mka),
             reads=K_NYT, writes=K_SHA)
        S.op("vector", lambda e: e.stream_shuffle(SHBWS, NYT32, mkb),
             reads=K_NYT, writes=K_SHB)
        S.op("vector", lambda e: e.tensor_mul(
            SCR[0:1, NYb + 1024:NYb + 2048], SHA[0:1, :], SHBWS[0:1, :]),
            reads=K_SHA + K_SHB, writes=K_NYA)
        inin = 0.0 if h == 0 else CAR[0:1, 9:10]
        rdn = K_NYA + ([] if h == 0 else ["car:n"])
        S.op("vector", lambda e, inin=inin: e.tensor_tensor_scan(
            SCR[0:1, NYb + 3072:NYb + 4096], SCR[0:1, NYb + 1024:NYb + 2048],
            SCR[0:1, NYb + 1024:NYb + 2048], inin, ALU.add, ALU.bypass),
            reads=rdn, writes=K_SHB)
        S.op("vector", lambda e: e.tensor_copy(
            CAR[0:1, 9:10], SCR[0:1, NYb + 4095:NYb + 4096]),
            reads=K_SHB, writes=["car:n"])
        S.op("vector", lambda e: e.tensor_mul(
            NVAL[0:1, :], SCR[0:1, NYb + 3072:NYb + 4096],
            SCR[0:1, NYb:NYb + 1024]),
            reads=K_SHB + K_NYT, writes=["NVAL"])

        PSLOT = [(PA, 0, "PA01"), (PA, 1024, "PA23"),
                 (PB, 0, "PB01"), (PB, 1024, "PB23")]

        def pr_slot(it, plane):
            return PSLOT[(3 * it + plane) % 4]

        def pr_mm(it):
            q, j = it // NJ, it % NJ
            xnt_r = [f"XNT:{4 * q + t}" for t in range(4)]

            def mmpl(e, w=0, dst=None, dstoff=0, j=j, q=q):
                last = None
                for p in range(2):
                    for c in range(ND):
                        lhsT = FW[:, (w * ND + c) * D + p * F + j * P:
                                  (w * ND + c) * D + p * F + j * P + P]
                        rhs = XNT[:, c * H + q * QW: c * H + (q + 1) * QW]
                        last = e.matmul(
                            dst[:, dstoff + p * F: dstoff + (p + 1) * F],
                            lhsT, rhs, start=(c == 0), stop=(c == ND - 1))
                return last
            for w in range(3):
                dst, off, key = pr_slot(it, w)
                S.op("tensor", lambda e, f=mmpl, w=w, dst=dst, off=off: f(
                    e, w=w, dst=dst, dstoff=off),
                    reads=[f"FW:{w},{c}" for c in range(ND)] + xnt_r,
                    writes=[key])

        def pr_norm(it):
            par = it % 2
            PS = pairs_slices(par)
            BS = bch_slices(par)
            FQN, K_FQN = BS["FQN"]; FKN, K_FKN = BS["FKN"]; TVB, K_TVB = BS["TVB"]
            qd, qo, qk = pr_slot(it, 0)
            kd, ko, kk = pr_slot(it, 1)
            vd, vo, vk = pr_slot(it, 2)
            QSpec = (qd[:, qo:qo + 1024], qk, "SQQ", "M2Q", "RQ", FQN, K_FQN)
            KSpec = (kd[:, ko:ko + 1024], kk, "SQK", "M2K", "RK", FKN, K_FKN)
            sl = {}
            for tag, (psl, pk, SQnm, M2nm, Rnm, Onm, Okey) in (("q", QSpec), ("k", KSpec)):
                sl[tag] = (psl, pk, PS[SQnm], PS[M2nm], PS[Rnm], Onm, Okey)
            for tag in ("q", "k"):
                psl, pk, (SQ, K_SQ), _, _, _, _ = sl[tag]
                S.op("scalar", lambda e, SQ=SQ, psl=psl: e.activation(
                    SQ, psl, AF.Square), reads=[pk], writes=K_SQ)
            for tag in ("q", "k"):
                _, _, (SQ, K_SQ), (M2, K_M2), _, _, _ = sl[tag]
                S.op("vector", lambda e, M2=M2, SQ=SQ: e.tensor_add(
                    M2, SQ[:, 0:512], SQ[:, 512:1024]), reads=K_SQ, writes=K_M2)
            for tag in ("q", "k"):
                _, _, _, (M2, K_M2), _, _, _ = sl[tag]
                S.op("scalar", lambda e, M2=M2: e.activation(
                    M2, M2, AF.Ln), reads=K_M2, writes=K_M2)
            for tag in ("q", "k"):
                _, _, _, (M2, K_M2), _, _, _ = sl[tag]
                S.op("scalar", lambda e, M2=M2: e.activation(
                    M2, M2, AF.Exp, scale=0.5), reads=K_M2, writes=K_M2)
            for tag in ("q", "k"):
                _, _, _, (M2, K_M2), _, _, _ = sl[tag]
                S.op("scalar", lambda e, M2=M2: e.activation(
                    M2, M2, AF.Ln, bias=EPS8[:, 0:1]),
                    reads=K_M2 + ["EPS8"], writes=K_M2)
            for tag in ("q", "k"):
                _, _, _, (M2, K_M2), (RR, K_RR), _, _ = sl[tag]
                S.op("scalar", lambda e, RR=RR, M2=M2: e.activation(
                    RR, M2, AF.Exp, scale=-1.0), reads=K_M2, writes=K_RR)
            for tag in ("q", "k"):
                psl, pk, _, _, (RR, K_RR), Onm, Okey = sl[tag]
                S.op("vector", lambda e, Onm=Onm, psl=psl, RR=RR: e.tensor_tensor(
                    Onm, psl, bass.AP(RR.tensor, RR.offset, [[8192, P], [0, 2], [1, 512]]),
                    ALU.mult), reads=[pk] + K_RR, writes=Okey)
            S.op("scalar", lambda e, TVB=TVB, vd=vd, vo=vo: e.copy(
                TVB, vd[:, vo:vo + 1024]), reads=[vk], writes=K_TVB)

        def pr_tail(it):
            q, j = it // NJ, it % NJ
            gq = h * NQ + q
            par = it % 2
            BS = bch_slices(par)
            FQN, K_FQN = BS["FQN"]; FKN, K_FKN = BS["FKN"]; TVB, K_TVB = BS["TVB"]
            PRD, K_PRD = BS["PRD"]; CSC, K_CSC = BS["CSC"]
            K_C0, K_C1 = [K_CSC[0]], [K_CSC[1]]
            K_P0, K_P1 = [K_PRD[0]], [K_PRD[1]]
            S.op("vector", lambda e, CSC=CSC, FKN=FKN, TVB=TVB: e.tensor_mul(
                CSC[:, 0:512], FKN[:, 0:512], TVB[:, 0:512]),
                reads=K_FKN + K_TVB, writes=K_C0)
            S.op("vector", lambda e, CSC=CSC, FKN=FKN, TVB=TVB: e.tensor_mul(
                CSC[:, 512:1024], FKN[:, 512:1024], TVB[:, 512:1024]),
                reads=K_FKN + K_TVB, writes=K_C1)
            S.op("vector", lambda e, PRD=PRD, CSC=CSC: e.tensor_sub(
                PRD[:, 0:512], CSC[:, 0:512], CSC[:, 512:1024]),
                reads=K_C0 + K_C1, writes=K_P0)
            S.op("vector", lambda e, CSC=CSC, FKN=FKN, TVB=TVB: e.tensor_mul(
                CSC[:, 0:512], FKN[:, 0:512], TVB[:, 512:1024]),
                reads=K_FKN + K_TVB, writes=K_C0)
            S.op("vector", lambda e, CSC=CSC, FKN=FKN, TVB=TVB: e.tensor_mul(
                CSC[:, 512:1024], FKN[:, 512:1024], TVB[:, 0:512]),
                reads=K_FKN + K_TVB, writes=K_C1)
            S.op("vector", lambda e, PRD=PRD, CSC=CSC: e.tensor_add(
                PRD[:, 512:1024], CSC[:, 0:512], CSC[:, 512:1024]),
                reads=K_C0 + K_C1, writes=K_P1)
            for pl in range(2):
                car_ap = CAR[:, 2 * j + pl: 2 * j + pl + 1]
                ini = 0.0 if gq == 0 else car_ap
                rd = [K_PRD[pl]] + ([] if gq == 0 else [f"car:{j},{pl}"])
                S.op("vector", lambda e, pl=pl, ini=ini, CSC=CSC, PRD=PRD:
                     e.tensor_tensor_scan(
                         CSC[:, pl * F:(pl + 1) * F], PRD[:, pl * F:(pl + 1) * F],
                         PRD[:, pl * F:(pl + 1) * F], ini, ALU.add, ALU.bypass),
                     reads=rd, writes=[K_CSC[pl]])
                S.op("vector", lambda e, pl=pl, car_ap=car_ap, CSC=CSC: e.tensor_copy(
                    car_ap, CSC[:, (pl + 1) * F - 1:(pl + 1) * F]),
                    reads=[K_CSC[pl]], writes=[f"car:{j},{pl}"])
            vr = VAL[:, j * H + q * QW: j * H + (q + 1) * QW]
            vi = VAL[:, (4 + j) * H + q * QW: (4 + j) * H + (q + 1) * QW]
            S.op("vector", lambda e, PRD=PRD, CSC=CSC, FQN=FQN: e.tensor_mul(
                PRD[:, 0:512], CSC[:, 0:512], FQN[:, 0:512]),
                reads=K_C0 + K_FQN, writes=K_P0)
            S.op("vector", lambda e, PRD=PRD, CSC=CSC, FQN=FQN: e.tensor_mul(
                PRD[:, 512:1024], CSC[:, 512:1024], FQN[:, 512:1024]),
                reads=K_C1 + K_FQN, writes=K_P1)
            S.op("vector", lambda e, vr=vr, PRD=PRD: e.tensor_add(
                vr, PRD[:, 0:512], PRD[:, 512:1024]),
                reads=K_P0 + K_P1, writes=[f"VAL:{j},{q}"])
            S.op("vector", lambda e, PRD=PRD, CSC=CSC, FQN=FQN: e.tensor_mul(
                PRD[:, 0:512], CSC[:, 512:1024], FQN[:, 0:512]),
                reads=K_C1 + K_FQN, writes=K_P0)
            S.op("vector", lambda e, PRD=PRD, CSC=CSC, FQN=FQN: e.tensor_mul(
                PRD[:, 512:1024], CSC[:, 0:512], FQN[:, 512:1024]),
                reads=K_C0 + K_FQN, writes=K_P1)
            S.op("vector", lambda e, vi=vi, PRD=PRD: e.tensor_sub(
                vi, PRD[:, 0:512], PRD[:, 512:1024]),
                reads=K_P0 + K_P1, writes=[f"VAL:{4 + j},{q}"])

        NIT = NQ * NJ
        for t in range(NIT + 2):
            if 0 <= t - 2 < NIT:
                pr_tail(t - 2)
            if 0 <= t - 1 < NIT:
                pr_norm(t - 1)
            if t < NIT:
                pr_mm(t)

        def po_mm(i):
            par = i % 2
            vh = PA[:, par * 1024: par * 1024 + 1024]
            vhk = "PA01" if par == 0 else "PA23"
            q = i // 4

            def mmir(e, i=i, vh=vh):
                last = None
                for g in range(8):
                    lhsT = VAL[:, g * H + i * P: g * H + (i + 1) * P]
                    for sgm in range(2):
                        last = e.matmul(
                            vh[:, sgm * 512:(sgm + 1) * 512], lhsT,
                            TAB[:, g * D + sgm * 512: g * D + (sgm + 1) * 512],
                            start=(g == 0), stop=False)
                for sgm in range(2):
                    last = e.matmul(vh[:, sgm * 512:(sgm + 1) * 512],
                                    NVAL[0:1, i * P:(i + 1) * P],
                                    GNYQ[0:1, sgm * 512:(sgm + 1) * 512],
                                    start=False, stop=True)
                return last
            S.op("tensor", mmir,
                 reads=[f"VAL:{g},{q}" for g in range(8)] + TABKEYS
                 + ["NVAL", "GNYQ"], writes=[vhk])

        def po_chain(i):
            vh = PA[:, (i % 2) * 1024: (i % 2) * 1024 + 1024]
            vhk = "PA01" if i % 2 == 0 else "PA23"
            par = i % 4
            PO = post_slices(par)
            UU, K_UU = PO["UU"]
            TRASH, K_TRASH = PO["TRASH"]; LNU, K_LNU = PO["LNU"]
            S.op("vector", lambda e, vh=vh, par=par, i=i, UU=UU:
                 e.scalar_tensor_tensor(
                     UU, XNB[:, i * D:(i + 1) * D], 1.0, vh, ALU.mult, ALU.add,
                     accum_out=stc(par, 0)),
                 reads=[f"XNB:{i}", vhk], writes=K_UU + [f"st:S{par}"])
            S.op("scalar", lambda e, par=par, UU=UU, TRASH=TRASH: e.activation(
                TRASH, UU, AF.Square, accum_out=stc(par, 1)),
                reads=K_UU, writes=[f"st:SS{par}"] + K_TRASH)
            fin(par, stc(par, 0), stc(par, 1), stc(par, 5), stc(par, 6),
                f"st:S{par}", f"st:SS{par}", f"st:R{par}", f"st:B{par}")
            S.op("scalar", lambda e, par=par, UU=UU, LNU=LNU: e.activation(
                LNU, UU, AF.Identity, bias=stc(par, 6), scale=stc(par, 5)),
                reads=K_UU + [f"st:R{par}", f"st:B{par}"], writes=K_LNU)
            S.op("vector", lambda e, i=i, LNU=LNU: e.scalar_tensor_tensor(
                XT[:, i * D:(i + 1) * D], LNU, 1.0, XT[:, i * D:(i + 1) * D],
                ALU.mult, ALU.add,
                accum_out=STAT[:, ST_S1 + i: ST_S1 + i + 1]),
                reads=[f"XT:{i}"] + K_LNU,
                writes=[f"XT:{i}", f"st:S1g:{i}"])

        def po_ln3(i):
            ln_tile(i, XNB[:, i * D:(i + 1) * D], f"XNB:{i}",
                    s_pre=f"st:S1g:{i}")

        def po_tr(i):
            par = i % 2
            transpose_tile(i, f"XNB:{i}", PB, par * 1024,
                           "PB01" if par == 0 else "PB23", slot=i)

        for t in range(NT + 3):
            if 0 <= t - 2 < NT:
                po_ln3(t - 2)
            if 0 <= t - 1 < NT:
                po_chain(t - 1)
            if t < NT:
                po_mm(t)
            if 0 <= t - 3 < NT:
                po_tr(t - 3)

        w1zone = [f"FW:0,{c}" for c in range(ND)]
        w2zone = [f"FW:1,{c}" for c in range(ND)]
        for g in range(NG):
            dma("gpsimd", "dW1", FW[:, 0:8192],
                AP(w1_d, g * 1024, [[DF, P], [DF * P, ND], [1, 1024]]),
                [], w1zone)
            dma("gpsimd", "dW2", FW[:, 8192:16384],
                AP(w2_d, g * 1024 * D, [[D, P], [D * P, ND], [1, D]]),
                [], w2zone)
            if h == 0 and g == 1:
                dma("gpsimd", "dXB", XNB[:, :],
                    AP(x_d, H * D, [[D, P], [D * P, NT], [1, D]]),
                    [f"XNB:{i}" for i in range(NT)],
                    [f"XNB:{i}" for i in range(NT)])
            for fc in range(ND):
                Fg = g * ND + fc
                par = fc % 2
                hp = PA[:, par * 1024: par * 1024 + 1024]
                hk = "PA01" if par == 0 else "PA23"
                HS = ffn_h_slices(par)
                S_U, K_SU = HS["S_U"]; C_U, K_CU = HS["C_U"]

                def mmh(e, fc=fc, hp=hp):
                    last = None
                    for c in range(ND):
                        lhsT = FW[:, c * 1024 + fc * P:
                                   c * 1024 + (fc + 1) * P]
                        for sgm in range(2):
                            last = e.matmul(
                                hp[:, sgm * 512:(sgm + 1) * 512], lhsT,
                                XNT[:, c * H + sgm * 512: c * H + (sgm + 1) * 512],
                                start=(c == 0), stop=(c == ND - 1))
                    return last
                S.op("tensor", mmh, reads=w1zone + XNTKEYS, writes=[hk])
                S.op("scalar", lambda e, hp=hp, Fg=Fg, S_U=S_U: e.activation(
                    S_U, hp, AF.Sin, bias=SB1[:, Fg:Fg + 1], scale=0.1),
                    reads=[hk, "QC:SB1"], writes=K_SU)
                S.op("scalar", lambda e, hp=hp, Fg=Fg, C_U=C_U: e.activation(
                    C_U, hp, AF.Sin, bias=SC1[:, Fg:Fg + 1], scale=0.1),
                    reads=[hk, "QC:SC1"], writes=K_CU)
                S.op("vector", lambda e, hp=hp, Fg=Fg, C_U=C_U: e.scalar_tensor_tensor(
                    C_U, C_U, Ac[:, Fg:Fg + 1], hp, ALU.mult, ALU.add),
                    reads=K_CU + [hk, "QC:A"], writes=K_CU)
                S.op("vector", lambda e, Fg=Fg, S_U=S_U, C_U=C_U: e.scalar_tensor_tensor(
                    S_U, S_U, Bc[:, Fg:Fg + 1], C_U, ALU.mult, ALU.add),
                    reads=K_SU + K_CU + ["QC:B"], writes=K_SU)
                S.op("vector", lambda e, Fg=Fg, S_U=S_U: e.tensor_scalar_add(
                    S_U, S_U, B1c[:, Fg:Fg + 1]),
                    reads=K_SU + ["QC:b1:0"], writes=K_SU)
                S.op("vector", lambda e, fc=fc, S_U=S_U: e.tensor_relu(
                    VAL[:, fc * H:(fc + 1) * H], S_U),
                    reads=K_SU, writes=[f"VAL:{fc},0", f"VAL:{fc},1"])
            def emit_ymms(tc):
                par = tc % 2
                yp = PB[:, par * 1024: par * 1024 + 1024]
                yk = "PB01" if par == 0 else "PB23"

                def mmy(e, g=g, tc=tc, yp=yp, fclo=0, fchi=ND):
                    last = None
                    first = (fclo == 0)
                    if g == 0 and fclo == 0:
                        for sgm in range(2):
                            last = e.matmul(yp[:, sgm * 512:(sgm + 1) * 512],
                                            ONE1[0:1, :],
                                            B2R[0:1, sgm * 512:(sgm + 1) * 512],
                                            start=True, stop=False)
                        first = False
                    for fc in range(fclo, fchi):
                        lhsT = VAL[:, fc * H + tc * P: fc * H + (tc + 1) * P]
                        for sgm in range(2):
                            last = e.matmul(
                                yp[:, sgm * 512:(sgm + 1) * 512], lhsT,
                                FW[:, 8192 + fc * 1024 + sgm * 512:
                                   8192 + fc * 1024 + (sgm + 1) * 512],
                                start=(first and fc == fclo),
                                stop=(fc == ND - 1))
                    return last
                rd0 = [f"VAL:{fc},{tc // 4}" for fc in range(4)] + w2zone
                if g == 0:
                    rd0 += ["ONE1", "B2R"]
                S.op("tensor", lambda e, f=mmy: f(e, fclo=0, fchi=4),
                     reads=rd0, writes=[yk])
                S.op("tensor", lambda e, f=mmy: f(e, fclo=4, fchi=ND),
                     reads=[f"VAL:{fc},{tc // 4}" for fc in range(4, ND)] + w2zone,
                     writes=[yk])
                return yp, yk

            if g == 0:
                for tc in range(NT):
                    yp, yk = emit_ymms(tc)
                    S.op("vector", lambda e, tc=tc, yp=yp: e.tensor_copy(
                        FW[:, 16384 + tc * 1024:16384 + (tc + 1) * 1024], yp),
                        reads=[yk], writes=[f"FW:2,{tc}"])
            elif g < NG - 1:
                for tc in range(NT):
                    yp, yk = emit_ymms(tc)
                    S.op("vector", lambda e, tc=tc, yp=yp: e.scalar_tensor_tensor(
                        FW[:, 16384 + tc * 1024:16384 + (tc + 1) * 1024], yp, 1.0,
                        FW[:, 16384 + tc * 1024:16384 + (tc + 1) * 1024],
                        ALU.mult, ALU.add),
                        reads=[yk, f"FW:2,{tc}"], writes=[f"FW:2,{tc}"])
            else:
                if h == 0:
                    dma("gpsimd", "dFWL", FW[:, 0:8192],
                        AP(fws_d, 0, [[24576, P], [1, 8192]]),
                        ["FWSPILL"], [f"FW:0,{c}" for c in range(ND)])

                def y_a(tc):
                    par = tc % 2
                    yp = PB[:, par * 1024: par * 1024 + 1024]
                    yk = "PB01" if par == 0 else "PB23"
                    YS = ffn_y_slices(par)
                    U2, K_U2 = YS["U2"]; TR4, K_TR4 = YS["TR4"]
                    S.op("vector", lambda e, tc=tc, yp=yp, par=par, U2=U2:
                         e.scalar_tensor_tensor(
                             U2, yp, 1.0,
                             FW[:, 16384 + tc * 1024:16384 + (tc + 1) * 1024],
                             ALU.mult, ALU.add, accum_out=stc(par, 0)),
                         reads=[yk, f"FW:2,{tc}"], writes=K_U2 + [f"st:S{par}"])
                    S.op("scalar", lambda e, par=par, U2=U2, TR4=TR4: e.activation(
                        TR4, U2, AF.Square, accum_out=stc(par, 1)),
                        reads=K_U2, writes=[f"st:SS{par}"] + K_TR4)

                def y_b(tc):
                    par = tc % 2
                    fin(par, stc(par, 0), stc(par, 1), stc(par, 5), stc(par, 6),
                        f"st:S{par}", f"st:SS{par}", f"st:R{par}", f"st:B{par}")

                def y_c(tc):
                    par = tc % 2
                    YS = ffn_y_slices(par)
                    U2, K_U2 = YS["U2"]; LNY, K_LNY = YS["LNY"]
                    S.op("scalar", lambda e, par=par, U2=U2, LNY=LNY: e.activation(
                        LNY, U2, AF.Identity, bias=stc(par, 6), scale=stc(par, 5)),
                        reads=K_U2 + [f"st:R{par}", f"st:B{par}"], writes=K_LNY)
                    OUTS, K_OUTS = mkslice(par * 1024, 0, 1024)
                    S.op("gpsimd", lambda e, tc=tc, LNY=LNY, OUTS=OUTS:
                         e.tensor_add(OUTS, XT[:, tc * D:(tc + 1) * D], LNY),
                         reads=[f"XT:{tc}"] + K_LNY, writes=K_OUTS)
                    dma("sync", f"dST{h}",
                        AP(out_d, (h * H + tc * P) * D, [[D, P], [1, D]]),
                        OUTS, K_OUTS, [])
                    if h == 0 and tc >= 1:
                        dma("scalar", "dX1", XT[:, (tc - 1) * D: tc * D],
                            AP(x_d, (H + (tc - 1) * P) * D, [[D, P], [1, D]]),
                            [], [f"XT:{tc - 1}"])

                def prep_stat(i):
                    sa = STAT[:, ST_S1 + i: ST_S1 + i + 1]
                    ssa = STAT[:, ST_SS1 + i: ST_SS1 + i + 1]
                    tb = 5120 + (i % 2) * 1024
                    TRb = BCH[:, tb: tb + 1024]
                    K_TRb = bchk(tb, tb + 1024)
                    S.op("vector", lambda e, i=i, sa=sa: e.reduce_sum(
                        sa, XNB[:, i * D:(i + 1) * D], AX.X),
                        reads=[f"XNB:{i}"], writes=[f"st:S1:{i}"])
                    S.op("scalar", lambda e, i=i, ssa=ssa, TRb=TRb:
                         e.activation(TRb, XNB[:, i * D:(i + 1) * D],
                                      AF.Square, accum_out=ssa),
                         reads=[f"XNB:{i}"], writes=[f"st:SS1:{i}"] + K_TRb)

                for t in range(NT + 3):
                    if 0 <= t - 3 < NT:
                        y_c(t - 3)
                    if 0 <= t - 2 < NT:
                        y_b(t - 2)
                        if h == 0:
                            prep_stat(t - 2)
                    if 0 <= t - 1 < NT:
                        y_a(t - 1)
                    if t < NT:
                        emit_ymms(t)
                    if h == 0 and t == NT:
                        dma("gpsimd", "dFWL", FW[:, 8192:16384],
                            AP(fws_d, 8192, [[24576, P], [1, 8192]]),
                            ["FWSPILL"], [f"FW:1,{c}" for c in range(ND)])
                    if h == 0 and t == NT + 2:
                        dma("gpsimd", "dFWL", FW[:, 16384:24576],
                            AP(fws_d, 16384, [[24576, P], [1, 8192]]),
                            ["FWSPILL"], [f"FW:2,{c}" for c in range(ND)])
                if h == 0:
                    dma("scalar", "dX1", XT[:, (NT - 1) * D: NT * D],
                        AP(x_d, (H + (NT - 1) * P) * D, [[D, P], [1, D]]),
                        [], [f"XT:{NT - 1}"])
                    S8 = STAT[:, ST_S1:ST_S1 + 8]
                    SS8 = STAT[:, ST_SS1:ST_SS1 + 8]
                    M8 = STAT[:, 64:72]
                    R8 = STAT[:, ST_R1:ST_R1 + 8]
                    B8 = STAT[:, ST_B1:ST_B1 + 8]
                    rdall = ([f"st:S1:{i}" for i in range(NT)]
                             + [f"st:SS1:{i}" for i in range(NT)])
                    S.op("vector", lambda e, M8=M8, S8=S8: e.tensor_scalar_mul(
                        M8, S8, float(1.0 / D)), reads=rdall, writes=["st:M8"])
                    S.op("vector", lambda e, M8=M8, R8=R8: e.tensor_mul(
                        R8, M8, M8), reads=["st:M8"], writes=["st:R8t"])
                    S.op("vector", lambda e, R8=R8, SS8=SS8:
                         e.scalar_tensor_tensor(
                             R8, SS8, float(1.0 / D), R8, ALU.mult,
                             ALU.subtract),
                         reads=rdall + ["st:R8t"], writes=["st:R8t"])
                    S.op("scalar", lambda e, R8=R8: e.activation(
                        R8, R8, AF.Ln, bias=EPSLN[:, 0:1]),
                        reads=["st:R8t", "EPSLN"], writes=["st:R8t"])
                    S.op("scalar", lambda e, R8=R8: e.activation(
                        R8, R8, AF.Exp, scale=-0.5),
                        reads=["st:R8t"], writes=["st:R8"])
                    S.op("vector", lambda e, B8=B8, M8=M8, R8=R8: e.tensor_mul(
                        B8, M8, R8), reads=["st:M8", "st:R8"],
                        writes=["st:B8t"])
                    S.op("vector", lambda e, B8=B8: e.tensor_scalar_mul(
                        B8, B8, -1.0), reads=["st:B8t"], writes=["st:B8"])
                    for i in range(NT):
                        S.op("scalar", lambda e, i=i: e.activation(
                            XNB[:, i * D:(i + 1) * D],
                            XNB[:, i * D:(i + 1) * D], AF.Identity,
                            bias=STAT[:, ST_B1 + i: ST_B1 + i + 1],
                            scale=STAT[:, ST_R1 + i: ST_R1 + i + 1]),
                            reads=[f"XNB:{i}", "st:R8", "st:B8"],
                            writes=[f"XNB:{i}"])
                        transpose_tile(i, f"XNB:{i}",
                                       PA if i % 2 == 0 else PB, 0,
                                       "PA01" if i % 2 == 0 else "PB01",
                                       slot=i, evac_eng="vector")

    fold_one(1, XNTKEYS, XNT)
    ln1_chains()
    fold_one(2, [f"FW:0,{c}" for c in range(ND)], FW)
    fold_one(0, VKEYS, VAL)
    ln1_transposes()
    dma("scalar", "dSPL", fws_d.ap(), FW[:, :], FWKEYS, ["FWSPILL"])
    dma("gpsimd", "dG", TAB[:, 0:8192],
        AP(g_d, 0, [[D, P], [D * P, ND], [1, D]]), TABKEYS, TABKEYS)
    dma("gpsimd", "dLOb0", TAB[:, 0:8192],
        AP(glo_d, 0, [[D, P], [D * P, ND], [1, D]]), TABKEYS, TABKEYS,
        accum=True)
    do_half(0, do_ln1=False)
    do_half(1, do_ln1=False)

    prog = S.build_prog()

    with es:
        blk = es.enter_context(nc.Block())

        def run_eng(engobj, name):
            for wl, emit, inc in prog[name]:
                for semname, val in wl:
                    engobj.wait_ge(sems[semname], val)
                inst = emit(engobj)
                if inc is not None:
                    inst.then_inc(sems[inc[0]], 1 if inc[0].startswith("E")
                                  else 16)
            if name == "sync":
                engobj.wait_ge(sems["dST0"], 16 * NT)
                engobj.wait_ge(sems["dST1"], 16 * NT)

        @blk.sync
        def _(e):
            run_eng(e, "sync")

        @blk.gpsimd
        def _(e):
            run_eng(e, "gpsimd")

        @blk.tensor
        def _(e):
            run_eng(e, "tensor")

        @blk.vector
        def _(e):
            run_eng(e, "vector")

        @blk.scalar
        def _(e):
            run_eng(e, "scalar")

    return nc


_NC_CACHE = [None]


def kernel(L0_attn_x, L0_attn_Wq, L0_attn_Wk, L0_attn_Wv, L0_ffn_W1, L0_ffn_b1,
           L0_ffn_quantum_phases, L0_ffn_quantum_weights_real,
           L0_ffn_quantum_weights_imag, L0_ffn_W2, L0_ffn_b2):
    cs, cslo, cslo2, g, glo, ident = make_tables()
    x = np.ascontiguousarray(np.asarray(L0_attn_x, dtype=np.float32))
    shared = {
        "wq": np.ascontiguousarray(np.asarray(L0_attn_Wq, np.float32)),
        "wk": np.ascontiguousarray(np.asarray(L0_attn_Wk, np.float32)),
        "wv": np.ascontiguousarray(np.asarray(L0_attn_Wv, np.float32)),
        "w1": np.ascontiguousarray(np.asarray(L0_ffn_W1, np.float32)),
        "w2": np.ascontiguousarray(np.asarray(L0_ffn_W2, np.float32)),
        "b1": np.ascontiguousarray(np.asarray(L0_ffn_b1, np.float32)),
        "b2": np.ascontiguousarray(np.asarray(L0_ffn_b2, np.float32)),
        "qth": np.ascontiguousarray(np.asarray(L0_ffn_quantum_phases, np.float32)),
        "qwr": np.ascontiguousarray(
            np.asarray(L0_ffn_quantum_weights_real, np.float32)),
        "qwi": np.ascontiguousarray(
            np.asarray(L0_ffn_quantum_weights_imag, np.float32)),
        "cs": cs, "cslo": cslo, "cslo2": cslo2, "gt": g, "gtlo": glo,
        "identc": ident,
    }
    if _NC_CACHE[0] is None:
        _NC_CACHE[0] = build()
    nc = _NC_CACHE[0]
    in_maps = [dict(shared, x=np.ascontiguousarray(x[b])) for b in range(8)]
    res = run_bass_kernel_spmd(nc, in_maps, core_ids=list(range(8)))
    return np.stack([res.results[b]["out"] for b in range(8)], axis=0)


if __name__ == "__main__":
    import reference
    inp = reference.setup_inputs(0)
    out = kernel(**{k: np.asarray(v) for k, v in inp.items()})
    print(out.shape, out.dtype)
